# version 01
# baseline (speedup 1.0000x reference)
import numpy as np
import ml_dtypes
import concourse.bass as bass
import concourse.mybir as mybir
from concourse.bass_utils import run_bass_kernel_spmd
from contextlib import ExitStack

F32 = mybir.dt.float32
BF16 = mybir.dt.bfloat16
FP8 = mybir.dt.float8e4
ALU = mybir.AluOpType
AX = mybir.AxisListType.X
ACTF = mybir.ActivationFunctionType
BF = ml_dtypes.bfloat16

_DEBUG = ()

D = 2048
TOK = 2048
WIN = 4096
I_FF = 8192



def _fft_constants():
    Gre = np.zeros((128, 16 * 2 * 2 * 128))
    Gim = np.zeros((128, 16 * 2 * 2 * 128))
    r_r = np.arange(256)
    a_r = np.arange(256)
    for s in range(16):
        ph = np.exp(-2j * np.pi * (np.outer(r_r, a_r) / 256.0 + s * a_r[None, :] / 4096.0))
        for rc in range(2):
            for ac in range(2):
                off = ((s * 2 + rc) * 2 + ac) * 128
                Gre[:, off:off + 128] = ph[rc * 128:(rc + 1) * 128, ac * 128:(ac + 1) * 128].real
                Gim[:, off:off + 128] = ph[rc * 128:(rc + 1) * 128, ac * 128:(ac + 1) * 128].imag
    BD2 = np.zeros((128, 72), np.complex128)
    for c in range(8):
        for s in range(16):
            for b in range(9):
                BD2[c * 16 + s, c * 9 + b] = np.exp(-2j * np.pi * s * b / 16.0)
    BDA = np.zeros((72, 128), np.complex128)
    for c in range(8):
        for b in range(9):
            for nu in range(16):
                BDA[c * 9 + b, c * 16 + nu] = np.exp(2j * np.pi * b * nu / 16.0)
    Hre = np.zeros((128, 16 * 2 * 128))
    Him = np.zeros((128, 16 * 2 * 128))
    for nu in range(16):
        for ac in range(2):
            a = (ac * 128 + np.arange(128))[:, None]
            eta = (128 + np.arange(128))[None, :]
            ph = np.exp(2j * np.pi * (a * nu / 4096.0 + a * eta / 256.0))
            off = (nu * 2 + ac) * 128
            Hre[:, off:off + 128] = ph.real
            Him[:, off:off + 128] = ph.imag
    cm = np.zeros((72, 256))
    for c in range(8):
        for b in range(9):
            if b < 8:
                w = np.full(256, 2.0)
                if b == 0:
                    w[0] = 1.0
            else:
                w = np.zeros(256)
                w[0] = 1.0
            cm[c * 9 + b] = w / 4096.0
    return {
        "Gre": Gre.astype(BF), "Gim": Gim.astype(BF),
        "BD2re": np.ascontiguousarray(BD2.real).astype(BF),
        "BD2im": np.ascontiguousarray(BD2.imag).astype(BF),
        "BD2im_neg": np.ascontiguousarray(-BD2.imag).astype(BF),
        "BDAre": np.ascontiguousarray(BDA.real).astype(BF),
        "BDAim": np.ascontiguousarray(BDA.imag).astype(BF),
        "BDAim_neg": np.ascontiguousarray(-BDA.imag).astype(BF),
        "Hre": Hre.astype(BF), "Him": Him.astype(BF),
        "cmask": cm.astype(np.float32),
        "identity": np.eye(128).astype(BF),
    }



class _Plan:
    def __init__(self, nc):
        self.nc = nc
        self.steps = {e: [] for e in ("t", "v", "s", "g", "q")}
        self.sems = {}
        self.counts = {}

    def sem(self, name):
        if name not in self.sems:
            self.sems[name] = self.nc.alloc_semaphore("sem_" + name)
            self.counts[name] = 0
        return self.sems[name]

    def n(self, name):
        self.sem(name)
        return self.counts[name]

    def add(self, eng, kind, *args, waits=(), incs=()):
        w = [(self.sem(nm), v) for nm, v in waits]
        inc = []
        for nm, amt in incs:
            inc.append((self.sem(nm), amt))
            self.counts[nm] += amt
        self.steps[eng].append((kind, args, w, inc))

    def replay(self, key, eng):
        for kind, args, waits, incs in self.steps[key]:
            for sem, val in waits:
                eng.wait_ge(sem, val)
            ins = self._emit(eng, kind, args)
            for sem, amt in incs:
                ins.then_inc(sem, amt)

    @staticmethod
    def _emit(eng, kind, a):
        if kind == "mm":
            out, lhsT, rhs, start, stop = a
            return eng.matmul(out, lhsT, rhs, start=start, stop=stop)
        if kind == "tr":
            out, in_, ident = a
            return eng.transpose(out, in_, ident)
        if kind == "copy":
            out, in_ = a
            if hasattr(eng, "tensor_copy"):
                return eng.tensor_copy(out, in_)
            return eng.copy(out, in_)
        if kind == "tt":
            out, in0, in1, op = a
            return eng.tensor_tensor(out, in0, in1, op=op)
        if kind == "ts":
            out, in0, s1, s2, op0, op1 = a
            if op1 is None:
                return eng.tensor_scalar(out, in0, s1, None, op0=op0)
            return eng.tensor_scalar(out, in0, s1, s2, op0=op0, op1=op1)
        if kind == "stt":
            out, in0, scal, in1, op0, op1 = a
            return eng.scalar_tensor_tensor(out, in0, scal, in1, op0=op0, op1=op1)
        if kind == "act":
            out, in_, func, bias, scale = a
            return eng.activation(out, in_, func, bias=bias, scale=scale)
        if kind == "red":
            out, in_, op, absval = a
            return eng.tensor_reduce(out, in_, axis=AX, op=op, apply_absolute_value=absval)
        if kind == "bnst":
            out, in_ = a
            return eng.bn_stats(out, in_)
        if kind == "bnag":
            out, in_ = a
            return eng.bn_aggr(out, in_)
        if kind == "dma":
            out, in_ = a
            return eng.dma_start(out, in_)
        if kind == "memset":
            ap, val = a
            return eng.memset(ap, val)
        if kind == "recip":
            out, in_ = a
            return eng.reciprocal(out, in_)
        if kind == "coll":
            kind_s, op, groups, ins_, outs_ = a
            return eng.collective_compute(kind_s, op, replica_groups=groups,
                                          ins=ins_, outs=outs_)
        raise KeyError(kind)


def _run_block(nc, plan):
    with nc.Block() as blk:
        if plan.steps["q"]:
            blk.sync(lambda e: plan.replay("q", e))
        if plan.steps["g"]:
            blk.gpsimd(lambda e: plan.replay("g", e))
        if plan.steps["v"]:
            blk.vector(lambda e: plan.replay("v", e))
        if plan.steps["s"]:
            blk.scalar(lambda e: plan.replay("s", e))
        if plan.steps["t"]:
            blk.tensor(lambda e: plan.replay("t", e))



class _ConvBufs:
    _SEQ = [0]

    def __init__(self, nc, stack):
        ent = stack.enter_context
        _ConvBufs._SEQ[0] += 1
        pref = "cv%d_" % _ConvBufs._SEQ[0]
        T = lambda n, sh, dt: ent(nc.sbuf_tensor(pref + n, sh, dt))
        self.S = T("S", [128, 16 * 2 * 256], BF16)
        self.E = T("E", [128, 2 * 2 * 4096], BF16)
        self.UT = T("UT", [128, 2 * 2 * 4096], BF16)
        self.Fbuf = T("F", [128, 2 * 1024], BF16)
        self.Z = T("Z", [128, 2 * 1024], BF16)
        self.Ksb = T("Ksb", [128, 2 * 1024], BF16)
        self.Kbuf = self.Ksb
        self.Usb = T("Usb", [128, 2 * 1024], BF16)
        self.tmp = T("tmp", [128, 2 * 1024], F32)
        self.tmp2 = T("tmp2", [128, 1024], F32)
        self.ysb = T("ysb", [128, 2 * 256], BF16)


class _Conv:

    def __init__(self, P, b, cb, pb):
        self.P, self.b, self.c, self.pb = P, b, cb, pb
        self.ptr_grp = 0
        self.ptr_evac = {}

    def tr_group(self, trs, dest, cons, extra_waits=(), evac_extra=()):
        P, b = self.P, self.b
        g = self.ptr_grp
        bank = self.pb["ptr"][g % 2]
        waits = list(extra_waits)
        if g - 2 in self.ptr_evac:
            waits.append(self.ptr_evac[g - 2])
        for i, in_ap in enumerate(trs):
            P.add("t", "tr", bank[:, i * 128:(i + 1) * 128], in_ap, self.c.ident[:],
                  waits=waits if i == 0 else (),
                  incs=[("ft", 1)] if i == len(trs) - 1 else ())
        cn = "fv" if cons == "v" else "fs"
        P.add(cons, "copy", dest, bank[:, 0:128 * len(trs)],
              waits=[("ft", P.n("ft"))] + list(evac_extra), incs=[(cn, 1)])
        self.ptr_evac[g] = (cn, P.n(cn))
        self.ptr_grp += 1

    def strided_tr(self, tiles, rchunks, extra_first=()):
        b = self.b
        first = True
        for s in range(16):
            for rc in range(rchunks):
                trs = [tiles[cq][:, s + 2048 * rc: s + 2048 * rc + 16 * 127 + 1: 16]
                      for cq in range(2)]
                off = (s * 2 + rc) * 256
                self.tr_group(trs, b.S[:, off:off + 256],
                              "v" if (s + rc) % 2 == 0 else "s",
                              extra_waits=extra_first if first else ())
                first = False

    def stage1(self, rchunks):
        P, b, c = self.P, self.b, self.c
        s1_evac = {}
        k = 0
        for s in range(16):
            for ac in range(2):
                for im in range(2):
                    bank = self.pb["pA"] if k % 2 == 0 else self.pb["pB"]
                    cons = "v" if k % 2 == 0 else "s"
                    waits = []
                    if k - 2 in s1_evac:
                        waits.append(s1_evac[k - 2])
                    if k <= 1:
                        waits += [("fv", P.n("fv")), ("fs", P.n("fs"))]
                    for rc in range(rchunks):
                        off = ((s * 2 + rc) * 2 + ac) * 128
                        G = (c.Gim if im else c.Gre)[:, off:off + 128]
                        soff = (s * 2 + rc) * 256
                        P.add("t", "mm", bank[:, 0:256], G, b.S[:, soff:soff + 256],
                              rc == 0, rc == rchunks - 1,
                              waits=waits if rc == 0 else (),
                              incs=[("ft", 1)] if rc == rchunks - 1 else ())
                    eoff = (ac * 2 + im) * 4096
                    dest = b.E[:, eoff:eoff + 4096].rearrange(
                        "p (g c2 s2) -> p g c2 s2", c2=8, s2=16)[:, :, :, s]
                    cn = "fv" if cons == "v" else "fs"
                    P.add(cons, "copy", dest, bank[:, 0:256],
                          waits=[("ft", P.n("ft"))], incs=[(cn, 1)])
                    s1_evac[k] = (cn, P.n(cn))
                    k += 1
        return P.n("fv"), P.n("fs")

    def pair_turn1(self, pair, extra_waits=()):
        b = self.b
        trs = []
        for im in range(2):
            for lg in range(2):
                g = pair * 2 + lg
                for ac in range(2):
                    eoff = (ac * 2 + im) * 4096
                    trs.append(b.E[:, eoff + g * 128: eoff + (g + 1) * 128])
        par = pair % 2
        dest = b.Fbuf[:, par * 1024: par * 1024 + 1024]
        self.tr_group(trs, dest, "v" if par == 0 else "s", extra_waits=extra_waits)

    def pair_stage2(self, pair, x_prev):
        P, b, c = self.P, self.b, self.c
        par = pair % 2
        fw = [self.ptr_evac[self.ptr_grp - 1]]
        if par in x_prev:
            fw.append(x_prev[par])
        F = lambda im: b.Fbuf[:, par * 1024 + im * 512: par * 1024 + im * 512 + 512]
        pXre, pXim = self.pb["pXre"], self.pb["pXim"]
        P.add("t", "mm", pXre[par][0:72, 0:512], c.BD2re[0:128, 0:72], F(0), True, False, waits=fw)
        P.add("t", "mm", pXre[par][0:72, 0:512], c.BD2im_neg[0:128, 0:72], F(1), False, True)
        P.add("t", "mm", pXim[par][0:72, 0:512], c.BD2im[0:128, 0:72], F(0), True, False)
        P.add("t", "mm", pXim[par][0:72, 0:512], c.BD2re[0:128, 0:72], F(1), False, True,
              incs=[("ft", 1)])
        return P.n("ft")

    def emit_ker(self, ker_tiles, kout, extra_first=()):
        P, b, c = self.P, self.b, self.c
        self.strided_tr(ker_tiles, 1, extra_first=extra_first)
        bar = self.stage1(1)
        x_prev = {}
        kq_guard = {}
        for pair in range(16):
            ew = [("fv", bar[0]), ("fs", bar[1])] if pair < 2 else ()
            self.pair_turn1(pair, extra_waits=ew)
            nft = self.pair_stage2(pair, x_prev)
            par = pair % 2
            cm = c.cmask[0:72, :].unsqueeze(1).broadcast_to([72, 2, 256])
            waits = [("ft", nft)]
            if par in kq_guard:
                waits.append(kq_guard[par])
            pXre, pXim = self.pb["pXre"], self.pb["pXim"]
            P.add("v", "tt",
                  b.Ksb[0:72, par * 1024: par * 1024 + 512].rearrange("p (l f) -> p l f", l=2),
                  pXre[par][0:72, 0:512].rearrange("p (l f) -> p l f", l=2), cm, ALU.mult,
                  waits=waits)
            P.add("v", "tt",
                  b.Ksb[0:72, par * 1024 + 512: par * 1024 + 1024].rearrange("p (l f) -> p l f", l=2),
                  pXim[par][0:72, 0:512].rearrange("p (l f) -> p l f", l=2), cm, ALU.mult,
                  incs=[("fv", 1)])
            x_prev[par] = ("fv", P.n("fv"))
            ks = "kq%d" % par
            P.add("q", "dma", kout[pair], b.Ksb[0:72, par * 1024: par * 1024 + 1024],
                  waits=[("fv", P.n("fv"))], incs=[(ks, 16)])
            kq_guard[par] = (ks, P.n(ks))

    def emit_kv(self, kv_tiles, kin, ych_ap, k_extra_wait=()):
        P, b, c = self.P, self.b, self.c
        pXre, pXim = self.pb["pXre"], self.pb["pXim"]
        pA, pB = self.pb["pA"], self.pb["pB"]
        self.strided_tr(kv_tiles, 2)
        bar = self.stage1(2)
        x_prev = {}
        z_done = {}
        u_prev = None
        usb_turn = {}
        ustage_prev = {}
        for pair in range(16):
            par = pair % 2
            kw = list(k_extra_wait)
            if par in z_done:
                kw.append(z_done[par])
            P.add("q", "dma", b.Kbuf[0:72, par * 1024: par * 1024 + 1024],
                  kin[pair], waits=kw, incs=[("kq", 16)])
            nkq = P.n("kq")
            ew = [("fv", bar[0]), ("fs", bar[1])] if pair < 2 else ()
            self.pair_turn1(pair, extra_waits=ew)
            nft = self.pair_stage2(pair, x_prev)
            sw = [("ft", nft)]
            if par in z_done:
                sw.append(z_done[par])
            P.add("s", "copy", b.tmp[0:72, par * 1024: par * 1024 + 512],
                  pXre[par][0:72, 0:512], waits=sw)
            P.add("s", "copy", b.tmp[0:72, par * 1024 + 512: par * 1024 + 1024],
                  pXim[par][0:72, 0:512], incs=[("fs", 1)])
            nfs = P.n("fs")
            x_prev[par] = ("fs", nfs)
            K = lambda im: b.Kbuf[0:72, par * 1024 + im * 512: par * 1024 + im * 512 + 512]
            Z = lambda im: b.Z[0:72, par * 1024 + im * 512: par * 1024 + im * 512 + 512]
            tre = b.tmp[0:72, par * 1024: par * 1024 + 512]
            tim = b.tmp[0:72, par * 1024 + 512: par * 1024 + 1024]
            zA, zB = b.tmp2[0:72, 0:512], b.tmp2[0:72, 512:1024]
            vw = [("fs", nfs), ("kq", nkq)]
            if par in ustage_prev:
                vw.append(ustage_prev[par])
            P.add("v", "tt", zA, tre, K(0), ALU.mult, waits=vw)
            P.add("v", "tt", zB, tim, K(1), ALU.mult)
            P.add("v", "tt", Z(0), zA, zB, ALU.subtract)
            P.add("v", "tt", zA, tre, K(1), ALU.mult)
            P.add("v", "tt", zB, tim, K(0), ALU.mult)
            P.add("v", "tt", Z(1), zA, zB, ALU.add, incs=[("fv", 1)])
            z_done[par] = ("fv", P.n("fv"))
            uw = [z_done[par]]
            if u_prev is not None:
                uw.append(u_prev)
            P.add("t", "mm", pA[:, 0:512], c.BDAre[0:72, 0:128], Z(0), True, False, waits=uw)
            P.add("t", "mm", pA[:, 0:512], c.BDAim_neg[0:72, 0:128], Z(1), False, True)
            P.add("t", "mm", pB[:, 0:512], c.BDAim[0:72, 0:128], Z(0), True, False)
            P.add("t", "mm", pB[:, 0:512], c.BDAre[0:72, 0:128], Z(1), False, True,
                  incs=[("ft", 1)])
            ustage_prev[par] = ("ft", P.n("ft"))
            sw = [("ft", P.n("ft"))]
            if par in usb_turn:
                sw.append(usb_turn[par])
            P.add("s", "copy", b.Usb[:, par * 1024: par * 1024 + 512], pA[:, 0:512],
                  waits=sw)
            P.add("s", "act", b.Usb[:, par * 1024 + 512: par * 1024 + 1024], pB[:, 0:512],
                  ACTF.Copy, 0.0, -1.0, incs=[("fs", 1)])
            u_prev = ("fs", P.n("fs"))
            trs = []
            for ac in range(2):
                for im in range(2):
                    for lg in range(2):
                        off = par * 1024 + im * 512 + lg * 256 + ac * 128
                        trs.append(b.Usb[:, off: off + 128])
            g0 = pair * 2
            dest = b.UT[:].rearrange("p (aci gg f) -> p aci gg f", aci=4, gg=32)[:, :, g0:g0 + 2, :]
            self.tr_group(trs, dest, "v" if par == 0 else "s", extra_waits=[u_prev])
            usb_turn[par] = ("ft", P.n("ft"))
        bar2 = (("fv", P.n("fv")), ("fs", P.n("fs")))
        py_prev = {}
        tb_done = {}
        for nu in range(16):
            par = nu % 2
            bank = pXre[par]
            w = [bar2[0], bar2[1]] if nu < 2 else []
            if par in py_prev:
                w.append(py_prev[par])
            if nu < 2 and par in x_prev:
                w.append(x_prev[par])
            first = True
            for ac in range(2):
                hoff = (nu * 2 + ac) * 128
                UTr = b.UT[:, 0 * 4096 + 0:].rearrange("p (aci gg f) -> p aci gg f", aci=4, gg=32)
                base_re = (ac * 2 + 0) * 4096
                base_im = (ac * 2 + 1) * 4096
                UTre = b.UT[:, base_re:base_re + 4096].rearrange(
                    "p (g c2 n2) -> p g c2 n2", c2=8, n2=16)[:, :, :, nu]
                UTim = b.UT[:, base_im:base_im + 4096].rearrange(
                    "p (g c2 n2) -> p g c2 n2", c2=8, n2=16)[:, :, :, nu]
                P.add("t", "mm", bank[:, 0:256], self.c.Hre[:, hoff:hoff + 128], UTre,
                      first, False, waits=w if first else ())
                first = False
                P.add("t", "mm", bank[:, 0:256], self.c.Him[:, hoff:hoff + 128], UTim,
                      False, ac == 1, incs=[("ft", 1)] if ac == 1 else ())
            cons = "v" if par == 0 else "s"
            cn = "fv" if par == 0 else "fs"
            yw = [("ft", P.n("ft"))]
            if par in tb_done:
                yw.append(tb_done[par])
            P.add(cons, "copy", self.b.ysb[:, par * 256: par * 256 + 256], bank[:, 0:256],
                  waits=yw, incs=[(cn, 1)])
            py_prev[par] = (cn, P.n(cn))
            trs = [self.b.ysb[:, par * 256 + cq * 128: par * 256 + (cq + 1) * 128]
                   for cq in range(2)]
            dest = ych_ap.rearrange("p (cq t) -> p cq t", cq=2)[:, :, nu:2048:16]
            self.tr_group(trs, dest, "v" if par == 0 else "s",
                          extra_waits=[py_prev[par]])
            tb_done[par] = ("ft", P.n("ft"))



def _build(debug=()):
    nc = bass.Bass("TRN2", target_bir_lowering=False)

    din = {}
    def DI(name, shape, dt=F32):
        din[name] = nc.dram_tensor(name, list(shape), dt, kind="ExternalInput")
        return din[name]

    u_own = DI("u_own", [TOK, D])
    u_ctx = DI("u_ctx", [TOK, D])
    uT_own = DI("uT_own", [D, TOK])
    ctxm = DI("ctxm", [128, 1])
    for w in ("WqT", "WkT", "WvT", "WoT"):
        DI(w, [D, D])
    for bname in ("bq", "bk", "bv", "bo", "cbq", "cbk", "dskip"):
        DI(bname, [128, 16])
    DI("cwq", [128, 48])
    DI("cwk", [128, 48])
    DI("ker_sl", [256, D])
    DI("gw_s", [1024, D])
    DI("uw_s", [1024, D])
    DI("dw_s", [256, I_FF])
    fc = _fft_constants()
    for name, arr in fc.items():
        DI("c_" + name, arr.shape, BF16 if arr.dtype == BF else F32)

    outT = nc.dram_tensor("outT", [D, TOK], F32, kind="ExternalOutput")

    wbf = {w: nc.dram_tensor("bf_" + w, [D, D], BF16, kind="Internal")
           for w in ("WqT", "WkT", "WvT", "WoT")}
    kv_scr = nc.dram_tensor("kv_scr", [D, WIN], BF16, kind="Internal")
    q_scr = nc.dram_tensor("q_scr", [D, TOK], BF16, kind="Internal")
    y_scr = nc.dram_tensor("y_scr", [D, TOK], BF16, kind="Internal")
    attn_scr = nc.dram_tensor("attn_scr", [D, TOK], BF16, kind="Internal")
    xdq_scr = nc.dram_tensor("xdq_scr", [D, TOK], BF16, kind="Internal")
    xn_scr = nc.dram_tensor("xn_scr", [TOK, D], BF16, kind="Internal")
    h_scr = nc.dram_tensor("h_scr", [I_FF, TOK], BF16, kind="Internal")
    ccK_in = nc.dram_tensor("ccK_in", [16, 72, 1024], BF16, kind="Internal")
    ccK_out = nc.dram_tensor("ccK_out", [8, 16, 72, 1024], BF16, kind="Internal")
    ccW_in = nc.dram_tensor("ccW_in", [6144, 1024], BF16, kind="Internal")
    ccW_out = nc.dram_tensor("ccW_out", [8, 6144, 1024], BF16, kind="Internal")
    dbg = {}
    for name in debug:
        scr = dict(kv_scr=kv_scr, q_scr=q_scr, y_scr=y_scr, attn_scr=attn_scr,
                   xdq_scr=xdq_scr, h_scr=h_scr, xn_scr=xn_scr, ccW_in=ccW_in,
                   ccK_in=ccK_in)[name]
        dbg[name] = (scr, nc.dram_tensor("dbg_" + name, list(scr.shape), BF16,
                                         kind="ExternalOutput"))

    with ExitStack() as persist:
        ent = persist.enter_context

        class _Consts:
            pass
        c = _Consts()
        for name in fc:
            tgt = "ident" if name == "identity" else name
            setattr(c, tgt, ent(nc.sbuf_tensor("k_" + tgt, list(fc[name].shape),
                                               BF16 if fc[name].dtype == BF else F32)))
        biases = ent(nc.sbuf_tensor("biases", [128, 16 * 7], F32))
        convw = ent(nc.sbuf_tensor("convw", [128, 16 * 6], F32))
        maskv = ent(nc.sbuf_tensor("maskv", [128, 1], F32))
        epsv = ent(nc.sbuf_tensor("epsv", [128, 1], F32))
        bm = ent(nc.sbuf_tensor("bm", [128, 16 * 3], F32))

        def bias_ap(idx, ecn):
            return biases[:, idx * 16 + ecn: idx * 16 + ecn + 1]

        def convw_ap(which, ecn, j):
            return convw[:, which * 48 + ecn * 3 + j: which * 48 + ecn * 3 + j + 1]

        pA = ent(nc.psum_tensor("pA", [128, 512], F32))
        pB = ent(nc.psum_tensor("pB", [128, 512], F32))
        ptr0 = ent(nc.psum_tensor("ptr0", [128, 1024], BF16))
        ptr1 = ent(nc.psum_tensor("ptr1", [128, 1024], BF16))
        pC = ent(nc.psum_tensor("pC", [128, 512], F32))
        pD = ent(nc.psum_tensor("pD", [128, 512], F32))
        pE = ent(nc.psum_tensor("pE", [128, 512], F32))
        pF = ent(nc.psum_tensor("pF", [128, 512], F32))
        pb = {"pA": pA, "pB": pB, "ptr": [ptr0, ptr1],
              "pXre": [pC, pD], "pXim": [pE, pF]}

        P = _Plan(nc)

        for name in fc:
            tgt = "ident" if name == "identity" else name
            rows = fc[name].shape[0]
            P.add("q", "dma", getattr(c, tgt)[0:rows, :], din["c_" + name][:],
                  incs=[("ld", 16)])
        for i, bname in enumerate(("bq", "bk", "bv", "bo", "cbq", "cbk", "dskip")):
            P.add("q", "dma", biases[:, i * 16:(i + 1) * 16], din[bname][:],
                  incs=[("ld", 16)])
        P.add("q", "dma", convw[:, 0:48], din["cwq"][:], incs=[("ld", 16)])
        P.add("q", "dma", convw[:, 48:96], din["cwk"][:], incs=[("ld", 16)])
        P.add("q", "dma", maskv[:], ctxm[:], incs=[("ld", 16)])
        n_const_ld = P.n("ld")
        P.add("v", "memset", epsv[:], 1e-5)
        P.add("v", "ts", bm[:, 0:48], biases[:, 0:48], maskv[:, 0:1], None, ALU.mult, None,
              waits=[("ld", n_const_ld)], incs=[("fv", 1)])

        with ExitStack() as sc1:
            e1 = sc1.enter_context
            kert = [e1(nc.sbuf_tensor(f"kert{i}", [128, 2048], BF16)) for i in range(2)]
            for i in range(2):
                P.add("g", "dma", kert[i][:], din["ker_sl"][i * 128:(i + 1) * 128, :],
                      incs=[("gk", 16)])
            n_kert = P.n("gk")
            for w in ("WqT", "WkT", "WvT", "WoT"):
                P.add("g", "dma", wbf[w][:], din[w][:], incs=[("gd", 16)])
            n_wcast = P.n("gd")

            bufsA = _ConvBufs(nc, sc1)
            cv = _Conv(P, bufsA, c, pb)
            cv.emit_ker(kert, ccK_in[:], extra_first=[("gk", n_kert), ("ld", n_const_ld)])
            nkq_ker = (P.n("kq0"), P.n("kq1"))

            wtile = e1(nc.sbuf_tensor("wtile", [128, 2 * 2048], F32))
            wq8 = e1(nc.sbuf_tensor("wq8", [128, 2048], FP8))
            wdq = e1(nc.sbuf_tensor("wdq", [128, 2048], BF16))
            wsc = e1(nc.sbuf_tensor("wsc", [128, 4 * 64], F32))
            wstage = e1(nc.sbuf_tensor("wstage", [128, 1024], BF16))

            def wtile_src(j):
                if j < 8:
                    return din["gw_s"][j * 128:(j + 1) * 128, :]
                if j < 16:
                    return din["uw_s"][(j - 8) * 128:(j - 8 + 1) * 128, :]
                jj = j - 16
                r, qq = jj // 4, jj % 4
                return din["dw_s"][r * 128:(r + 1) * 128, qq * 2048:(qq + 1) * 2048]

            def wtile_dst(j, hc4, stg):
                if j < 16:
                    base = 0 if j < 8 else 2048
                    col = (j % 8) * 128
                    rows = ccW_in[base + hc4 * 512: base + (hc4 + 1) * 512, col:col + 128]
                    return rows.rearrange("(b p) i -> p b i", p=128)
                jj = j - 16
                r, qq = jj // 4, jj % 4
                rows = ccW_in[4096 + hc4 * 512: 4096 + (hc4 + 1) * 512,
                              qq * 256 + r * 128: qq * 256 + (r + 1) * 128]
                return rows.rearrange("(b p) e -> p b e", p=128)

            wq_done = {}
            wstage_guard = {}
            wt_prev = None
            for j in range(24):
                par = j % 2
                lw = []
                if par in wq_done:
                    lw.append(wq_done[par])
                P.add("q", "dma", wtile[:, par * 2048:(par + 1) * 2048], wtile_src(j),
                      waits=lw, incs=[("lwt%d" % par, 16)])
                nld = P.n("lwt%d" % par)
                src = wtile[:, par * 2048:(par + 1) * 2048]
                vw = [("lwt%d" % par, nld)]
                if wt_prev is not None:
                    vw.append(wt_prev)
                P.add("v", "red", wsc[:, 0:64], src.rearrange("p (b k) -> p b k", k=32),
                      ALU.max, True, waits=vw, incs=[("fv", 1)])
                P.add("v", "ts", wsc[:, 64:128], wsc[:, 0:64], 1.0 / 448.0, 1e-30,
                      ALU.mult, ALU.max, waits=[("fv", P.n("fv"))], incs=[("fv", 1)])
                P.add("v", "recip", wsc[:, 128:192], wsc[:, 64:128],
                      waits=[("fv", P.n("fv"))], incs=[("fv", 1)])
                P.add("v", "ts", wsc[:, 192:256], wsc[:, 64:128], 2.0, None, ALU.mult, None,
                      waits=[("fv", P.n("fv"))], incs=[("fv", 1)])
                P.add("v", "stt", wq8[:].rearrange("p (b k) -> p b k", k=32),
                      src.rearrange("p (b k) -> p b k", k=32), 0.5,
                      wsc[:, 128:192].unsqueeze(-1).broadcast_to([128, 64, 32]),
                      ALU.mult, ALU.mult, waits=[("fv", P.n("fv"))])
                P.add("v", "tt", wdq[:].rearrange("p (b k) -> p b k", k=32),
                      wq8[:].rearrange("p (b k) -> p b k", k=32),
                      wsc[:, 192:256].unsqueeze(-1).broadcast_to([128, 64, 32]),
                      ALU.mult, incs=[("fv", 1)])
                wq_done[par] = ("fv", P.n("fv"))
                for hc4 in range(4):
                    gslot = cv.ptr_grp % 2
                    stg = wstage[:, gslot * 512:(gslot + 1) * 512]
                    trs = [wdq[:, (hc4 * 4 + i2) * 128:(hc4 * 4 + i2 + 1) * 128]
                           for i2 in range(4)]
                    wsem = "wd%d" % gslot
                    ee = [(wsem, wstage_guard[gslot])] if gslot in wstage_guard else []
                    cv.tr_group(trs, stg, "v", extra_waits=[wq_done[par]], evac_extra=ee)
                    P.add("q", "dma", wtile_dst(j, hc4, stg),
                          stg.rearrange("p (b f) -> p b f", b=4),
                          waits=[cv.ptr_evac[cv.ptr_grp - 1]], incs=[(wsem, 16)])
                    wstage_guard[gslot] = P.n(wsem)
                wt_prev = ("ft", P.n("ft"))
            n_wd = (P.n("wd0"), P.n("wd1"))

        P.add("g", "coll", "AllGather", ALU.bypass, [[0, 1, 2, 3, 4, 5, 6, 7]],
              [ccK_in[:]], [ccK_out[:]],
              waits=[("kq0", nkq_ker[0]), ("kq1", nkq_ker[1])], incs=[("ck", 1)])
        P.add("g", "coll", "AllGather", ALU.bypass, [[0, 1, 2, 3, 4, 5, 6, 7]],
              [ccW_in[:]], [ccW_out[:]],
              waits=[("wd0", n_wd[0]), ("wd1", n_wd[1])], incs=[("cw", 1)])

        with ExitStack() as sc3:
            e3 = sc3.enter_context
            ubuf = e3(nc.sbuf_tensor("ubuf", [128, 2 * 2048], F32))
            xsq = e3(nc.sbuf_tensor("xsq", [128, 2048], F32))
            xnf = e3(nc.sbuf_tensor("xnf", [128, 2048], F32))
            cscr = e3(nc.sbuf_tensor("cscr", [128, 2 * 1024], F32))
            xnb = e3(nc.sbuf_tensor("xnb", [128, 2 * 2048], BF16))
            stats = e3(nc.sbuf_tensor("stats", [128, 32], F32))
            lnsc = e3(nc.sbuf_tensor("lnsc", [128, 8], F32))
            xnT = e3(nc.sbuf_tensor("xnT", [128, 16 * 1024], BF16))
            wslab = e3(nc.sbuf_tensor("wslab", [128, 2 * 2048], BF16))
            kraw = e3(nc.sbuf_tensor("kraw", [128, 2 * 1028], BF16))
            vraw = e3(nc.sbuf_tensor("vraw", [128, 2 * 1024], BF16))
            qraw = e3(nc.sbuf_tensor("qraw", [128, 2 * 1028], BF16))
            ktail = e3(nc.sbuf_tensor("ktail", [128, 16 * 4], BF16))
            qtail = e3(nc.sbuf_tensor("qtail", [128, 16 * 4], BF16))
            kvec = e3(nc.sbuf_tensor("kvec", [128, 2 * 1024], BF16))
            qcec = e3(nc.sbuf_tensor("qcec", [128, 2 * 1024], BF16))
            xq8 = e3(nc.sbuf_tensor("xq8", [128, 2048], FP8))
            xdq = e3(nc.sbuf_tensor("xdq", [128, 2048], BF16))
            xsc = e3(nc.sbuf_tensor("xsc", [128, 4 * 64], F32))

            uload_done = {}
            xstage_guard = {}
            xn_done = {}
            wsl_done = {}
            conv_done_par = {}
            kv_dma_par = {}
            qc_dma_par = {}
            pp_evac = {}
            pp_k = 0

            def ln_tile(gi, own):
                par = gi % 2
                src = u_own if own else u_ctx
                row0 = (gi * 128) % 2048
                lw = []
                if par in uload_done:
                    lw.append(uload_done[par])
                P.add("q", "dma", ubuf[:, par * 2048:(par + 1) * 2048],
                      src[row0:row0 + 128, :], waits=lw, incs=[("lu%d" % par, 16)])
                u_ap = ubuf[:, par * 2048:(par + 1) * 2048]
                nld_u = P.n("lu%d" % par)
                vw = [("lu%d" % par, nld_u)]
                if gi > 0:
                    vw.append(xn_done[gi - 1])
                P.add("s", "act", xsq[:], u_ap, ACTF.Square, 0.0, 1.0,
                      waits=[("lu%d" % par, nld_u)] + ([xn_done[gi - 1]] if gi > 0 else []),
                      incs=[("fs", 1)])
                nfs_sq = P.n("fs")
                P.add("v", "red", stats[:, 0:1], u_ap.unsqueeze(1), ALU.add, False,
                      waits=vw)
                P.add("v", "red", stats[:, 1:2], xsq[:].unsqueeze(1), ALU.add, False,
                      waits=[("fs", nfs_sq)], incs=[("fv", 1)])
                P.add("v", "ts", stats[:, 2:3], stats[:, 0:1], 1.0 / 2048.0, None,
                      ALU.mult, None, waits=[("fv", P.n("fv"))], incs=[("fv", 1)])
                P.add("v", "tt", stats[:, 3:4], stats[:, 2:3], stats[:, 2:3], ALU.mult,
                      waits=[("fv", P.n("fv"))], incs=[("fv", 1)])
                P.add("v", "ts", stats[:, 4:5], stats[:, 1:2], 1.0 / 2048.0,
                      stats[:, 3:4], ALU.mult, ALU.subtract,
                      waits=[("fv", P.n("fv"))], incs=[("fv", 1)])
                P.add("s", "act", lnsc[:, 0:1], stats[:, 4:5], ACTF.Ln, epsv[:, 0:1], 1.0,
                      waits=[("fv", P.n("fv"))])
                P.add("s", "act", lnsc[:, 1:2], lnsc[:, 0:1], ACTF.Exp, 0.0, -0.5,
                      incs=[("fs", 1)])
                P.add("v", "ts", xnf[:], u_ap, stats[:, 2:3], lnsc[:, 1:2],
                      ALU.subtract, ALU.mult, waits=[("fs", P.n("fs"))], incs=[("fv", 1)])
                uload_done[par] = ("fv", P.n("fv"))
                P.add("v", "copy", xnb[:, par * 2048:(par + 1) * 2048], xnf[:],
                      incs=[("fv", 1)])
                xn_done[gi] = ("fv", P.n("fv"))
                if own and _DEBUG:
                    P.add("q", "dma", xn_scr[(gi * 128) % 2048:(gi * 128) % 2048 + 128, :],
                          xnb[:, par * 2048:(par + 1) * 2048], waits=[xn_done[gi]],
                          incs=[("xs", 16)])
                if own:
                    row0o = row0
                    P.add("v", "red", xsc[:, 0:64], xnf[:].rearrange("p (b k) -> p b k", k=32),
                          ALU.max, True, incs=[("fv", 1)])
                    P.add("v", "ts", xsc[:, 64:128], xsc[:, 0:64], 1.0 / 448.0, 1e-30,
                          ALU.mult, ALU.max, waits=[("fv", P.n("fv"))], incs=[("fv", 1)])
                    P.add("v", "recip", xsc[:, 128:192], xsc[:, 64:128],
                          waits=[("fv", P.n("fv"))], incs=[("fv", 1)])
                    P.add("v", "ts", xsc[:, 192:256], xsc[:, 64:128], 2.0, None, ALU.mult, None,
                          waits=[("fv", P.n("fv"))], incs=[("fv", 1)])
                    P.add("v", "stt", xq8[:].rearrange("p (b k) -> p b k", k=32),
                          xnf[:].rearrange("p (b k) -> p b k", k=32), 0.5,
                          xsc[:, 128:192].unsqueeze(-1).broadcast_to([128, 64, 32]),
                          ALU.mult, ALU.mult, waits=[("fv", P.n("fv"))])
                    P.add("v", "tt", xdq[:].rearrange("p (b k) -> p b k", k=32),
                          xq8[:].rearrange("p (b k) -> p b k", k=32),
                          xsc[:, 192:256].unsqueeze(-1).broadcast_to([128, 64, 32]),
                          ALU.mult, incs=[("fv", 1)])
                    nq = ("fv", P.n("fv"))
                    for hc4 in range(4):
                        gslot = cv.ptr_grp % 2
                        stg = wstage_x[:, gslot * 512:(gslot + 1) * 512]
                        trs = [xdq[:, (hc4 * 4 + i2) * 128:(hc4 * 4 + i2 + 1) * 128]
                               for i2 in range(4)]
                        xsem = "xd%d" % gslot
                        ee = [(xsem, xstage_guard[gslot])] if gslot in xstage_guard else []
                        cv.tr_group(trs, stg, "v", extra_waits=[nq], evac_extra=ee)
                        dest = xdq_scr[hc4 * 512:(hc4 + 1) * 512, row0o:row0o + 128].rearrange(
                            "(b p) t -> p b t", p=128)
                        P.add("q", "dma", dest, stg.rearrange("p (b f) -> p b f", b=4),
                              waits=[cv.ptr_evac[cv.ptr_grp - 1]], incs=[(xsem, 16)])
                        xstage_guard[gslot] = P.n(xsem)

            wstage_x = e3(nc.sbuf_tensor("wstage_x", [128, 1024], BF16))

            def xn_tr(gi, gb):
                par = gi % 2
                src = xnb[:, par * 2048:(par + 1) * 2048]
                for hc4 in range(4):
                    trs = [src[:, (hc4 * 4 + i2) * 128:(hc4 * 4 + i2 + 1) * 128]
                           for i2 in range(4)]
                    dest = xnT[:].rearrange("p (hc t) -> p hc t", hc=16)[
                        :, hc4 * 4:(hc4 + 1) * 4, gb * 128:(gb + 1) * 128]
                    cv.tr_group(trs, dest, "v" if hc4 % 2 == 0 else "s",
                                extra_waits=[xn_done[gi]])

            qtsave = e3(nc.sbuf_tensor("qtsave", [128, 32], BF16))
            qmini_prev = None
            qts_bar = None
            for grp in range(4):
                own_grp = grp >= 2
                for gb in range(8):
                    gi = grp * 8 + gb
                    ln_tile(gi, own_grp)
                    xn_tr(gi, gb)
                xnt_bar = (("fv", P.n("fv")), ("fs", P.n("fs")))
                if grp == 1:
                    P.add("v", "copy", qtsave[:].rearrange("p (hc t) -> p hc t", t=2),
                          xnT[:].rearrange("p (hc t) -> p hc t", hc=16)[:, :, 1022:1024],
                          waits=[xnt_bar[0], xnt_bar[1]], incs=[("fv", 1)])
                    qts_bar = ("fv", P.n("fv"))
                projs = ([("WqT", 0, qraw, True)] if own_grp else []) + \
                    [("WkT", 1, kraw, True), ("WvT", 2, vraw, False)]
                for ec in range(16):
                    raw_done = {}
                    rpar = ec % 2
                    for wname, bidx, rawbuf, haspad in projs:
                        wpar = pp_k % 2
                        lw = []
                        if wpar in wsl_done:
                            lw.append(wsl_done[wpar])
                        if grp == 0 and ec == 0 and wname == projs[0][0]:
                            lw = lw + [("gd", n_wcast)]
                        P.add("q", "dma", wslab[:, wpar * 2048:(wpar + 1) * 2048].rearrange(
                            "p (hc f) -> p hc f", hc=16),
                            wbf[wname][:, ec * 128:(ec + 1) * 128].rearrange(
                            "(hc p) f -> p hc f", p=128), waits=lw, incs=[("ls%d" % wpar, 16)])
                        nld = P.n("ls%d" % wpar)
                        pad = 4 if haspad else 0
                        rw = rawbuf[:, rpar * (1024 + pad): rpar * (1024 + pad) + 1024 + pad]
                        for blk2 in range(2):
                            bank = pA if pp_k % 2 == 0 else pB
                            cons = "v" if pp_k % 2 == 0 else "s"
                            waits = [("ls%d" % wpar, nld)]
                            if ec == 0 and wname == projs[0][0] and blk2 == 0:
                                waits += [xnt_bar[0], xnt_bar[1]]
                            if pp_k - 2 in pp_evac:
                                waits.append(pp_evac[pp_k - 2])
                            for hc in range(16):
                                P.add("t", "mm", bank[:, 0:512],
                                      wslab[:, wpar * 2048 + hc * 128: wpar * 2048 + (hc + 1) * 128],
                                      xnT[:].rearrange("p (hc t) -> p hc t", hc=16)[
                                          :, hc, blk2 * 512:(blk2 + 1) * 512],
                                      hc == 0, hc == 15,
                                      waits=waits if hc == 0 else (),
                                      incs=[("ft", 1)] if hc == 15 else ())
                            dst = rw[:, pad + blk2 * 512: pad + (blk2 + 1) * 512]
                            cn = "fv" if cons == "v" else "fs"
                            evw = [("ft", P.n("ft"))]
                            if rpar in conv_done_par:
                                evw.append(conv_done_par[rpar])
                            if own_grp:
                                if cons == "v":
                                    P.add(cons, "ts", dst, bank[:, 0:512], 1.0,
                                          bias_ap(bidx, ec), ALU.mult, ALU.add,
                                          waits=evw, incs=[(cn, 1)])
                                else:
                                    P.add(cons, "act", dst, bank[:, 0:512], ACTF.Identity,
                                          bias_ap(bidx, ec), 1.0,
                                          waits=evw, incs=[(cn, 1)])
                            else:
                                if cons == "v":
                                    P.add(cons, "ts", dst, bank[:, 0:512], maskv[:, 0:1],
                                          bm[:, bidx * 16 + ec: bidx * 16 + ec + 1],
                                          ALU.mult, ALU.add,
                                          waits=evw, incs=[(cn, 1)])
                                else:
                                    P.add(cons, "act", dst, bank[:, 0:512], ACTF.Identity,
                                          bm[:, bidx * 16 + ec: bidx * 16 + ec + 1],
                                          maskv[:, 0:1],
                                          waits=evw, incs=[(cn, 1)])
                            pp_evac[pp_k] = (cn, P.n(cn))
                            raw_done[wname] = (cn, P.n(cn))
                            pp_k += 1
                        if wname == "WqT" and grp == 2:
                            mw = [qts_bar]
                            if qmini_prev is not None:
                                mw.append(qmini_prev)
                            for hc in range(16):
                                P.add("t", "mm", pb["pXre"][0][:, 0:2],
                                      wslab[:, wpar * 2048 + hc * 128: wpar * 2048 + (hc + 1) * 128],
                                      qtsave[:, hc * 2:(hc + 1) * 2],
                                      hc == 0, hc == 15,
                                      waits=mw if hc == 0 else (),
                                      incs=[("ft", 1)] if hc == 15 else ())
                            qw0 = qraw[:, rpar * 1028: rpar * 1028 + 1028]
                            P.add("v", "ts", qw0[:, 2:4], pb["pXre"][0][:, 0:2],
                                  maskv[:, 0:1], bm[:, 0 * 16 + ec: 0 * 16 + ec + 1],
                                  ALU.mult, ALU.add,
                                  waits=[("ft", P.n("ft"))], incs=[("fv", 1)])
                            qmini_prev = ("fv", P.n("fv"))
                        wsl_done[wpar] = ("ft", P.n("ft"))
                    kw_ = kraw[:, rpar * 1028: rpar * 1028 + 1028]
                    vw_ = vraw[:, rpar * 1024: rpar * 1024 + 1024]
                    tw = [raw_done["WkT"], raw_done["WvT"]]
                    if own_grp:
                        tw.append(raw_done["WqT"])
                    if grp == 0:
                        P.add("v", "memset", kw_[:, 0:4], 0.0, waits=tw)
                    else:
                        P.add("v", "copy", kw_[:, 0:4], ktail[:, ec * 4:(ec + 1) * 4], waits=tw)
                    P.add("v", "ts", cscr[:, 0:1024], kw_[:, 2:1026], convw_ap(1, ec, 0),
                          bias_ap(5, ec), ALU.mult, ALU.add)
                    P.add("v", "stt", cscr[:, 1024:2048], kw_[:, 3:1027], convw_ap(1, ec, 1),
                          cscr[:, 0:1024], ALU.mult, ALU.add)
                    P.add("v", "stt", cscr[:, 0:1024], kw_[:, 4:1028], convw_ap(1, ec, 2),
                          cscr[:, 1024:2048], ALU.mult, ALU.add)
                    P.add("v", "copy", ktail[:, ec * 4:(ec + 1) * 4], kw_[:, 1024:1028])
                    kvw2 = [("kv%d" % rpar, kv_dma_par[rpar])] if rpar in kv_dma_par else []
                    P.add("v", "tt", kvec[:, rpar * 1024:(rpar + 1) * 1024],
                          cscr[:, 0:1024], vw_, ALU.mult, waits=kvw2, incs=[("fv", 1)])
                    kv_done = ("fv", P.n("fv"))
                    P.add("q", "dma",
                          kv_scr[ec * 128:(ec + 1) * 128, grp * 1024:(grp + 1) * 1024],
                          kvec[:, rpar * 1024:(rpar + 1) * 1024],
                          waits=[kv_done], incs=[("kv%d" % rpar, 16)])
                    kv_dma_par[rpar] = P.n("kv%d" % rpar)
                    if own_grp:
                        qw_ = qraw[:, rpar * 1028: rpar * 1028 + 1028]
                        if grp != 2:
                            P.add("v", "copy", qw_[:, 0:4], qtail[:, ec * 4:(ec + 1) * 4])
                        P.add("v", "ts", cscr[:, 0:1024], qw_[:, 2:1026], convw_ap(0, ec, 0),
                              bias_ap(4, ec), ALU.mult, ALU.add)
                        P.add("v", "stt", cscr[:, 1024:2048], qw_[:, 3:1027], convw_ap(0, ec, 1),
                              cscr[:, 0:1024], ALU.mult, ALU.add)
                        P.add("v", "stt", cscr[:, 0:1024], qw_[:, 4:1028], convw_ap(0, ec, 2),
                              cscr[:, 1024:2048], ALU.mult, ALU.add)
                        P.add("v", "copy", qtail[:, ec * 4:(ec + 1) * 4], qw_[:, 1024:1028])
                        qcw2 = [("qc%d" % rpar, qc_dma_par[rpar])] if rpar in qc_dma_par else []
                        P.add("v", "copy", qcec[:, rpar * 1024:(rpar + 1) * 1024],
                              cscr[:, 0:1024], waits=qcw2, incs=[("fv", 1)])
                        P.add("q", "dma",
                              q_scr[ec * 128:(ec + 1) * 128, (grp - 2) * 1024:(grp - 1) * 1024],
                              qcec[:, rpar * 1024:(rpar + 1) * 1024],
                              waits=[("fv", P.n("fv"))], incs=[("qc%d" % rpar, 16)])
                        qc_dma_par[rpar] = P.n("qc%d" % rpar)
                    conv_done_par[rpar] = ("fv", P.n("fv"))
            kv_all = [(n, P.n(n)) for n in ("kv0", "kv1", "qc0", "qc1")]

        with ExitStack() as sc4:
            bufsB = _ConvBufs(nc, sc4)
            cv.b = bufsB
            kvt = [sc4.enter_context(nc.sbuf_tensor(f"kvt{i}", [128, 4096], BF16))
                   for i in range(2)]
            ych = sc4.enter_context(nc.sbuf_tensor("ych", [128, 2 * 2048], BF16))
            for ch in range(8):
                lw = list(kv_all)
                if ch > 0:
                    lw.append(("ft", P.n("ft")))
                for i in range(2):
                    P.add("q", "dma", kvt[i][:],
                          kv_scr[(2 * ch + i) * 128:(2 * ch + i + 1) * 128, :],
                          waits=lw if i == 0 else (), incs=[("kt", 16)])
                nld = P.n("kt")
                ef = [("kt", nld)] + ([("st", 16 * ch)] if ch > 0 else [])
                cv.strided_tr(kvt, 2, extra_first=ef)
                _emit_kv_rest(P, cv, ccK_out[ch], ych[:], k_wait=("ck", 1))
                P.add("q", "dma", y_scr[2 * ch * 128:(2 * ch + 2) * 128, :].rearrange(
                    "(cq p) t -> p cq t", cq=2), ych[:],
                    waits=[("fv", P.n("fv")), ("fs", P.n("fs"))], incs=[("st", 16)])
            n_yst = P.n("st")

        with ExitStack() as sc5:
            e5 = sc5.enter_context
            yblk = e5(nc.sbuf_tensor("yblk", [128, 16 * 512], BF16))
            kvob = e5(nc.sbuf_tensor("kvob", [128, 16 * 512], BF16))
            qcb = e5(nc.sbuf_tensor("qcb", [128, 16 * 512], BF16))
            yTb = e5(nc.sbuf_tensor("yTb", [128, 16 * 512], BF16))
            atb = e5(nc.sbuf_tensor("atb", [128, 16 * 512], BF16))
            wslab5 = e5(nc.sbuf_tensor("wslab5", [128, 2 * 2048], BF16))
            cscr5 = e5(nc.sbuf_tensor("cscr5", [128, 512], F32))
            wsl_done5 = {}
            wo_evac = {}
            p5_prev = None
            for blk in range(4):
                t0 = blk * 512
                lw = [("st", n_yst)]
                if p5_prev is not None:
                    lw.append(p5_prev)
                P.add("q", "dma", yblk[:].rearrange("p (hc t) -> p hc t", hc=16),
                      y_scr[:, t0:t0 + 512].rearrange("(hc p) t -> p hc t", p=128),
                      waits=lw, incs=[("l5", 16)])
                P.add("q", "dma", kvob[:].rearrange("p (hc t) -> p hc t", hc=16),
                      kv_scr[:, 2048 + t0: 2048 + t0 + 512].rearrange("(hc p) t -> p hc t", p=128),
                      incs=[("l5", 16)])
                P.add("q", "dma", qcb[:].rearrange("p (hc t) -> p hc t", hc=16),
                      q_scr[:, t0:t0 + 512].rearrange("(hc p) t -> p hc t", p=128),
                      incs=[("l5", 16)])
                nld = P.n("l5")
                for ec in range(16):
                    sl = slice(ec * 512, (ec + 1) * 512)
                    vw2 = [("l5", nld)] if ec == 0 else []
                    if ec == 0 and p5_prev is not None:
                        vw2.append(p5_prev)
                    P.add("v", "stt", cscr5[:, 0:512], kvob[:, sl],
                          bias_ap(6, ec), yblk[:, sl], ALU.mult, ALU.add, waits=vw2)
                    P.add("v", "tt", yTb[:, sl], cscr5[:, 0:512], qcb[:, sl], ALU.mult,
                          incs=[("fv", 1)] if ec == 15 else ())
                ybar = ("fv", P.n("fv"))
                for ec in range(16):
                    wpar = ec % 2
                    lw2 = []
                    if blk == 0 and ec == 0:
                        lw2.append(("gd", n_wcast))
                    if wpar in wsl_done5:
                        lw2.append(wsl_done5[wpar])
                    P.add("q", "dma", wslab5[:, wpar * 2048:(wpar + 1) * 2048].rearrange(
                        "p (hc f) -> p hc f", hc=16),
                        wbf["WoT"][:, ec * 128:(ec + 1) * 128].rearrange(
                        "(hc p) f -> p hc f", p=128), waits=lw2, incs=[("l5w%d" % wpar, 16)])
                    nld2 = P.n("l5w%d" % wpar)
                    bank = pA if ec % 2 == 0 else pB
                    cons = "v" if ec % 2 == 0 else "s"
                    waits = [("l5w%d" % wpar, nld2), ybar]
                    if ec - 2 in wo_evac:
                        waits.append(wo_evac[ec - 2])
                    for hc in range(16):
                        P.add("t", "mm", bank[:, 0:512],
                              wslab5[:, wpar * 2048 + hc * 128: wpar * 2048 + (hc + 1) * 128],
                              yTb[:, hc * 512:(hc + 1) * 512],
                              hc == 0, hc == 15,
                              waits=waits if hc == 0 else (),
                              incs=[("ft", 1)] if hc == 15 else ())
                    cn = "fv" if cons == "v" else "fs"
                    ew2 = [("ft", P.n("ft"))]
                    if p5_prev is not None and ec < 2:
                        ew2.append(p5_prev)
                    if cons == "v":
                        P.add(cons, "ts", atb[:, ec * 512:(ec + 1) * 512], bank[:, 0:512], 1.0,
                              bias_ap(3, ec), ALU.mult, ALU.add,
                              waits=ew2, incs=[(cn, 1)])
                    else:
                        P.add(cons, "act", atb[:, ec * 512:(ec + 1) * 512], bank[:, 0:512],
                              ACTF.Identity, bias_ap(3, ec), 1.0,
                              waits=ew2, incs=[(cn, 1)])
                    wo_evac[ec] = (cn, P.n(cn))
                    wsl_done5[wpar] = ("ft", P.n("ft"))
                wo_evac = {}
                P.add("q", "dma", attn_scr[:, t0:t0 + 512].rearrange("(hc p) t -> p hc t", p=128),
                      atb[:].rearrange("p (hc t) -> p hc t", hc=16),
                      waits=[("fv", P.n("fv")), ("fs", P.n("fs"))], incs=[("at", 16)])
                p5_prev = ("at", P.n("at"))
            n_at = P.n("at")

        with ExitStack() as sc6:
            e6 = sc6.enter_context
            xdqT = e6(nc.sbuf_tensor("xdqT", [128, 16 * 2048], BF16))
            P.add("q", "dma", xdqT[:].rearrange("p (hc t) -> p hc t", hc=16),
                  xdq_scr[:].rearrange("(hc p) t -> p hc t", p=128),
                  waits=[("xd0", P.n("xd0")), ("xd1", P.n("xd1"))], incs=[("lx", 16)])
            n_xdqt = P.n("lx")
            islab = e6(nc.sbuf_tensor("islab", [128, 4 * 8192], BF16))
            hsil = e6(nc.sbuf_tensor("hsil", [128, 512], BF16))
            hf = e6(nc.sbuf_tensor("hf", [128, 512], F32))
            hq8 = e6(nc.sbuf_tensor("hq8", [128, 512], FP8))
            hdq = e6(nc.sbuf_tensor("hdq", [128, 512], BF16))
            hsc = e6(nc.sbuf_tensor("hsc", [128, 4 * 16], F32))
            hstage = e6(nc.sbuf_tensor("hstage", [128, 1024], BF16))
            isl_done = {}
            hstage_guard = {}
            h_tr_done = None
            h_done = {}
            for ic in range(16):
                ipar = ic % 2
                rank, li = ic // 2, (ic % 2) * 512
                lw = [("cw", 1)]
                if ipar in isl_done:
                    lw.append(isl_done[ipar])
                P.add("q", "dma", islab[:, ipar * 16384: ipar * 16384 + 8192].rearrange(
                    "p (hc f) -> p hc f", hc=16),
                    ccW_out[rank, 0:2048, li:li + 512].rearrange("(hc p) f -> p hc f", p=128),
                    waits=lw, incs=[("li%d" % ipar, 16)])
                P.add("q", "dma", islab[:, ipar * 16384 + 8192: ipar * 16384 + 16384].rearrange(
                    "p (hc f) -> p hc f", hc=16),
                    ccW_out[rank, 2048:4096, li:li + 512].rearrange("(hc p) f -> p hc f", p=128),
                    incs=[("li%d" % ipar, 16)])
                nld = P.n("li%d" % ipar)
                for ts_ in range(16):
                    kk = ic * 16 + ts_
                    bank = pC if kk % 2 == 0 else pD
                    bank2 = pE if kk % 2 == 0 else pF
                    waits = [("li%d" % ipar, nld), ("lx", n_xdqt)]
                    if kk - 2 in h_done:
                        waits.append(h_done[kk - 2])
                    for hc in range(16):
                        P.add("t", "mm", bank[:, 0:512],
                              xdqT[:].rearrange("p (hc t) -> p hc t", hc=16)[
                                  :, hc, ts_ * 128:(ts_ + 1) * 128],
                              islab[:, ipar * 16384 + hc * 512: ipar * 16384 + (hc + 1) * 512],
                              hc == 0, hc == 15,
                              waits=waits if hc == 0 else ())
                    for hc in range(16):
                        P.add("t", "mm", bank2[:, 0:512],
                              xdqT[:].rearrange("p (hc t) -> p hc t", hc=16)[
                                  :, hc, ts_ * 128:(ts_ + 1) * 128],
                              islab[:, ipar * 16384 + 8192 + hc * 512:
                                    ipar * 16384 + 8192 + (hc + 1) * 512],
                              hc == 0, hc == 15,
                              incs=[("ft", 1)] if hc == 15 else ())
                    nftg = P.n("ft")
                    silw = [("ft", nftg)]
                    if kk - 1 in h_done:
                        silw.append(h_done[kk - 1])
                    P.add("s", "act", hsil[:], bank[:, 0:512], ACTF.Silu, 0.0, 1.0,
                          waits=silw, incs=[("fs", 1)])
                    hfw = [("fs", P.n("fs"))]
                    if h_tr_done is not None:
                        hfw.append(h_tr_done)
                    P.add("v", "tt", hf[:], hsil[:], bank2[:, 0:512], ALU.mult,
                          waits=hfw)
                    P.add("v", "red", hsc[:, 0:16], hf[:].rearrange("p (b k) -> p b k", k=32),
                          ALU.max, True, incs=[("fv", 1)])
                    P.add("v", "ts", hsc[:, 16:32], hsc[:, 0:16], 1.0 / 448.0, 1e-30,
                          ALU.mult, ALU.max, waits=[("fv", P.n("fv"))], incs=[("fv", 1)])
                    P.add("v", "recip", hsc[:, 32:48], hsc[:, 16:32],
                          waits=[("fv", P.n("fv"))], incs=[("fv", 1)])
                    P.add("v", "ts", hsc[:, 48:64], hsc[:, 16:32], 2.0, None, ALU.mult, None,
                          waits=[("fv", P.n("fv"))], incs=[("fv", 1)])
                    P.add("v", "stt", hq8[:].rearrange("p (b k) -> p b k", k=32),
                          hf[:].rearrange("p (b k) -> p b k", k=32), 0.5,
                          hsc[:, 32:48].unsqueeze(-1).broadcast_to([128, 16, 32]),
                          ALU.mult, ALU.mult, waits=[("fv", P.n("fv"))])
                    P.add("v", "tt", hdq[:].rearrange("p (b k) -> p b k", k=32),
                          hq8[:].rearrange("p (b k) -> p b k", k=32),
                          hsc[:, 48:64].unsqueeze(-1).broadcast_to([128, 16, 32]),
                          ALU.mult, incs=[("fv", 1)])
                    h_done[kk] = ("fv", P.n("fv"))
                    gslot = cv.ptr_grp % 2
                    stg = hstage[:, gslot * 512:(gslot + 1) * 512]
                    trs = [hdq[:, i2 * 128:(i2 + 1) * 128] for i2 in range(4)]
                    hsem = "hd%d" % gslot
                    ee = [(hsem, hstage_guard[gslot])] if gslot in hstage_guard else []
                    cv.tr_group(trs, stg, "v", extra_waits=[h_done[kk]], evac_extra=ee)
                    h_tr_done = ("ft", P.n("ft"))
                    dest = h_scr[ic * 512:(ic + 1) * 512, ts_ * 128:(ts_ + 1) * 128].rearrange(
                        "(b p) t -> p b t", p=128)
                    P.add("q", "dma", dest, stg.rearrange("p (b f) -> p b f", b=4),
                          waits=[cv.ptr_evac[cv.ptr_grp - 1]], incs=[(hsem, 16)])
                    hstage_guard[gslot] = P.n(hsem)
                isl_done[ipar] = ("ft", P.n("ft"))
            n_hd = (P.n("hd0"), P.n("hd1"))

        with ExitStack() as sc7:
            e7 = sc7.enter_context
            hslab = e7(nc.sbuf_tensor("hslab", [128, 64 * 512], BF16))
            dwsl = e7(nc.sbuf_tensor("dwsl", [128, 2 * 8192], BF16))
            uTs = e7(nc.sbuf_tensor("uTs", [128, 16 * 512], F32))
            atl = e7(nc.sbuf_tensor("atl", [128, 16 * 512], BF16))
            ofin = e7(nc.sbuf_tensor("ofin", [128, 2 * 512], F32))
            dw_done = {}
            fin_done = {}
            st_par = {}
            for tc in range(4):
                t0 = tc * 512
                lw = [("hd0", n_hd[0]), ("hd1", n_hd[1]), ("at", n_at)]
                if tc > 0:
                    lw.append(("ft", P.n("ft")))
                    lw.append(("fv", P.n("fv")))
                    lw.append(("fs", P.n("fs")))
                P.add("q", "dma", hslab[:].rearrange("p (ic t) -> p ic t", ic=64),
                      h_scr[:, t0:t0 + 512].rearrange("(ic p) t -> p ic t", p=128),
                      waits=lw, incs=[("l6", 16)])
                P.add("q", "dma", uTs[:].rearrange("p (hc t) -> p hc t", hc=16),
                      uT_own[:, t0:t0 + 512].rearrange("(hc p) t -> p hc t", p=128),
                      incs=[("l6", 16)])
                P.add("q", "dma", atl[:].rearrange("p (hc t) -> p hc t", hc=16),
                      attn_scr[:, t0:t0 + 512].rearrange("(hc p) t -> p hc t", p=128),
                      incs=[("l6", 16)])
                nld = P.n("l6")
                for ec in range(16):
                    wpar = ec % 2
                    rank = ec // 2
                    qq_base = 0
                    lw2 = []
                    if wpar in dw_done:
                        lw2.append(dw_done[wpar])
                    le = (ec % 2) * 128 + (ec // 2) * 0
                    erank, esub = (ec * 128) // 256, (ec * 128) % 256
                    src = ccW_out[erank, 4096:6144, :].rearrange(
                        "(icl p) (m e) -> p m icl e", p=128, m=4)[:, :, :, esub:esub + 128]
                    P.add("q", "dma", dwsl[:, wpar * 8192:(wpar + 1) * 8192].rearrange(
                        "p (m icl e) -> p m icl e", m=4, icl=16), src,
                        waits=lw2, incs=[("l6w%d" % wpar, 16)])
                    nld2 = P.n("l6w%d" % wpar)
                    bank = pA if ec % 2 == 0 else pB
                    waits = [("l6", nld), ("l6w%d" % wpar, nld2)]
                    if ec % 2 in fin_done:
                        waits.append(fin_done[ec % 2])
                    for icc in range(64):
                        P.add("t", "mm", bank[:, 0:512],
                              dwsl[:, wpar * 8192 + icc * 128: wpar * 8192 + (icc + 1) * 128],
                              hslab[:, icc * 512:(icc + 1) * 512],
                              icc == 0, icc == 63,
                              waits=waits if icc == 0 else (),
                              incs=[("ft", 1)] if icc == 63 else ())
                    cn = "fv"
                    of = ofin[:, (ec % 2) * 512:(ec % 2 + 1) * 512]
                    ow = [("ft", P.n("ft"))]
                    if ec % 2 in st_par:
                        ow.append(("so%d" % (ec % 2), st_par[ec % 2]))
                    P.add("v", "tt", of, bank[:, 0:512], atl[:, ec * 512:(ec + 1) * 512],
                          ALU.add, waits=ow)
                    P.add("v", "tt", of, of, uTs[:, ec * 512:(ec + 1) * 512], ALU.add,
                          incs=[(cn, 1)])
                    fin_done[ec % 2] = (cn, P.n(cn))
                    P.add("q", "dma", outT[ec * 128:(ec + 1) * 128, t0:t0 + 512], of,
                          waits=[fin_done[ec % 2]], incs=[("so%d" % (ec % 2), 16)])
                    st_par[ec % 2] = P.n("so%d" % (ec % 2))
                    dw_done[wpar] = ("ft", P.n("ft"))

        P.add("g", "memset", biases[0:1, 0:8], 0.0,
              waits=[("so0", P.n("so0")), ("so1", P.n("so1")), ("st", P.n("st"))])
        for name, (scr, dout) in dbg.items():
            P.add("g", "dma", dout[:], scr[:], incs=[("db", 16)])
        if dbg:
            P.add("g", "memset", biases[0:1, 8:16], 0.0, waits=[("db", P.n("db"))])

        _run_block(nc, P)
    return nc


def _emit_kv_rest(P, cv, kin, ych_ap, k_wait):
    b, c = cv.b, cv.c
    pXre, pXim = cv.pb["pXre"], cv.pb["pXim"]
    pA, pB = cv.pb["pA"], cv.pb["pB"]
    bar = cv.stage1(2)
    x_prev = {}
    z_done = {}
    u_prev = None
    usb_turn = {}
    ustage_prev = {}
    for pair in range(16):
        par = pair % 2
        kw = [k_wait]
        if par in z_done:
            kw.append(z_done[par])
        ks = "kk%d" % par
        P.add("q", "dma", b.Kbuf[0:72, par * 1024: par * 1024 + 1024],
              kin[pair], waits=kw, incs=[(ks, 16)])
        nkq = P.n(ks)
        ew = [("fv", bar[0]), ("fs", bar[1])] if pair < 2 else ()
        cv.pair_turn1(pair, extra_waits=ew)
        nft = cv.pair_stage2(pair, x_prev)
        sw = [("ft", nft)]
        if par in z_done:
            sw.append(z_done[par])
        P.add("s", "copy", b.tmp[0:72, par * 1024: par * 1024 + 512],
              pXre[par][0:72, 0:512], waits=sw)
        P.add("s", "copy", b.tmp[0:72, par * 1024 + 512: par * 1024 + 1024],
              pXim[par][0:72, 0:512], incs=[("fs", 1)])
        nfs = P.n("fs")
        x_prev[par] = ("fs", nfs)
        K = lambda im: b.Kbuf[0:72, par * 1024 + im * 512: par * 1024 + im * 512 + 512]
        Z = lambda im: b.Z[0:72, par * 1024 + im * 512: par * 1024 + im * 512 + 512]
        tre = b.tmp[0:72, par * 1024: par * 1024 + 512]
        tim = b.tmp[0:72, par * 1024 + 512: par * 1024 + 1024]
        zA, zB = b.tmp2[0:72, 0:512], b.tmp2[0:72, 512:1024]
        vw = [("fs", nfs), (ks, nkq)]
        if par in ustage_prev:
            vw.append(ustage_prev[par])
        P.add("v", "tt", zA, tre, K(0), ALU.mult, waits=vw)
        P.add("v", "tt", zB, tim, K(1), ALU.mult)
        P.add("v", "tt", Z(0), zA, zB, ALU.subtract)
        P.add("v", "tt", zA, tre, K(1), ALU.mult)
        P.add("v", "tt", zB, tim, K(0), ALU.mult)
        P.add("v", "tt", Z(1), zA, zB, ALU.add, incs=[("fv", 1)])
        z_done[par] = ("fv", P.n("fv"))
        uw = [z_done[par]]
        if u_prev is not None:
            uw.append(u_prev)
        P.add("t", "mm", pA[:, 0:512], c.BDAre[0:72, 0:128], Z(0), True, False, waits=uw)
        P.add("t", "mm", pA[:, 0:512], c.BDAim_neg[0:72, 0:128], Z(1), False, True)
        P.add("t", "mm", pB[:, 0:512], c.BDAim[0:72, 0:128], Z(0), True, False)
        P.add("t", "mm", pB[:, 0:512], c.BDAre[0:72, 0:128], Z(1), False, True,
              incs=[("ft", 1)])
        ustage_prev[par] = ("ft", P.n("ft"))
        sw = [("ft", P.n("ft"))]
        if par in usb_turn:
            sw.append(usb_turn[par])
        P.add("s", "copy", b.Usb[:, par * 1024: par * 1024 + 512], pA[:, 0:512], waits=sw)
        P.add("s", "act", b.Usb[:, par * 1024 + 512: par * 1024 + 1024], pB[:, 0:512],
              ACTF.Copy, 0.0, -1.0, incs=[("fs", 1)])
        u_prev = ("fs", P.n("fs"))
        trs = []
        for ac in range(2):
            for im in range(2):
                for lg in range(2):
                    off = par * 1024 + im * 512 + lg * 256 + ac * 128
                    trs.append(b.Usb[:, off: off + 128])
        g0 = pair * 2
        dest = b.UT[:].rearrange("p (aci gg f) -> p aci gg f", aci=4, gg=32)[:, :, g0:g0 + 2, :]
        cv.tr_group(trs, dest, "v" if par == 0 else "s", extra_waits=[u_prev])
        usb_turn[par] = ("ft", P.n("ft"))
    bar2 = (("fv", P.n("fv")), ("fs", P.n("fs")))
    py_prev = {}
    tb_done = {}
    for nu in range(16):
        par = nu % 2
        bank = pXre[par]
        w = [bar2[0], bar2[1]] if nu < 2 else []
        if par in py_prev:
            w.append(py_prev[par])
        if nu < 2 and par in x_prev:
            w.append(x_prev[par])
        first = True
        for ac in range(2):
            hoff = (nu * 2 + ac) * 128
            base_re = (ac * 2 + 0) * 4096
            base_im = (ac * 2 + 1) * 4096
            UTre = b.UT[:, base_re:base_re + 4096].rearrange(
                "p (g c2 n2) -> p g c2 n2", c2=8, n2=16)[:, :, :, nu]
            UTim = b.UT[:, base_im:base_im + 4096].rearrange(
                "p (g c2 n2) -> p g c2 n2", c2=8, n2=16)[:, :, :, nu]
            P.add("t", "mm", bank[:, 0:256], c.Hre[:, hoff:hoff + 128], UTre,
                  first, False, waits=w if first else ())
            first = False
            P.add("t", "mm", bank[:, 0:256], c.Him[:, hoff:hoff + 128], UTim,
                  False, ac == 1, incs=[("ft", 1)] if ac == 1 else ())
        cons = "v" if par == 0 else "s"
        cn = "fv" if par == 0 else "fs"
        yw = [("ft", P.n("ft"))]
        if par in tb_done:
            yw.append(tb_done[par])
        P.add(cons, "copy", b.ysb[:, par * 256: par * 256 + 256], bank[:, 0:256],
              waits=yw, incs=[(cn, 1)])
        py_prev[par] = (cn, P.n(cn))
        trs = [b.ysb[:, par * 256 + cq * 128: par * 256 + (cq + 1) * 128]
               for cq in range(2)]
        dest = ych_ap.rearrange("p (cq t) -> p cq t", cq=2)[:, :, nu:2048:16]
        cv.tr_group(trs, dest, "v" if par == 0 else "s", extra_waits=[py_prev[par]])
        tb_done[par] = ("ft", P.n("ft"))



_CACHE = {}


def _get_nc():
    if "nc" not in _CACHE:
        _CACHE["nc"] = _build(_DEBUG)
    return _CACHE["nc"]


def _chunkvec(v):
    return np.ascontiguousarray(np.asarray(v, np.float32).reshape(16, 128).T)


def _chunkw3(v):
    return np.ascontiguousarray(
        np.asarray(v, np.float32).reshape(16, 128, 3).transpose(1, 0, 2).reshape(128, 48))


def kernel(**inputs):
    u = np.asarray(inputs["L0_attn_u"], dtype=np.float32)
    fc = _fft_constants()
    shared = {
        "WqT": np.ascontiguousarray(np.asarray(inputs["L0_attn_Wq"], np.float32).T),
        "WkT": np.ascontiguousarray(np.asarray(inputs["L0_attn_Wk"], np.float32).T),
        "WvT": np.ascontiguousarray(np.asarray(inputs["L0_attn_Wv"], np.float32).T),
        "WoT": np.ascontiguousarray(np.asarray(inputs["L0_attn_Wo"], np.float32).T),
        "bq": _chunkvec(inputs["L0_attn_bq"]),
        "bk": _chunkvec(inputs["L0_attn_bk"]),
        "bv": _chunkvec(inputs["L0_attn_bv"]),
        "bo": _chunkvec(inputs["L0_attn_bo"]),
        "cbq": _chunkvec(inputs["L0_attn_convq_b"]),
        "cbk": _chunkvec(inputs["L0_attn_convk_b"]),
        "dskip": _chunkvec(inputs["L0_attn_D_skip"]),
        "cwq": _chunkw3(inputs["L0_attn_convq_w"]),
        "cwk": _chunkw3(inputs["L0_attn_convk_w"]),
    }
    for n, a in fc.items():
        shared["c_" + n] = a
    ker = np.asarray(inputs["L0_attn_ssm_kernel"], np.float32)
    gw = np.asarray(inputs["L0_ffn_gate_w"], np.float32)
    uw = np.asarray(inputs["L0_ffn_up_w"], np.float32)
    dw = np.asarray(inputs["L0_ffn_down_w"], np.float32)
    zeros = np.zeros((TOK, D), np.float32)
    in_maps = []
    for core in range(8):
        s, h = core // 2, core % 2
        m = dict(shared)
        uo = np.ascontiguousarray(u[s, h * TOK:(h + 1) * TOK, :])
        m["u_own"] = uo
        m["u_ctx"] = np.ascontiguousarray(u[s, 0:TOK, :]) if h == 1 else uo
        m["uT_own"] = np.ascontiguousarray(uo.T)
        m["ctxm"] = np.full((128, 1), float(h), np.float32)
        m["ker_sl"] = np.ascontiguousarray(ker[core * 256:(core + 1) * 256, :])
        m["gw_s"] = np.ascontiguousarray(gw[core * 1024:(core + 1) * 1024, :])
        m["uw_s"] = np.ascontiguousarray(uw[core * 1024:(core + 1) * 1024, :])
        m["dw_s"] = np.ascontiguousarray(dw[core * 256:(core + 1) * 256, :])
        in_maps.append(m)
    nc = _get_nc()
    res = run_bass_kernel_spmd(nc, in_maps, core_ids=list(range(8)))
    out = np.empty((4, 4096, D), np.float32)
    for core in range(8):
        s, h = core // 2, core % 2
        out[s, h * TOK:(h + 1) * TOK, :] = res.results[core]["outT"].T
    if _DEBUG:
        kernel._dbg = [{k: v for k, v in r.items() if k.startswith("dbg_")}
                       for r in res.results]
    return out


# version 03
# speedup vs baseline: 1.0542x; 1.0542x over previous
import numpy as np
import ml_dtypes
import concourse.bass as bass
import concourse.mybir as mybir
from concourse.bass_utils import run_bass_kernel_spmd
from contextlib import ExitStack

F32 = mybir.dt.float32
BF16 = mybir.dt.bfloat16
FP8 = mybir.dt.float8e4
ALU = mybir.AluOpType
AX = mybir.AxisListType.X
ACTF = mybir.ActivationFunctionType
BF = ml_dtypes.bfloat16

_DEBUG = ()

D = 2048
TOK = 2048
WIN = 4096
I_FF = 8192



def _fft_constants():
    Gre = np.zeros((128, 16 * 2 * 2 * 128))
    Gim = np.zeros((128, 16 * 2 * 2 * 128))
    r_r = np.arange(256)
    a_r = np.arange(256)
    for s in range(16):
        ph = np.exp(-2j * np.pi * (np.outer(r_r, a_r) / 256.0 + s * a_r[None, :] / 4096.0))
        for rc in range(2):
            for ac in range(2):
                off = ((s * 2 + rc) * 2 + ac) * 128
                Gre[:, off:off + 128] = ph[rc * 128:(rc + 1) * 128, ac * 128:(ac + 1) * 128].real
                Gim[:, off:off + 128] = ph[rc * 128:(rc + 1) * 128, ac * 128:(ac + 1) * 128].imag
    BD2 = np.zeros((128, 72), np.complex128)
    for c in range(8):
        for s in range(16):
            for b in range(9):
                BD2[c * 16 + s, c * 9 + b] = np.exp(-2j * np.pi * s * b / 16.0)
    BDA = np.zeros((72, 128), np.complex128)
    for c in range(8):
        for b in range(9):
            for nu in range(16):
                BDA[c * 9 + b, c * 16 + nu] = np.exp(2j * np.pi * b * nu / 16.0)
    Hre = np.zeros((128, 16 * 2 * 128))
    Him = np.zeros((128, 16 * 2 * 128))
    for nu in range(16):
        for ac in range(2):
            a = (ac * 128 + np.arange(128))[:, None]
            eta = (128 + np.arange(128))[None, :]
            ph = np.exp(2j * np.pi * (a * nu / 4096.0 + a * eta / 256.0))
            off = (nu * 2 + ac) * 128
            Hre[:, off:off + 128] = ph.real
            Him[:, off:off + 128] = ph.imag
    cm = np.zeros((72, 256))
    for c in range(8):
        for b in range(9):
            if b < 8:
                w = np.full(256, 2.0)
                if b == 0:
                    w[0] = 1.0
            else:
                w = np.zeros(256)
                w[0] = 1.0
            cm[c * 9 + b] = w / 4096.0
    return {
        "Gre": Gre.astype(BF), "Gim": Gim.astype(BF),
        "BD2re": np.ascontiguousarray(BD2.real).astype(BF),
        "BD2im": np.ascontiguousarray(BD2.imag).astype(BF),
        "BD2im_neg": np.ascontiguousarray(-BD2.imag).astype(BF),
        "BDAre": np.ascontiguousarray(BDA.real).astype(BF),
        "BDAim": np.ascontiguousarray(BDA.imag).astype(BF),
        "BDAim_neg": np.ascontiguousarray(-BDA.imag).astype(BF),
        "Hre": Hre.astype(BF), "Him": Him.astype(BF),
        "cmask": cm.astype(np.float32),
        "identity": np.eye(128).astype(BF),
    }



class _Plan:
    def __init__(self, nc):
        self.nc = nc
        self.steps = {e: [] for e in ("t", "v", "s", "g", "q")}
        self.sems = {}
        self.counts = {}

    def sem(self, name):
        if name not in self.sems:
            self.sems[name] = self.nc.alloc_semaphore("sem_" + name)
            self.counts[name] = 0
        return self.sems[name]

    def n(self, name):
        self.sem(name)
        return self.counts[name]

    def add(self, eng, kind, *args, waits=(), incs=()):
        w = [(self.sem(nm), v) for nm, v in waits]
        inc = []
        for nm, amt in incs:
            inc.append((self.sem(nm), amt))
            self.counts[nm] += amt
        self.steps[eng].append((kind, args, w, inc))

    def replay(self, key, eng):
        for kind, args, waits, incs in self.steps[key]:
            for sem, val in waits:
                eng.wait_ge(sem, val)
            ins = self._emit(eng, kind, args)
            for sem, amt in incs:
                ins.then_inc(sem, amt)

    @staticmethod
    def _emit(eng, kind, a):
        if kind == "mm":
            out, lhsT, rhs, start, stop = a
            return eng.matmul(out, lhsT, rhs, start=start, stop=stop)
        if kind == "tr":
            out, in_, ident = a
            return eng.transpose(out, in_, ident)
        if kind == "copy":
            out, in_ = a
            if hasattr(eng, "tensor_copy"):
                return eng.tensor_copy(out, in_)
            return eng.copy(out, in_)
        if kind == "tt":
            out, in0, in1, op = a
            return eng.tensor_tensor(out, in0, in1, op=op)
        if kind == "ts":
            out, in0, s1, s2, op0, op1 = a
            if op1 is None:
                return eng.tensor_scalar(out, in0, s1, None, op0=op0)
            return eng.tensor_scalar(out, in0, s1, s2, op0=op0, op1=op1)
        if kind == "stt":
            out, in0, scal, in1, op0, op1 = a
            return eng.scalar_tensor_tensor(out, in0, scal, in1, op0=op0, op1=op1)
        if kind == "act":
            out, in_, func, bias, scale = a
            return eng.activation(out, in_, func, bias=bias, scale=scale)
        if kind == "red":
            out, in_, op, absval = a
            return eng.tensor_reduce(out, in_, axis=AX, op=op, apply_absolute_value=absval)
        if kind == "bnst":
            out, in_ = a
            return eng.bn_stats(out, in_)
        if kind == "bnag":
            out, in_ = a
            return eng.bn_aggr(out, in_)
        if kind == "dma":
            out, in_ = a
            return eng.dma_start(out, in_)
        if kind == "memset":
            ap, val = a
            return eng.memset(ap, val)
        if kind == "recip":
            out, in_ = a
            return eng.reciprocal(out, in_)
        if kind == "coll":
            kind_s, op, groups, ins_, outs_ = a
            return eng.collective_compute(kind_s, op, replica_groups=groups,
                                          ins=ins_, outs=outs_)
        raise KeyError(kind)


def _run_block(nc, plan):
    with nc.Block() as blk:
        if plan.steps["q"]:
            blk.sync(lambda e: plan.replay("q", e))
        if plan.steps["g"]:
            blk.gpsimd(lambda e: plan.replay("g", e))
        if plan.steps["v"]:
            blk.vector(lambda e: plan.replay("v", e))
        if plan.steps["s"]:
            blk.scalar(lambda e: plan.replay("s", e))
        if plan.steps["t"]:
            blk.tensor(lambda e: plan.replay("t", e))



class _ConvBufs:
    _SEQ = [0]

    def __init__(self, nc, stack):
        ent = stack.enter_context
        _ConvBufs._SEQ[0] += 1
        pref = "cv%d_" % _ConvBufs._SEQ[0]
        T = lambda n, sh, dt: ent(nc.sbuf_tensor(pref + n, sh, dt))
        self.S = T("S", [128, 16 * 2 * 256], BF16)
        self.E = T("E", [128, 2 * 2 * 4096], BF16)
        self.UT = T("UT", [128, 2 * 2 * 4096], BF16)
        self.Fbuf = T("F", [128, 2 * 1024], BF16)
        self.Z = T("Z", [128, 2 * 1024], BF16)
        self.Ksb = T("Ksb", [128, 2 * 1024], BF16)
        self.Kbuf = self.Ksb
        self.Usb = T("Usb", [128, 2 * 1024], BF16)
        self.tmp = T("tmp", [128, 2 * 1024], F32)
        self.tmp2 = T("tmp2", [128, 1024], F32)
        self.ysb = T("ysb", [128, 2 * 256], BF16)


class _Conv:

    def __init__(self, P, b, cb, pb):
        self.P, self.b, self.c, self.pb = P, b, cb, pb
        self.ptr_grp = 0
        self.ptr_evac = {}

    def tr_group(self, trs, dest, cons, extra_waits=(), evac_extra=()):
        P, b = self.P, self.b
        g = self.ptr_grp
        bank = self.pb["ptr"][g % 2]
        waits = list(extra_waits)
        if g - 2 in self.ptr_evac:
            waits.append(self.ptr_evac[g - 2])
        for i, in_ap in enumerate(trs):
            P.add("t", "tr", bank[:, i * 128:(i + 1) * 128], in_ap, self.c.ident[:],
                  waits=waits if i == 0 else (),
                  incs=[("ft", 1)] if i == len(trs) - 1 else ())
        cn = "fv" if cons == "v" else "fs"
        P.add(cons, "copy", dest, bank[:, 0:128 * len(trs)],
              waits=[("ft", P.n("ft"))] + list(evac_extra), incs=[(cn, 1)])
        self.ptr_evac[g] = (cn, P.n(cn))
        self.ptr_grp += 1

    def strided_tr(self, tiles, rchunks, extra_first=()):
        b = self.b
        first = True
        for s in range(16):
            for rc in range(rchunks):
                trs = [tiles[cq][:, s + 2048 * rc: s + 2048 * rc + 16 * 127 + 1: 16]
                      for cq in range(2)]
                off = (s * 2 + rc) * 256
                self.tr_group(trs, b.S[:, off:off + 256],
                              "v" if (s + rc) % 2 == 0 else "s",
                              extra_waits=extra_first if first else ())
                first = False

    def stage1(self, rchunks):
        P, b, c = self.P, self.b, self.c
        s1_evac = {}
        k = 0
        for s in range(16):
            for ac in range(2):
                for im in range(2):
                    bank = self.pb["pA"] if k % 2 == 0 else self.pb["pB"]
                    cons = "v" if k % 2 == 0 else "s"
                    waits = []
                    if k - 2 in s1_evac:
                        waits.append(s1_evac[k - 2])
                    if k <= 1:
                        waits += [("fv", P.n("fv")), ("fs", P.n("fs"))]
                    for rc in range(rchunks):
                        off = ((s * 2 + rc) * 2 + ac) * 128
                        G = (c.Gim if im else c.Gre)[:, off:off + 128]
                        soff = (s * 2 + rc) * 256
                        P.add("t", "mm", bank[:, 0:256], G, b.S[:, soff:soff + 256],
                              rc == 0, rc == rchunks - 1,
                              waits=waits if rc == 0 else (),
                              incs=[("ft", 1)] if rc == rchunks - 1 else ())
                    eoff = (ac * 2 + im) * 4096
                    dest = b.E[:, eoff:eoff + 4096].rearrange(
                        "p (g c2 s2) -> p g c2 s2", c2=8, s2=16)[:, :, :, s]
                    cn = "fv" if cons == "v" else "fs"
                    P.add(cons, "copy", dest, bank[:, 0:256],
                          waits=[("ft", P.n("ft"))], incs=[(cn, 1)])
                    s1_evac[k] = (cn, P.n(cn))
                    k += 1
        return P.n("fv"), P.n("fs")

    def pair_turn1(self, pair, extra_waits=()):
        b = self.b
        trs = []
        for im in range(2):
            for lg in range(2):
                g = pair * 2 + lg
                for ac in range(2):
                    eoff = (ac * 2 + im) * 4096
                    trs.append(b.E[:, eoff + g * 128: eoff + (g + 1) * 128])
        par = pair % 2
        dest = b.Fbuf[:, par * 1024: par * 1024 + 1024]
        self.tr_group(trs, dest, "v" if par == 0 else "s", extra_waits=extra_waits)

    def pair_stage2(self, pair, x_prev):
        P, b, c = self.P, self.b, self.c
        par = pair % 2
        fw = [self.ptr_evac[self.ptr_grp - 1]]
        if par in x_prev:
            fw.append(x_prev[par])
        F = lambda im: b.Fbuf[:, par * 1024 + im * 512: par * 1024 + im * 512 + 512]
        pXre, pXim = self.pb["pXre"], self.pb["pXim"]
        P.add("t", "mm", pXre[par][0:72, 0:512], c.BD2re[0:128, 0:72], F(0), True, False, waits=fw)
        P.add("t", "mm", pXre[par][0:72, 0:512], c.BD2im_neg[0:128, 0:72], F(1), False, True)
        P.add("t", "mm", pXim[par][0:72, 0:512], c.BD2im[0:128, 0:72], F(0), True, False)
        P.add("t", "mm", pXim[par][0:72, 0:512], c.BD2re[0:128, 0:72], F(1), False, True,
              incs=[("ft", 1)])
        return P.n("ft")

    def emit_ker(self, ker_tiles, kout, extra_first=()):
        P, b, c = self.P, self.b, self.c
        self.strided_tr(ker_tiles, 1, extra_first=extra_first)
        bar = self.stage1(1)
        x_prev = {}
        kq_guard = {}
        for pair in range(16):
            ew = [("fv", bar[0]), ("fs", bar[1])] if pair < 2 else ()
            self.pair_turn1(pair, extra_waits=ew)
            nft = self.pair_stage2(pair, x_prev)
            par = pair % 2
            cm = c.cmask[0:72, :].unsqueeze(1).broadcast_to([72, 2, 256])
            waits = [("ft", nft)]
            if par in kq_guard:
                waits.append(kq_guard[par])
            pXre, pXim = self.pb["pXre"], self.pb["pXim"]
            P.add("v", "tt",
                  b.Ksb[0:72, par * 1024: par * 1024 + 512].rearrange("p (l f) -> p l f", l=2),
                  pXre[par][0:72, 0:512].rearrange("p (l f) -> p l f", l=2), cm, ALU.mult,
                  waits=waits)
            P.add("v", "tt",
                  b.Ksb[0:72, par * 1024 + 512: par * 1024 + 1024].rearrange("p (l f) -> p l f", l=2),
                  pXim[par][0:72, 0:512].rearrange("p (l f) -> p l f", l=2), cm, ALU.mult,
                  incs=[("fv", 1)])
            x_prev[par] = ("fv", P.n("fv"))
            ks = "kq%d" % par
            P.add("q", "dma", kout[pair], b.Ksb[0:72, par * 1024: par * 1024 + 1024],
                  waits=[("fv", P.n("fv"))], incs=[(ks, 16)])
            kq_guard[par] = (ks, P.n(ks))

    def emit_kv(self, kv_tiles, kin, ych_ap, k_extra_wait=()):
        P, b, c = self.P, self.b, self.c
        pXre, pXim = self.pb["pXre"], self.pb["pXim"]
        pA, pB = self.pb["pA"], self.pb["pB"]
        self.strided_tr(kv_tiles, 2)
        bar = self.stage1(2)
        x_prev = {}
        z_done = {}
        u_prev = None
        usb_turn = {}
        ustage_prev = {}
        for pair in range(16):
            par = pair % 2
            kw = list(k_extra_wait)
            if par in z_done:
                kw.append(z_done[par])
            P.add("q", "dma", b.Kbuf[0:72, par * 1024: par * 1024 + 1024],
                  kin[pair], waits=kw, incs=[("kq", 16)])
            nkq = P.n("kq")
            ew = [("fv", bar[0]), ("fs", bar[1])] if pair < 2 else ()
            self.pair_turn1(pair, extra_waits=ew)
            nft = self.pair_stage2(pair, x_prev)
            sw = [("ft", nft)]
            if par in z_done:
                sw.append(z_done[par])
            P.add("s", "copy", b.tmp[0:72, par * 1024: par * 1024 + 512],
                  pXre[par][0:72, 0:512], waits=sw)
            P.add("s", "copy", b.tmp[0:72, par * 1024 + 512: par * 1024 + 1024],
                  pXim[par][0:72, 0:512], incs=[("fs", 1)])
            nfs = P.n("fs")
            x_prev[par] = ("fs", nfs)
            K = lambda im: b.Kbuf[0:72, par * 1024 + im * 512: par * 1024 + im * 512 + 512]
            Z = lambda im: b.Z[0:72, par * 1024 + im * 512: par * 1024 + im * 512 + 512]
            tre = b.tmp[0:72, par * 1024: par * 1024 + 512]
            tim = b.tmp[0:72, par * 1024 + 512: par * 1024 + 1024]
            zA, zB = b.tmp2[0:72, 0:512], b.tmp2[0:72, 512:1024]
            vw = [("fs", nfs), ("kq", nkq)]
            if par in ustage_prev:
                vw.append(ustage_prev[par])
            P.add("v", "tt", zA, tre, K(0), ALU.mult, waits=vw)
            P.add("v", "tt", zB, tim, K(1), ALU.mult)
            P.add("v", "tt", Z(0), zA, zB, ALU.subtract)
            P.add("v", "tt", zA, tre, K(1), ALU.mult)
            P.add("v", "tt", zB, tim, K(0), ALU.mult)
            P.add("v", "tt", Z(1), zA, zB, ALU.add, incs=[("fv", 1)])
            z_done[par] = ("fv", P.n("fv"))
            uw = [z_done[par]]
            if u_prev is not None:
                uw.append(u_prev)
            P.add("t", "mm", pA[:, 0:512], c.BDAre[0:72, 0:128], Z(0), True, False, waits=uw)
            P.add("t", "mm", pA[:, 0:512], c.BDAim_neg[0:72, 0:128], Z(1), False, True)
            P.add("t", "mm", pB[:, 0:512], c.BDAim[0:72, 0:128], Z(0), True, False)
            P.add("t", "mm", pB[:, 0:512], c.BDAre[0:72, 0:128], Z(1), False, True,
                  incs=[("ft", 1)])
            ustage_prev[par] = ("ft", P.n("ft"))
            sw = [("ft", P.n("ft"))]
            if par in usb_turn:
                sw.append(usb_turn[par])
            P.add("s", "copy", b.Usb[:, par * 1024: par * 1024 + 512], pA[:, 0:512],
                  waits=sw)
            P.add("s", "act", b.Usb[:, par * 1024 + 512: par * 1024 + 1024], pB[:, 0:512],
                  ACTF.Copy, 0.0, -1.0, incs=[("fs", 1)])
            u_prev = ("fs", P.n("fs"))
            trs = []
            for ac in range(2):
                for im in range(2):
                    for lg in range(2):
                        off = par * 1024 + im * 512 + lg * 256 + ac * 128
                        trs.append(b.Usb[:, off: off + 128])
            g0 = pair * 2
            dest = b.UT[:].rearrange("p (aci gg f) -> p aci gg f", aci=4, gg=32)[:, :, g0:g0 + 2, :]
            self.tr_group(trs, dest, "v" if par == 0 else "s", extra_waits=[u_prev])
            usb_turn[par] = ("ft", P.n("ft"))
        bar2 = (("fv", P.n("fv")), ("fs", P.n("fs")))
        py_prev = {}
        tb_done = {}
        for nu in range(16):
            par = nu % 2
            bank = pXre[par]
            w = [bar2[0], bar2[1]] if nu < 2 else []
            if par in py_prev:
                w.append(py_prev[par])
            if nu < 2 and par in x_prev:
                w.append(x_prev[par])
            first = True
            for ac in range(2):
                hoff = (nu * 2 + ac) * 128
                UTr = b.UT[:, 0 * 4096 + 0:].rearrange("p (aci gg f) -> p aci gg f", aci=4, gg=32)
                base_re = (ac * 2 + 0) * 4096
                base_im = (ac * 2 + 1) * 4096
                UTre = b.UT[:, base_re:base_re + 4096].rearrange(
                    "p (g c2 n2) -> p g c2 n2", c2=8, n2=16)[:, :, :, nu]
                UTim = b.UT[:, base_im:base_im + 4096].rearrange(
                    "p (g c2 n2) -> p g c2 n2", c2=8, n2=16)[:, :, :, nu]
                P.add("t", "mm", bank[:, 0:256], self.c.Hre[:, hoff:hoff + 128], UTre,
                      first, False, waits=w if first else ())
                first = False
                P.add("t", "mm", bank[:, 0:256], self.c.Him[:, hoff:hoff + 128], UTim,
                      False, ac == 1, incs=[("ft", 1)] if ac == 1 else ())
            cons = "v" if par == 0 else "s"
            cn = "fv" if par == 0 else "fs"
            yw = [("ft", P.n("ft"))]
            if par in tb_done:
                yw.append(tb_done[par])
            P.add(cons, "copy", self.b.ysb[:, par * 256: par * 256 + 256], bank[:, 0:256],
                  waits=yw, incs=[(cn, 1)])
            py_prev[par] = (cn, P.n(cn))
            trs = [self.b.ysb[:, par * 256 + cq * 128: par * 256 + (cq + 1) * 128]
                   for cq in range(2)]
            dest = ych_ap.rearrange("p (cq t) -> p cq t", cq=2)[:, :, nu:2048:16]
            self.tr_group(trs, dest, "v" if par == 0 else "s",
                          extra_waits=[py_prev[par]])
            tb_done[par] = ("ft", P.n("ft"))



def _build(debug=()):
    nc = bass.Bass("TRN2", target_bir_lowering=False)

    din = {}
    def DI(name, shape, dt=F32):
        din[name] = nc.dram_tensor(name, list(shape), dt, kind="ExternalInput")
        return din[name]

    u_own = DI("u_own", [TOK, D])
    u_ctx = DI("u_ctx", [TOK, D])
    uT_own = DI("uT_own", [D, TOK])
    ctxm = DI("ctxm", [128, 1])
    for w in ("WqT", "WkT", "WvT", "WoT"):
        DI(w, [D, D])
    for bname in ("bq", "bk", "bv", "bo", "cbq", "cbk", "dskip"):
        DI(bname, [128, 16])
    DI("cwq", [128, 48])
    DI("cwk", [128, 48])
    DI("ker_sl", [256, D])
    DI("gw_s", [1024, D])
    DI("uw_s", [1024, D])
    DI("dw_s", [256, I_FF])
    fc = _fft_constants()
    for name, arr in fc.items():
        DI("c_" + name, arr.shape, BF16 if arr.dtype == BF else F32)

    outT = nc.dram_tensor("outT", [D, TOK], F32, kind="ExternalOutput")

    wbf = {w: nc.dram_tensor("bf_" + w, [D, D], BF16, kind="Internal")
           for w in ("WqT", "WkT", "WvT", "WoT")}
    kv_scr = nc.dram_tensor("kv_scr", [D, WIN], BF16, kind="Internal")
    q_scr = nc.dram_tensor("q_scr", [D, TOK], BF16, kind="Internal")
    y_scr = nc.dram_tensor("y_scr", [D, TOK], BF16, kind="Internal")
    attn_scr = nc.dram_tensor("attn_scr", [D, TOK], BF16, kind="Internal")
    xdq_scr = nc.dram_tensor("xdq_scr", [D, TOK], BF16, kind="Internal")
    xn_scr = nc.dram_tensor("xn_scr", [TOK, D], BF16, kind="Internal")
    h_scr = nc.dram_tensor("h_scr", [I_FF, TOK], BF16, kind="Internal")
    ccK_in = nc.dram_tensor("ccK_in", [16, 72, 1024], BF16, kind="Internal")
    ccK_out = nc.dram_tensor("ccK_out", [8, 16, 72, 1024], BF16, kind="Internal")
    ccW_in = nc.dram_tensor("ccW_in", [6144, 1024], BF16, kind="Internal")
    ccW_out = nc.dram_tensor("ccW_out", [8, 6144, 1024], BF16, kind="Internal")
    dbg = {}
    for name in debug:
        scr = dict(kv_scr=kv_scr, q_scr=q_scr, y_scr=y_scr, attn_scr=attn_scr,
                   xdq_scr=xdq_scr, h_scr=h_scr, xn_scr=xn_scr, ccW_in=ccW_in,
                   ccK_in=ccK_in)[name]
        dbg[name] = (scr, nc.dram_tensor("dbg_" + name, list(scr.shape), BF16,
                                         kind="ExternalOutput"))

    with ExitStack() as persist:
        ent = persist.enter_context

        class _Consts:
            pass
        c = _Consts()
        for name in fc:
            tgt = "ident" if name == "identity" else name
            setattr(c, tgt, ent(nc.sbuf_tensor("k_" + tgt, list(fc[name].shape),
                                               BF16 if fc[name].dtype == BF else F32)))
        biases = ent(nc.sbuf_tensor("biases", [128, 16 * 7], F32))
        convw = ent(nc.sbuf_tensor("convw", [128, 16 * 6], F32))
        maskv = ent(nc.sbuf_tensor("maskv", [128, 1], F32))
        epsv = ent(nc.sbuf_tensor("epsv", [128, 1], F32))
        bm = ent(nc.sbuf_tensor("bm", [128, 16 * 3], F32))

        def bias_ap(idx, ecn):
            return biases[:, idx * 16 + ecn: idx * 16 + ecn + 1]

        def convw_ap(which, ecn, j):
            return convw[:, which * 48 + ecn * 3 + j: which * 48 + ecn * 3 + j + 1]

        pA = ent(nc.psum_tensor("pA", [128, 512], F32))
        pB = ent(nc.psum_tensor("pB", [128, 512], F32))
        ptr0 = ent(nc.psum_tensor("ptr0", [128, 1024], BF16))
        ptr1 = ent(nc.psum_tensor("ptr1", [128, 1024], BF16))
        pC = ent(nc.psum_tensor("pC", [128, 512], F32))
        pD = ent(nc.psum_tensor("pD", [128, 512], F32))
        pE = ent(nc.psum_tensor("pE", [128, 512], F32))
        pF = ent(nc.psum_tensor("pF", [128, 512], F32))
        pb = {"pA": pA, "pB": pB, "ptr": [ptr0, ptr1],
              "pXre": [pC, pD], "pXim": [pE, pF]}

        P = _Plan(nc)

        for name in fc:
            tgt = "ident" if name == "identity" else name
            rows = fc[name].shape[0]
            P.add("q", "dma", getattr(c, tgt)[0:rows, :], din["c_" + name][:],
                  incs=[("ld", 16)])
        for i, bname in enumerate(("bq", "bk", "bv", "bo", "cbq", "cbk", "dskip")):
            P.add("q", "dma", biases[:, i * 16:(i + 1) * 16], din[bname][:],
                  incs=[("ld", 16)])
        P.add("q", "dma", convw[:, 0:48], din["cwq"][:], incs=[("ld", 16)])
        P.add("q", "dma", convw[:, 48:96], din["cwk"][:], incs=[("ld", 16)])
        P.add("q", "dma", maskv[:], ctxm[:], incs=[("ld", 16)])
        n_const_ld = P.n("ld")
        P.add("v", "memset", epsv[:], 1e-5)
        cv = _Conv(P, None, c, pb)
        for w in ("WqT", "WkT", "WvT", "WoT"):
            P.add("g", "dma", wbf[w][:], din[w][:], incs=[("gd", 16)])
        n_wcast = P.n("gd")
        P.add("v", "ts", bm[:, 0:48], biases[:, 0:48], maskv[:, 0:1], None, ALU.mult, None,
              waits=[("ld", n_const_ld)], incs=[("fv", 1)])

        with ExitStack() as sc3:
            e3 = sc3.enter_context
            ubuf = e3(nc.sbuf_tensor("ubuf", [128, 2 * 2048], F32))
            xsq = e3(nc.sbuf_tensor("xsq", [128, 2048], F32))
            xnf = e3(nc.sbuf_tensor("xnf", [128, 2048], F32))
            cscr = e3(nc.sbuf_tensor("cscr", [128, 2 * 1024], F32))
            xnb = e3(nc.sbuf_tensor("xnb", [128, 2 * 2048], BF16))
            stats = e3(nc.sbuf_tensor("stats", [128, 32], F32))
            lnsc = e3(nc.sbuf_tensor("lnsc", [128, 8], F32))
            xnT = e3(nc.sbuf_tensor("xnT", [128, 16 * 1024], BF16))
            wslab = e3(nc.sbuf_tensor("wslab", [128, 2 * 2048], BF16))
            kraw = e3(nc.sbuf_tensor("kraw", [128, 2 * 1028], BF16))
            vraw = e3(nc.sbuf_tensor("vraw", [128, 2 * 1024], BF16))
            qraw = e3(nc.sbuf_tensor("qraw", [128, 2 * 1028], BF16))
            ktail = e3(nc.sbuf_tensor("ktail", [128, 16 * 4], BF16))
            qtail = e3(nc.sbuf_tensor("qtail", [128, 16 * 4], BF16))
            kvec = e3(nc.sbuf_tensor("kvec", [128, 2 * 1024], BF16))
            qcec = e3(nc.sbuf_tensor("qcec", [128, 2 * 1024], BF16))
            xq8 = e3(nc.sbuf_tensor("xq8", [128, 2048], FP8))
            xdq = e3(nc.sbuf_tensor("xdq", [128, 2048], BF16))
            xsc = e3(nc.sbuf_tensor("xsc", [128, 4 * 64], F32))

            uload_done = {}
            xstage_guard = {}
            xn_done = {}
            wsl_done = {}
            conv_done_par = {}
            kv_dma_par = {}
            qc_dma_par = {}
            pp_evac = {}
            pp_k = 0

            def ln_tile(gi, own):
                par = gi % 2
                src = u_own if own else u_ctx
                row0 = (gi * 128) % 2048
                lw = []
                if par in uload_done:
                    lw.append(uload_done[par])
                P.add("q", "dma", ubuf[:, par * 2048:(par + 1) * 2048],
                      src[row0:row0 + 128, :], waits=lw, incs=[("lu%d" % par, 16)])
                u_ap = ubuf[:, par * 2048:(par + 1) * 2048]
                nld_u = P.n("lu%d" % par)
                vw = [("lu%d" % par, nld_u)]
                if gi > 0:
                    vw.append(xn_done[gi - 1])
                P.add("s", "act", xsq[:], u_ap, ACTF.Square, 0.0, 1.0,
                      waits=[("lu%d" % par, nld_u)] + ([xn_done[gi - 1]] if gi > 0 else []),
                      incs=[("fs", 1)])
                nfs_sq = P.n("fs")
                P.add("v", "red", stats[:, 0:1], u_ap.unsqueeze(1), ALU.add, False,
                      waits=vw)
                P.add("v", "red", stats[:, 1:2], xsq[:].unsqueeze(1), ALU.add, False,
                      waits=[("fs", nfs_sq)], incs=[("fv", 1)])
                P.add("v", "ts", stats[:, 2:3], stats[:, 0:1], 1.0 / 2048.0, None,
                      ALU.mult, None, waits=[("fv", P.n("fv"))], incs=[("fv", 1)])
                P.add("v", "tt", stats[:, 3:4], stats[:, 2:3], stats[:, 2:3], ALU.mult,
                      waits=[("fv", P.n("fv"))], incs=[("fv", 1)])
                P.add("v", "ts", stats[:, 4:5], stats[:, 1:2], 1.0 / 2048.0,
                      stats[:, 3:4], ALU.mult, ALU.subtract,
                      waits=[("fv", P.n("fv"))], incs=[("fv", 1)])
                P.add("s", "act", lnsc[:, 0:1], stats[:, 4:5], ACTF.Ln, epsv[:, 0:1], 1.0,
                      waits=[("fv", P.n("fv"))])
                P.add("s", "act", lnsc[:, 1:2], lnsc[:, 0:1], ACTF.Exp, 0.0, -0.5,
                      incs=[("fs", 1)])
                P.add("v", "ts", xnf[:], u_ap, stats[:, 2:3], lnsc[:, 1:2],
                      ALU.subtract, ALU.mult, waits=[("fs", P.n("fs"))], incs=[("fv", 1)])
                uload_done[par] = ("fv", P.n("fv"))
                P.add("v", "copy", xnb[:, par * 2048:(par + 1) * 2048], xnf[:],
                      incs=[("fv", 1)])
                xn_done[gi] = ("fv", P.n("fv"))
                if own and _DEBUG:
                    P.add("q", "dma", xn_scr[(gi * 128) % 2048:(gi * 128) % 2048 + 128, :],
                          xnb[:, par * 2048:(par + 1) * 2048], waits=[xn_done[gi]],
                          incs=[("xs", 16)])
                if own:
                    row0o = row0
                    P.add("v", "red", xsc[:, 0:64], xnf[:].rearrange("p (b k) -> p b k", k=32),
                          ALU.max, True, incs=[("fv", 1)])
                    P.add("v", "ts", xsc[:, 64:128], xsc[:, 0:64], 1.0 / 448.0, 1e-30,
                          ALU.mult, ALU.max, waits=[("fv", P.n("fv"))], incs=[("fv", 1)])
                    P.add("v", "recip", xsc[:, 128:192], xsc[:, 64:128],
                          waits=[("fv", P.n("fv"))], incs=[("fv", 1)])
                    P.add("v", "ts", xsc[:, 192:256], xsc[:, 64:128], 2.0, None, ALU.mult, None,
                          waits=[("fv", P.n("fv"))], incs=[("fv", 1)])
                    P.add("v", "stt", xq8[:].rearrange("p (b k) -> p b k", k=32),
                          xnf[:].rearrange("p (b k) -> p b k", k=32), 0.5,
                          xsc[:, 128:192].unsqueeze(-1).broadcast_to([128, 64, 32]),
                          ALU.mult, ALU.mult, waits=[("fv", P.n("fv"))])
                    P.add("v", "tt", xdq[:].rearrange("p (b k) -> p b k", k=32),
                          xq8[:].rearrange("p (b k) -> p b k", k=32),
                          xsc[:, 192:256].unsqueeze(-1).broadcast_to([128, 64, 32]),
                          ALU.mult, incs=[("fv", 1)])
                    nq = ("fv", P.n("fv"))
                    for hc4 in range(4):
                        gslot = cv.ptr_grp % 2
                        stg = wstage_x[:, gslot * 512:(gslot + 1) * 512]
                        trs = [xdq[:, (hc4 * 4 + i2) * 128:(hc4 * 4 + i2 + 1) * 128]
                               for i2 in range(4)]
                        xsem = "xd%d" % gslot
                        ee = [(xsem, xstage_guard[gslot])] if gslot in xstage_guard else []
                        cv.tr_group(trs, stg, "v", extra_waits=[nq], evac_extra=ee)
                        dest = xdq_scr[hc4 * 512:(hc4 + 1) * 512, row0o:row0o + 128].rearrange(
                            "(b p) t -> p b t", p=128)
                        P.add("q", "dma", dest, stg.rearrange("p (b f) -> p b f", b=4),
                              waits=[cv.ptr_evac[cv.ptr_grp - 1]], incs=[(xsem, 16)])
                        xstage_guard[gslot] = P.n(xsem)

            wstage_x = e3(nc.sbuf_tensor("wstage_x", [128, 1024], BF16))

            def xn_tr(gi, gb):
                par = gi % 2
                src = xnb[:, par * 2048:(par + 1) * 2048]
                for hc4 in range(4):
                    trs = [src[:, (hc4 * 4 + i2) * 128:(hc4 * 4 + i2 + 1) * 128]
                           for i2 in range(4)]
                    dest = xnT[:].rearrange("p (hc t) -> p hc t", hc=16)[
                        :, hc4 * 4:(hc4 + 1) * 4, gb * 128:(gb + 1) * 128]
                    ew = [xn_done[gi]]
                    if gi == 0 and hc4 == 0:
                        ew.append(("ld", n_const_ld))
                    cv.tr_group(trs, dest, "v" if hc4 % 2 == 0 else "s",
                                extra_waits=ew)

            qtsave = e3(nc.sbuf_tensor("qtsave", [128, 32], BF16))
            qmini_prev = None
            qts_bar = None
            for grp in range(4):
                own_grp = grp >= 2
                for gb in range(8):
                    gi = grp * 8 + gb
                    ln_tile(gi, own_grp)
                    xn_tr(gi, gb)
                xnt_bar = (("fv", P.n("fv")), ("fs", P.n("fs")))
                if grp == 1:
                    P.add("v", "copy", qtsave[:].rearrange("p (hc t) -> p hc t", t=2),
                          xnT[:].rearrange("p (hc t) -> p hc t", hc=16)[:, :, 1022:1024],
                          waits=[xnt_bar[0], xnt_bar[1]], incs=[("fv", 1)])
                    qts_bar = ("fv", P.n("fv"))
                projs = ([("WqT", 0, qraw, True)] if own_grp else []) + \
                    [("WkT", 1, kraw, True), ("WvT", 2, vraw, False)]
                for ec in range(16):
                    raw_done = {}
                    rpar = ec % 2
                    for wname, bidx, rawbuf, haspad in projs:
                        wpar = pp_k % 2
                        lw = []
                        if wpar in wsl_done:
                            lw.append(wsl_done[wpar])
                        if grp == 0 and ec == 0 and wname == projs[0][0]:
                            lw = lw + [("gd", n_wcast)]
                        P.add("q", "dma", wslab[:, wpar * 2048:(wpar + 1) * 2048].rearrange(
                            "p (hc f) -> p hc f", hc=16),
                            wbf[wname][:, ec * 128:(ec + 1) * 128].rearrange(
                            "(hc p) f -> p hc f", p=128), waits=lw, incs=[("ls%d" % wpar, 16)])
                        nld = P.n("ls%d" % wpar)
                        pad = 4 if haspad else 0
                        rw = rawbuf[:, rpar * (1024 + pad): rpar * (1024 + pad) + 1024 + pad]
                        for blk2 in range(2):
                            bank = pA if pp_k % 2 == 0 else pB
                            cons = "v" if pp_k % 2 == 0 else "s"
                            waits = [("ls%d" % wpar, nld)]
                            if ec == 0 and wname == projs[0][0] and blk2 == 0:
                                waits += [xnt_bar[0], xnt_bar[1]]
                            if pp_k - 2 in pp_evac:
                                waits.append(pp_evac[pp_k - 2])
                            for hc in range(16):
                                P.add("t", "mm", bank[:, 0:512],
                                      wslab[:, wpar * 2048 + hc * 128: wpar * 2048 + (hc + 1) * 128],
                                      xnT[:].rearrange("p (hc t) -> p hc t", hc=16)[
                                          :, hc, blk2 * 512:(blk2 + 1) * 512],
                                      hc == 0, hc == 15,
                                      waits=waits if hc == 0 else (),
                                      incs=[("ft", 1)] if hc == 15 else ())
                            dst = rw[:, pad + blk2 * 512: pad + (blk2 + 1) * 512]
                            cn = "fv" if cons == "v" else "fs"
                            evw = [("ft", P.n("ft"))]
                            if rpar in conv_done_par:
                                evw.append(conv_done_par[rpar])
                            if own_grp:
                                if cons == "v":
                                    P.add(cons, "ts", dst, bank[:, 0:512], 1.0,
                                          bias_ap(bidx, ec), ALU.mult, ALU.add,
                                          waits=evw, incs=[(cn, 1)])
                                else:
                                    P.add(cons, "act", dst, bank[:, 0:512], ACTF.Identity,
                                          bias_ap(bidx, ec), 1.0,
                                          waits=evw, incs=[(cn, 1)])
                            else:
                                if cons == "v":
                                    P.add(cons, "ts", dst, bank[:, 0:512], maskv[:, 0:1],
                                          bm[:, bidx * 16 + ec: bidx * 16 + ec + 1],
                                          ALU.mult, ALU.add,
                                          waits=evw, incs=[(cn, 1)])
                                else:
                                    P.add(cons, "act", dst, bank[:, 0:512], ACTF.Identity,
                                          bm[:, bidx * 16 + ec: bidx * 16 + ec + 1],
                                          maskv[:, 0:1],
                                          waits=evw, incs=[(cn, 1)])
                            pp_evac[pp_k] = (cn, P.n(cn))
                            raw_done[wname] = (cn, P.n(cn))
                            pp_k += 1
                        if wname == "WqT" and grp == 2:
                            mw = [qts_bar]
                            if qmini_prev is not None:
                                mw.append(qmini_prev)
                            for hc in range(16):
                                P.add("t", "mm", pb["pXre"][0][:, 0:2],
                                      wslab[:, wpar * 2048 + hc * 128: wpar * 2048 + (hc + 1) * 128],
                                      qtsave[:, hc * 2:(hc + 1) * 2],
                                      hc == 0, hc == 15,
                                      waits=mw if hc == 0 else (),
                                      incs=[("ft", 1)] if hc == 15 else ())
                            qw0 = qraw[:, rpar * 1028: rpar * 1028 + 1028]
                            P.add("v", "ts", qw0[:, 2:4], pb["pXre"][0][:, 0:2],
                                  maskv[:, 0:1], bm[:, 0 * 16 + ec: 0 * 16 + ec + 1],
                                  ALU.mult, ALU.add,
                                  waits=[("ft", P.n("ft"))], incs=[("fv", 1)])
                            qmini_prev = ("fv", P.n("fv"))
                        wsl_done[wpar] = ("ft", P.n("ft"))
                    kw_ = kraw[:, rpar * 1028: rpar * 1028 + 1028]
                    vw_ = vraw[:, rpar * 1024: rpar * 1024 + 1024]
                    tw = [raw_done["WkT"], raw_done["WvT"]]
                    if own_grp:
                        tw.append(raw_done["WqT"])
                    if grp == 0:
                        P.add("v", "memset", kw_[:, 0:4], 0.0, waits=tw)
                    else:
                        P.add("v", "copy", kw_[:, 0:4], ktail[:, ec * 4:(ec + 1) * 4], waits=tw)
                    P.add("v", "ts", cscr[:, 0:1024], kw_[:, 2:1026], convw_ap(1, ec, 0),
                          bias_ap(5, ec), ALU.mult, ALU.add)
                    P.add("v", "stt", cscr[:, 1024:2048], kw_[:, 3:1027], convw_ap(1, ec, 1),
                          cscr[:, 0:1024], ALU.mult, ALU.add)
                    P.add("v", "stt", cscr[:, 0:1024], kw_[:, 4:1028], convw_ap(1, ec, 2),
                          cscr[:, 1024:2048], ALU.mult, ALU.add)
                    P.add("v", "copy", ktail[:, ec * 4:(ec + 1) * 4], kw_[:, 1024:1028])
                    kvw2 = [("kv%d" % rpar, kv_dma_par[rpar])] if rpar in kv_dma_par else []
                    P.add("v", "tt", kvec[:, rpar * 1024:(rpar + 1) * 1024],
                          cscr[:, 0:1024], vw_, ALU.mult, waits=kvw2, incs=[("fv", 1)])
                    kv_done = ("fv", P.n("fv"))
                    P.add("q", "dma",
                          kv_scr[ec * 128:(ec + 1) * 128, grp * 1024:(grp + 1) * 1024],
                          kvec[:, rpar * 1024:(rpar + 1) * 1024],
                          waits=[kv_done], incs=[("kv%d" % rpar, 16)])
                    kv_dma_par[rpar] = P.n("kv%d" % rpar)
                    if own_grp:
                        qw_ = qraw[:, rpar * 1028: rpar * 1028 + 1028]
                        if grp != 2:
                            P.add("v", "copy", qw_[:, 0:4], qtail[:, ec * 4:(ec + 1) * 4])
                        P.add("v", "ts", cscr[:, 0:1024], qw_[:, 2:1026], convw_ap(0, ec, 0),
                              bias_ap(4, ec), ALU.mult, ALU.add)
                        P.add("v", "stt", cscr[:, 1024:2048], qw_[:, 3:1027], convw_ap(0, ec, 1),
                              cscr[:, 0:1024], ALU.mult, ALU.add)
                        qcw2 = [("qc%d" % rpar, qc_dma_par[rpar])] if rpar in qc_dma_par else []
                        P.add("v", "stt", qcec[:, rpar * 1024:(rpar + 1) * 1024],
                              qw_[:, 4:1028], convw_ap(0, ec, 2),
                              cscr[:, 1024:2048], ALU.mult, ALU.add,
                              waits=qcw2, incs=[("fv", 1)])
                        P.add("v", "copy", qtail[:, ec * 4:(ec + 1) * 4], qw_[:, 1024:1028],
                              incs=[("fv", 1)])
                        P.add("q", "dma",
                              q_scr[ec * 128:(ec + 1) * 128, (grp - 2) * 1024:(grp - 1) * 1024],
                              qcec[:, rpar * 1024:(rpar + 1) * 1024],
                              waits=[("fv", P.n("fv"))], incs=[("qc%d" % rpar, 16)])
                        qc_dma_par[rpar] = P.n("qc%d" % rpar)
                    conv_done_par[rpar] = ("fv", P.n("fv"))
            kv_all = [(n, P.n(n)) for n in ("kv0", "kv1", "qc0", "qc1")]

        with ExitStack() as sc1:
            e1 = sc1.enter_context
            kert = [e1(nc.sbuf_tensor(f"kert{i}", [128, 2048], BF16)) for i in range(2)]
            p3_bar = [("ft", P.n("ft")), ("fv", P.n("fv")), ("fs", P.n("fs"))] + \
                list(kv_all) + [("xd0", P.n("xd0")), ("xd1", P.n("xd1"))]
            for i in range(2):
                P.add("g", "dma", kert[i][:], din["ker_sl"][i * 128:(i + 1) * 128, :],
                      waits=p3_bar if i == 0 else (), incs=[("gk", 16)])
            n_kert = P.n("gk")

            bufsA = _ConvBufs(nc, sc1)
            cv.b = bufsA
            cv.emit_ker(kert, ccK_in[:], extra_first=[("gk", n_kert), ("ld", n_const_ld)])
            nkq_ker = (P.n("kq0"), P.n("kq1"))

            wtile = e1(nc.sbuf_tensor("wtile", [128, 2 * 2048], F32))
            wq8 = e1(nc.sbuf_tensor("wq8", [128, 2048], FP8))
            wdq = e1(nc.sbuf_tensor("wdq", [128, 2048], BF16))
            wsc = e1(nc.sbuf_tensor("wsc", [128, 4 * 64], F32))
            wstage = e1(nc.sbuf_tensor("wstage", [128, 1024], BF16))

            def wtile_src(j):
                if j < 8:
                    return din["gw_s"][j * 128:(j + 1) * 128, :]
                if j < 16:
                    return din["uw_s"][(j - 8) * 128:(j - 8 + 1) * 128, :]
                jj = j - 16
                r, qq = jj // 4, jj % 4
                return din["dw_s"][r * 128:(r + 1) * 128, qq * 2048:(qq + 1) * 2048]

            def wtile_dst(j, hc4, stg):
                if j < 16:
                    base = 0 if j < 8 else 2048
                    col = (j % 8) * 128
                    rows = ccW_in[base + hc4 * 512: base + (hc4 + 1) * 512, col:col + 128]
                    return rows.rearrange("(b p) i -> p b i", p=128)
                jj = j - 16
                r, qq = jj // 4, jj % 4
                rows = ccW_in[4096 + hc4 * 512: 4096 + (hc4 + 1) * 512,
                              qq * 256 + r * 128: qq * 256 + (r + 1) * 128]
                return rows.rearrange("(b p) e -> p b e", p=128)

            wq_done = {}
            wstage_guard = {}
            wt_prev = None
            for j in range(24):
                par = j % 2
                lw = []
                if j == 0:
                    lw += list(kv_all) + [("xd0", P.n("xd0")), ("xd1", P.n("xd1"))]
                if par in wq_done:
                    lw.append(wq_done[par])
                P.add("q", "dma", wtile[:, par * 2048:(par + 1) * 2048], wtile_src(j),
                      waits=lw, incs=[("lwt%d" % par, 16)])
                nld = P.n("lwt%d" % par)
                src = wtile[:, par * 2048:(par + 1) * 2048]
                vw = [("lwt%d" % par, nld)]
                if wt_prev is not None:
                    vw.append(wt_prev)
                P.add("v", "red", wsc[:, 0:64], src.rearrange("p (b k) -> p b k", k=32),
                      ALU.max, True, waits=vw, incs=[("fv", 1)])
                P.add("v", "ts", wsc[:, 64:128], wsc[:, 0:64], 1.0 / 448.0, 1e-30,
                      ALU.mult, ALU.max, waits=[("fv", P.n("fv"))], incs=[("fv", 1)])
                P.add("v", "recip", wsc[:, 128:192], wsc[:, 64:128],
                      waits=[("fv", P.n("fv"))], incs=[("fv", 1)])
                P.add("v", "ts", wsc[:, 192:256], wsc[:, 64:128], 2.0, None, ALU.mult, None,
                      waits=[("fv", P.n("fv"))], incs=[("fv", 1)])
                P.add("v", "stt", wq8[:].rearrange("p (b k) -> p b k", k=32),
                      src.rearrange("p (b k) -> p b k", k=32), 0.5,
                      wsc[:, 128:192].unsqueeze(-1).broadcast_to([128, 64, 32]),
                      ALU.mult, ALU.mult, waits=[("fv", P.n("fv"))])
                P.add("v", "tt", wdq[:].rearrange("p (b k) -> p b k", k=32),
                      wq8[:].rearrange("p (b k) -> p b k", k=32),
                      wsc[:, 192:256].unsqueeze(-1).broadcast_to([128, 64, 32]),
                      ALU.mult, incs=[("fv", 1)])
                wq_done[par] = ("fv", P.n("fv"))
                for hc4 in range(4):
                    gslot = cv.ptr_grp % 2
                    stg = wstage[:, gslot * 512:(gslot + 1) * 512]
                    trs = [wdq[:, (hc4 * 4 + i2) * 128:(hc4 * 4 + i2 + 1) * 128]
                           for i2 in range(4)]
                    wsem = "wd%d" % gslot
                    ee = [(wsem, wstage_guard[gslot])] if gslot in wstage_guard else []
                    cv.tr_group(trs, stg, "v", extra_waits=[wq_done[par]], evac_extra=ee)
                    P.add("q", "dma", wtile_dst(j, hc4, stg),
                          stg.rearrange("p (b f) -> p b f", b=4),
                          waits=[cv.ptr_evac[cv.ptr_grp - 1]], incs=[(wsem, 16)])
                    wstage_guard[gslot] = P.n(wsem)
                wt_prev = ("ft", P.n("ft"))
            n_wd = (P.n("wd0"), P.n("wd1"))

        P.add("g", "coll", "AllGather", ALU.bypass, [[0, 1, 2, 3, 4, 5, 6, 7]],
              [ccK_in[:]], [ccK_out[:]],
              waits=[("kq0", nkq_ker[0]), ("kq1", nkq_ker[1])], incs=[("ck", 1)])
        P.add("g", "coll", "AllGather", ALU.bypass, [[0, 1, 2, 3, 4, 5, 6, 7]],
              [ccW_in[:]], [ccW_out[:]],
              waits=[("wd0", n_wd[0]), ("wd1", n_wd[1])], incs=[("cw", 1)])

        with ExitStack() as sc4:
            bufsB = _ConvBufs(nc, sc4)
            cv.b = bufsB
            kvt = [sc4.enter_context(nc.sbuf_tensor(f"kvt{i}", [128, 4096], BF16))
                   for i in range(2)]
            ych = sc4.enter_context(nc.sbuf_tensor("ych", [128, 2 * 2048], BF16))
            for ch in range(8):
                lw = list(kv_all)
                if ch > 0:
                    lw.append(("ft", P.n("ft")))
                else:
                    lw += [("ft", P.n("ft")), ("wd0", n_wd[0]), ("wd1", n_wd[1]),
                           ("kq0", nkq_ker[0]), ("kq1", nkq_ker[1])]
                for i in range(2):
                    P.add("q", "dma", kvt[i][:],
                          kv_scr[(2 * ch + i) * 128:(2 * ch + i + 1) * 128, :],
                          waits=lw if i == 0 else (), incs=[("kt", 16)])
                nld = P.n("kt")
                ef = [("kt", nld)] + ([("st", 16 * ch)] if ch > 0 else [])
                cv.strided_tr(kvt, 2, extra_first=ef)
                _emit_kv_rest(P, cv, ccK_out[ch], ych[:], k_wait=("ck", 1))
                P.add("q", "dma", y_scr[2 * ch * 128:(2 * ch + 2) * 128, :].rearrange(
                    "(cq p) t -> p cq t", cq=2), ych[:],
                    waits=[("fv", P.n("fv")), ("fs", P.n("fs"))], incs=[("st", 16)])
            n_yst = P.n("st")

        with ExitStack() as sc5:
            e5 = sc5.enter_context
            yblk = e5(nc.sbuf_tensor("yblk", [128, 16 * 512], BF16))
            kvob = e5(nc.sbuf_tensor("kvob", [128, 16 * 512], BF16))
            qcb = e5(nc.sbuf_tensor("qcb", [128, 16 * 512], BF16))
            yTb = e5(nc.sbuf_tensor("yTb", [128, 16 * 512], BF16))
            atb = e5(nc.sbuf_tensor("atb", [128, 16 * 512], BF16))
            wslab5 = e5(nc.sbuf_tensor("wslab5", [128, 2 * 2048], BF16))
            cscr5 = e5(nc.sbuf_tensor("cscr5", [128, 512], F32))
            wsl_done5 = {}
            wo_evac = {}
            p5_prev = None
            for blk in range(4):
                t0 = blk * 512
                lw = [("st", n_yst)]
                if p5_prev is not None:
                    lw.append(p5_prev)
                P.add("q", "dma", yblk[:].rearrange("p (hc t) -> p hc t", hc=16),
                      y_scr[:, t0:t0 + 512].rearrange("(hc p) t -> p hc t", p=128),
                      waits=lw, incs=[("l5", 16)])
                P.add("q", "dma", kvob[:].rearrange("p (hc t) -> p hc t", hc=16),
                      kv_scr[:, 2048 + t0: 2048 + t0 + 512].rearrange("(hc p) t -> p hc t", p=128),
                      incs=[("l5", 16)])
                P.add("q", "dma", qcb[:].rearrange("p (hc t) -> p hc t", hc=16),
                      q_scr[:, t0:t0 + 512].rearrange("(hc p) t -> p hc t", p=128),
                      incs=[("l5", 16)])
                nld = P.n("l5")
                for ec in range(16):
                    sl = slice(ec * 512, (ec + 1) * 512)
                    vw2 = [("l5", nld)] if ec == 0 else []
                    if ec == 0 and p5_prev is not None:
                        vw2.append(p5_prev)
                    P.add("v", "stt", cscr5[:, 0:512], kvob[:, sl],
                          bias_ap(6, ec), yblk[:, sl], ALU.mult, ALU.add, waits=vw2)
                    P.add("v", "tt", yTb[:, sl], cscr5[:, 0:512], qcb[:, sl], ALU.mult,
                          incs=[("fv", 1)] if ec == 15 else ())
                ybar = ("fv", P.n("fv"))
                for ec in range(16):
                    wpar = ec % 2
                    lw2 = []
                    if blk == 0 and ec == 0:
                        lw2.append(("gd", n_wcast))
                    if wpar in wsl_done5:
                        lw2.append(wsl_done5[wpar])
                    P.add("q", "dma", wslab5[:, wpar * 2048:(wpar + 1) * 2048].rearrange(
                        "p (hc f) -> p hc f", hc=16),
                        wbf["WoT"][:, ec * 128:(ec + 1) * 128].rearrange(
                        "(hc p) f -> p hc f", p=128), waits=lw2, incs=[("l5w%d" % wpar, 16)])
                    nld2 = P.n("l5w%d" % wpar)
                    bank = pA if ec % 2 == 0 else pB
                    cons = "v" if ec % 2 == 0 else "s"
                    waits = [("l5w%d" % wpar, nld2), ybar]
                    if ec - 2 in wo_evac:
                        waits.append(wo_evac[ec - 2])
                    for hc in range(16):
                        P.add("t", "mm", bank[:, 0:512],
                              wslab5[:, wpar * 2048 + hc * 128: wpar * 2048 + (hc + 1) * 128],
                              yTb[:, hc * 512:(hc + 1) * 512],
                              hc == 0, hc == 15,
                              waits=waits if hc == 0 else (),
                              incs=[("ft", 1)] if hc == 15 else ())
                    cn = "fv" if cons == "v" else "fs"
                    ew2 = [("ft", P.n("ft"))]
                    if p5_prev is not None and ec < 2:
                        ew2.append(p5_prev)
                    if cons == "v":
                        P.add(cons, "ts", atb[:, ec * 512:(ec + 1) * 512], bank[:, 0:512], 1.0,
                              bias_ap(3, ec), ALU.mult, ALU.add,
                              waits=ew2, incs=[(cn, 1)])
                    else:
                        P.add(cons, "act", atb[:, ec * 512:(ec + 1) * 512], bank[:, 0:512],
                              ACTF.Identity, bias_ap(3, ec), 1.0,
                              waits=ew2, incs=[(cn, 1)])
                    wo_evac[ec] = (cn, P.n(cn))
                    wsl_done5[wpar] = ("ft", P.n("ft"))
                wo_evac = {}
                P.add("q", "dma", attn_scr[:, t0:t0 + 512].rearrange("(hc p) t -> p hc t", p=128),
                      atb[:].rearrange("p (hc t) -> p hc t", hc=16),
                      waits=[("fv", P.n("fv")), ("fs", P.n("fs"))], incs=[("at", 16)])
                p5_prev = ("at", P.n("at"))
            n_at = P.n("at")

        with ExitStack() as sc6:
            e6 = sc6.enter_context
            xdqT = e6(nc.sbuf_tensor("xdqT", [128, 16 * 2048], BF16))
            P.add("q", "dma", xdqT[:].rearrange("p (hc t) -> p hc t", hc=16),
                  xdq_scr[:].rearrange("(hc p) t -> p hc t", p=128),
                  waits=[("xd0", P.n("xd0")), ("xd1", P.n("xd1"))], incs=[("lx", 16)])
            n_xdqt = P.n("lx")
            islab = e6(nc.sbuf_tensor("islab", [128, 4 * 8192], BF16))
            hsil = e6(nc.sbuf_tensor("hsil", [128, 512], BF16))
            hf = e6(nc.sbuf_tensor("hf", [128, 512], F32))
            hq8 = e6(nc.sbuf_tensor("hq8", [128, 512], FP8))
            hdq = e6(nc.sbuf_tensor("hdq", [128, 512], BF16))
            hsc = e6(nc.sbuf_tensor("hsc", [128, 4 * 16], F32))
            hstage = e6(nc.sbuf_tensor("hstage", [128, 1024], BF16))
            isl_done = {}
            hstage_guard = {}
            h_tr_done = None
            h_done = {}
            hf_done = {}
            for ic in range(16):
                ipar = ic % 2
                rank, li = ic // 2, (ic % 2) * 512
                lw = [("cw", 1)]
                if ipar in isl_done:
                    lw.append(isl_done[ipar])
                P.add("q", "dma", islab[:, ipar * 16384: ipar * 16384 + 8192].rearrange(
                    "p (hc f) -> p hc f", hc=16),
                    ccW_out[rank, 0:2048, li:li + 512].rearrange("(hc p) f -> p hc f", p=128),
                    waits=lw, incs=[("li%d" % ipar, 16)])
                P.add("q", "dma", islab[:, ipar * 16384 + 8192: ipar * 16384 + 16384].rearrange(
                    "p (hc f) -> p hc f", hc=16),
                    ccW_out[rank, 2048:4096, li:li + 512].rearrange("(hc p) f -> p hc f", p=128),
                    incs=[("li%d" % ipar, 16)])
                nld = P.n("li%d" % ipar)
                for ts_ in range(16):
                    kk = ic * 16 + ts_
                    bank = [pC, pD, pA][kk % 3]
                    bank2 = [pE, pF, pB][kk % 3]
                    waits = [("li%d" % ipar, nld), ("lx", n_xdqt)]
                    if kk < 3:
                        waits.append(("at", n_at))
                    if kk - 3 in h_done:
                        waits.append(h_done[kk - 3])
                    for hc in range(16):
                        P.add("t", "mm", bank[:, 0:512],
                              xdqT[:].rearrange("p (hc t) -> p hc t", hc=16)[
                                  :, hc, ts_ * 128:(ts_ + 1) * 128],
                              islab[:, ipar * 16384 + hc * 512: ipar * 16384 + (hc + 1) * 512],
                              hc == 0, hc == 15,
                              waits=waits if hc == 0 else ())
                    for hc in range(16):
                        P.add("t", "mm", bank2[:, 0:512],
                              xdqT[:].rearrange("p (hc t) -> p hc t", hc=16)[
                                  :, hc, ts_ * 128:(ts_ + 1) * 128],
                              islab[:, ipar * 16384 + 8192 + hc * 512:
                                    ipar * 16384 + 8192 + (hc + 1) * 512],
                              hc == 0, hc == 15,
                              incs=[("ft", 1)] if hc == 15 else ())
                    nftg = P.n("ft")
                    silw = [("ft", nftg)]
                    if kk - 1 in hf_done:
                        silw.append(hf_done[kk - 1])
                    P.add("s", "act", hsil[:], bank[:, 0:512], ACTF.Silu, 0.0, 1.0,
                          waits=silw, incs=[("fs", 1)])
                    hfw = [("fs", P.n("fs"))]
                    if h_tr_done is not None:
                        hfw.append(h_tr_done)
                    P.add("v", "tt", hf[:], hsil[:], bank2[:, 0:512], ALU.mult,
                          waits=hfw, incs=[("fv", 1)])
                    hf_done[kk] = ("fv", P.n("fv"))
                    P.add("v", "red", hsc[:, 0:16], hf[:].rearrange("p (b k) -> p b k", k=32),
                          ALU.max, True, incs=[("fv", 1)])
                    P.add("v", "ts", hsc[:, 16:32], hsc[:, 0:16], 1.0 / 448.0, 1e-30,
                          ALU.mult, ALU.max, waits=[("fv", P.n("fv"))], incs=[("fv", 1)])
                    P.add("v", "recip", hsc[:, 32:48], hsc[:, 16:32],
                          waits=[("fv", P.n("fv"))], incs=[("fv", 1)])
                    P.add("v", "ts", hsc[:, 48:64], hsc[:, 16:32], 2.0, None, ALU.mult, None,
                          waits=[("fv", P.n("fv"))], incs=[("fv", 1)])
                    P.add("v", "stt", hq8[:].rearrange("p (b k) -> p b k", k=32),
                          hf[:].rearrange("p (b k) -> p b k", k=32), 0.5,
                          hsc[:, 32:48].unsqueeze(-1).broadcast_to([128, 16, 32]),
                          ALU.mult, ALU.mult, waits=[("fv", P.n("fv"))])
                    P.add("v", "tt", hdq[:].rearrange("p (b k) -> p b k", k=32),
                          hq8[:].rearrange("p (b k) -> p b k", k=32),
                          hsc[:, 48:64].unsqueeze(-1).broadcast_to([128, 16, 32]),
                          ALU.mult, incs=[("fv", 1)])
                    h_done[kk] = ("fv", P.n("fv"))
                    gslot = cv.ptr_grp % 2
                    stg = hstage[:, gslot * 512:(gslot + 1) * 512]
                    trs = [hdq[:, i2 * 128:(i2 + 1) * 128] for i2 in range(4)]
                    hsem = "hd%d" % gslot
                    ee = [(hsem, hstage_guard[gslot])] if gslot in hstage_guard else []
                    cv.tr_group(trs, stg, "v", extra_waits=[h_done[kk]], evac_extra=ee)
                    h_tr_done = ("ft", P.n("ft"))
                    dest = h_scr[ic * 512:(ic + 1) * 512, ts_ * 128:(ts_ + 1) * 128].rearrange(
                        "(b p) t -> p b t", p=128)
                    P.add("q", "dma", dest, stg.rearrange("p (b f) -> p b f", b=4),
                          waits=[cv.ptr_evac[cv.ptr_grp - 1]], incs=[(hsem, 16)])
                    hstage_guard[gslot] = P.n(hsem)
                isl_done[ipar] = ("ft", P.n("ft"))
            n_hd = (P.n("hd0"), P.n("hd1"))

        with ExitStack() as sc7:
            e7 = sc7.enter_context
            hslab = e7(nc.sbuf_tensor("hslab", [128, 64 * 512], BF16))
            dwsl = e7(nc.sbuf_tensor("dwsl", [128, 2 * 8192], BF16))
            uTs = e7(nc.sbuf_tensor("uTs", [128, 16 * 512], F32))
            atl = e7(nc.sbuf_tensor("atl", [128, 16 * 512], BF16))
            ofin = e7(nc.sbuf_tensor("ofin", [128, 2 * 512], F32))
            dw_done = {}
            fin_done = {}
            st_par = {}
            for tc in range(4):
                t0 = tc * 512
                lw = [("hd0", n_hd[0]), ("hd1", n_hd[1]), ("at", n_at)]
                if tc > 0:
                    lw.append(("ft", P.n("ft")))
                    lw.append(("fv", P.n("fv")))
                    lw.append(("fs", P.n("fs")))
                P.add("q", "dma", hslab[:].rearrange("p (ic t) -> p ic t", ic=64),
                      h_scr[:, t0:t0 + 512].rearrange("(ic p) t -> p ic t", p=128),
                      waits=lw, incs=[("l6", 16)])
                P.add("q", "dma", uTs[:].rearrange("p (hc t) -> p hc t", hc=16),
                      uT_own[:, t0:t0 + 512].rearrange("(hc p) t -> p hc t", p=128),
                      incs=[("l6", 16)])
                P.add("q", "dma", atl[:].rearrange("p (hc t) -> p hc t", hc=16),
                      attn_scr[:, t0:t0 + 512].rearrange("(hc p) t -> p hc t", p=128),
                      incs=[("l6", 16)])
                nld = P.n("l6")
                for ec in range(16):
                    wpar = ec % 2
                    rank = ec // 2
                    qq_base = 0
                    lw2 = []
                    if wpar in dw_done:
                        lw2.append(dw_done[wpar])
                    le = (ec % 2) * 128 + (ec // 2) * 0
                    erank, esub = (ec * 128) // 256, (ec * 128) % 256
                    src = ccW_out[erank, 4096:6144, :].rearrange(
                        "(icl p) (m e) -> p m icl e", p=128, m=4)[:, :, :, esub:esub + 128]
                    P.add("q", "dma", dwsl[:, wpar * 8192:(wpar + 1) * 8192].rearrange(
                        "p (m icl e) -> p m icl e", m=4, icl=16), src,
                        waits=lw2, incs=[("l6w%d" % wpar, 16)])
                    nld2 = P.n("l6w%d" % wpar)
                    bank = pA if ec % 2 == 0 else pB
                    waits = [("l6", nld), ("l6w%d" % wpar, nld2)]
                    if ec % 2 in fin_done:
                        waits.append(fin_done[ec % 2])
                    for icc in range(64):
                        P.add("t", "mm", bank[:, 0:512],
                              dwsl[:, wpar * 8192 + icc * 128: wpar * 8192 + (icc + 1) * 128],
                              hslab[:, icc * 512:(icc + 1) * 512],
                              icc == 0, icc == 63,
                              waits=waits if icc == 0 else (),
                              incs=[("ft", 1)] if icc == 63 else ())
                    cn = "fv"
                    of = ofin[:, (ec % 2) * 512:(ec % 2 + 1) * 512]
                    ow = [("ft", P.n("ft"))]
                    if ec % 2 in st_par:
                        ow.append(("so%d" % (ec % 2), st_par[ec % 2]))
                    P.add("v", "tt", of, bank[:, 0:512], atl[:, ec * 512:(ec + 1) * 512],
                          ALU.add, waits=ow)
                    P.add("v", "tt", of, of, uTs[:, ec * 512:(ec + 1) * 512], ALU.add,
                          incs=[(cn, 1)])
                    fin_done[ec % 2] = (cn, P.n(cn))
                    P.add("q", "dma", outT[ec * 128:(ec + 1) * 128, t0:t0 + 512], of,
                          waits=[fin_done[ec % 2]], incs=[("so%d" % (ec % 2), 16)])
                    st_par[ec % 2] = P.n("so%d" % (ec % 2))
                    dw_done[wpar] = ("ft", P.n("ft"))

        P.add("g", "memset", biases[0:1, 0:8], 0.0,
              waits=[("so0", P.n("so0")), ("so1", P.n("so1")), ("st", P.n("st"))])
        for name, (scr, dout) in dbg.items():
            P.add("g", "dma", dout[:], scr[:], incs=[("db", 16)])
        if dbg:
            P.add("g", "memset", biases[0:1, 8:16], 0.0, waits=[("db", P.n("db"))])

        _run_block(nc, P)
    return nc


def _emit_kv_rest(P, cv, kin, ych_ap, k_wait):
    b, c = cv.b, cv.c
    pXre, pXim = cv.pb["pXre"], cv.pb["pXim"]
    pA, pB = cv.pb["pA"], cv.pb["pB"]
    bar = cv.stage1(2)
    x_prev = {}
    z_done = {}
    st_state = {"u_prev": None}
    usb_turn = {}
    ustage_prev = {}
    prev_pair = None

    def stageA_turn2(pp):
        ppar = pp % 2
        Zp = lambda im: b.Z[0:72, ppar * 1024 + im * 512: ppar * 1024 + im * 512 + 512]
        uw = [z_done[ppar]]
        if st_state["u_prev"] is not None:
            uw.append(st_state["u_prev"])
        P.add("t", "mm", pA[:, 0:512], c.BDAre[0:72, 0:128], Zp(0), True, False, waits=uw)
        P.add("t", "mm", pA[:, 0:512], c.BDAim_neg[0:72, 0:128], Zp(1), False, True)
        P.add("t", "mm", pB[:, 0:512], c.BDAim[0:72, 0:128], Zp(0), True, False)
        P.add("t", "mm", pB[:, 0:512], c.BDAre[0:72, 0:128], Zp(1), False, True,
              incs=[("ft", 1)])
        ustage_prev[ppar] = ("ft", P.n("ft"))
        sw = [("ft", P.n("ft"))]
        if ppar in usb_turn:
            sw.append(usb_turn[ppar])
        P.add("s", "copy", b.Usb[:, ppar * 1024: ppar * 1024 + 512], pA[:, 0:512], waits=sw)
        P.add("s", "act", b.Usb[:, ppar * 1024 + 512: ppar * 1024 + 1024], pB[:, 0:512],
              ACTF.Copy, 0.0, -1.0, incs=[("fs", 1)])
        st_state["u_prev"] = ("fs", P.n("fs"))
        trs = []
        for ac in range(2):
            for im in range(2):
                for lg in range(2):
                    off = ppar * 1024 + im * 512 + lg * 256 + ac * 128
                    trs.append(b.Usb[:, off: off + 128])
        g0 = pp * 2
        dest = b.UT[:].rearrange("p (aci gg f) -> p aci gg f", aci=4, gg=32)[:, :, g0:g0 + 2, :]
        cv.tr_group(trs, dest, "v" if ppar == 0 else "s",
                    extra_waits=[st_state["u_prev"]])
        usb_turn[ppar] = ("ft", P.n("ft"))

    for pair in range(16):
        par = pair % 2
        kw = [k_wait]
        if par in z_done:
            kw.append(z_done[par])
        ks = "kk%d" % par
        P.add("q", "dma", b.Kbuf[0:72, par * 1024: par * 1024 + 1024],
              kin[pair], waits=kw, incs=[(ks, 16)])
        nkq = P.n(ks)
        ew = [("fv", bar[0]), ("fs", bar[1])] if pair < 2 else ()
        cv.pair_turn1(pair, extra_waits=ew)
        nft = cv.pair_stage2(pair, x_prev)
        sw = [("ft", nft)]
        if par in z_done:
            sw.append(z_done[par])
        P.add("s", "copy", b.tmp[0:72, par * 1024: par * 1024 + 512],
              pXre[par][0:72, 0:512], waits=sw)
        P.add("s", "copy", b.tmp[0:72, par * 1024 + 512: par * 1024 + 1024],
              pXim[par][0:72, 0:512], incs=[("fs", 1)])
        nfs = P.n("fs")
        x_prev[par] = ("fs", nfs)
        K = lambda im: b.Kbuf[0:72, par * 1024 + im * 512: par * 1024 + im * 512 + 512]
        Z = lambda im: b.Z[0:72, par * 1024 + im * 512: par * 1024 + im * 512 + 512]
        tre = b.tmp[0:72, par * 1024: par * 1024 + 512]
        tim = b.tmp[0:72, par * 1024 + 512: par * 1024 + 1024]
        zA, zB = b.tmp2[0:72, 0:512], b.tmp2[0:72, 512:1024]
        vw = [("fs", nfs), (ks, nkq)]
        if par in ustage_prev:
            vw.append(ustage_prev[par])
        P.add("v", "tt", zA, tre, K(0), ALU.mult, waits=vw)
        P.add("v", "tt", zB, tim, K(1), ALU.mult)
        P.add("v", "tt", Z(0), zA, zB, ALU.subtract)
        P.add("v", "tt", zA, tre, K(1), ALU.mult)
        P.add("v", "tt", zB, tim, K(0), ALU.mult)
        P.add("v", "tt", Z(1), zA, zB, ALU.add, incs=[("fv", 1)])
        z_done[par] = ("fv", P.n("fv"))
        if prev_pair is not None:
            stageA_turn2(prev_pair)
        prev_pair = pair
    stageA_turn2(prev_pair)
    bar2 = (("fv", P.n("fv")), ("fs", P.n("fs")))
    py_prev = {}
    tb_done = {}
    for nu in range(16):
        par = nu % 2
        bank = pXre[par]
        w = [bar2[0], bar2[1]] if nu < 2 else []
        if par in py_prev:
            w.append(py_prev[par])
        if nu < 2 and par in x_prev:
            w.append(x_prev[par])
        first = True
        for ac in range(2):
            hoff = (nu * 2 + ac) * 128
            base_re = (ac * 2 + 0) * 4096
            base_im = (ac * 2 + 1) * 4096
            UTre = b.UT[:, base_re:base_re + 4096].rearrange(
                "p (g c2 n2) -> p g c2 n2", c2=8, n2=16)[:, :, :, nu]
            UTim = b.UT[:, base_im:base_im + 4096].rearrange(
                "p (g c2 n2) -> p g c2 n2", c2=8, n2=16)[:, :, :, nu]
            P.add("t", "mm", bank[:, 0:256], c.Hre[:, hoff:hoff + 128], UTre,
                  first, False, waits=w if first else ())
            first = False
            P.add("t", "mm", bank[:, 0:256], c.Him[:, hoff:hoff + 128], UTim,
                  False, ac == 1, incs=[("ft", 1)] if ac == 1 else ())
        cons = "v" if par == 0 else "s"
        cn = "fv" if par == 0 else "fs"
        yw = [("ft", P.n("ft"))]
        if par in tb_done:
            yw.append(tb_done[par])
        P.add(cons, "copy", b.ysb[:, par * 256: par * 256 + 256], bank[:, 0:256],
              waits=yw, incs=[(cn, 1)])
        py_prev[par] = (cn, P.n(cn))
        trs = [b.ysb[:, par * 256 + cq * 128: par * 256 + (cq + 1) * 128]
               for cq in range(2)]
        dest = ych_ap.rearrange("p (cq t) -> p cq t", cq=2)[:, :, nu:2048:16]
        cv.tr_group(trs, dest, "v" if par == 0 else "s", extra_waits=[py_prev[par]])
        tb_done[par] = ("ft", P.n("ft"))



_CACHE = {}


def _get_nc():
    if "nc" not in _CACHE:
        _CACHE["nc"] = _build(_DEBUG)
    return _CACHE["nc"]


def _chunkvec(v):
    return np.ascontiguousarray(np.asarray(v, np.float32).reshape(16, 128).T)


def _chunkw3(v):
    return np.ascontiguousarray(
        np.asarray(v, np.float32).reshape(16, 128, 3).transpose(1, 0, 2).reshape(128, 48))


def kernel(**inputs):
    u = np.asarray(inputs["L0_attn_u"], dtype=np.float32)
    fc = _fft_constants()
    shared = {
        "WqT": np.ascontiguousarray(np.asarray(inputs["L0_attn_Wq"], np.float32).T),
        "WkT": np.ascontiguousarray(np.asarray(inputs["L0_attn_Wk"], np.float32).T),
        "WvT": np.ascontiguousarray(np.asarray(inputs["L0_attn_Wv"], np.float32).T),
        "WoT": np.ascontiguousarray(np.asarray(inputs["L0_attn_Wo"], np.float32).T),
        "bq": _chunkvec(inputs["L0_attn_bq"]),
        "bk": _chunkvec(inputs["L0_attn_bk"]),
        "bv": _chunkvec(inputs["L0_attn_bv"]),
        "bo": _chunkvec(inputs["L0_attn_bo"]),
        "cbq": _chunkvec(inputs["L0_attn_convq_b"]),
        "cbk": _chunkvec(inputs["L0_attn_convk_b"]),
        "dskip": _chunkvec(inputs["L0_attn_D_skip"]),
        "cwq": _chunkw3(inputs["L0_attn_convq_w"]),
        "cwk": _chunkw3(inputs["L0_attn_convk_w"]),
    }
    for n, a in fc.items():
        shared["c_" + n] = a
    ker = np.asarray(inputs["L0_attn_ssm_kernel"], np.float32)
    gw = np.asarray(inputs["L0_ffn_gate_w"], np.float32)
    uw = np.asarray(inputs["L0_ffn_up_w"], np.float32)
    dw = np.asarray(inputs["L0_ffn_down_w"], np.float32)
    zeros = np.zeros((TOK, D), np.float32)
    in_maps = []
    for core in range(8):
        s, h = core // 2, core % 2
        m = dict(shared)
        uo = np.ascontiguousarray(u[s, h * TOK:(h + 1) * TOK, :])
        m["u_own"] = uo
        m["u_ctx"] = np.ascontiguousarray(u[s, 0:TOK, :]) if h == 1 else uo
        m["uT_own"] = np.ascontiguousarray(uo.T)
        m["ctxm"] = np.full((128, 1), float(h), np.float32)
        m["ker_sl"] = np.ascontiguousarray(ker[core * 256:(core + 1) * 256, :])
        m["gw_s"] = np.ascontiguousarray(gw[core * 1024:(core + 1) * 1024, :])
        m["uw_s"] = np.ascontiguousarray(uw[core * 1024:(core + 1) * 1024, :])
        m["dw_s"] = np.ascontiguousarray(dw[core * 256:(core + 1) * 256, :])
        in_maps.append(m)
    nc = _get_nc()
    res = run_bass_kernel_spmd(nc, in_maps, core_ids=list(range(8)))
    out = np.empty((4, 4096, D), np.float32)
    for core in range(8):
        s, h = core // 2, core % 2
        out[s, h * TOK:(h + 1) * TOK, :] = res.results[core]["outT"].T
    if _DEBUG:
        kernel._dbg = [{k: v for k, v in r.items() if k.startswith("dbg_")}
                       for r in res.results]
    return out


# version 04
# speedup vs baseline: 1.2055x; 1.1435x over previous
import numpy as np
import ml_dtypes
import concourse.bass as bass
import concourse.mybir as mybir
from concourse.bass_utils import run_bass_kernel_spmd
from contextlib import ExitStack

F32 = mybir.dt.float32
BF16 = mybir.dt.bfloat16
FP8 = mybir.dt.float8e4
ALU = mybir.AluOpType
AX = mybir.AxisListType.X
ACTF = mybir.ActivationFunctionType
BF = ml_dtypes.bfloat16

_DEBUG = ()

D = 2048
TOK = 2048
WIN = 4096
I_FF = 8192



def _fft_constants():
    Gre = np.zeros((128, 16 * 2 * 2 * 128))
    Gim = np.zeros((128, 16 * 2 * 2 * 128))
    r_r = np.arange(256)
    a_r = np.arange(256)
    for s in range(16):
        ph = np.exp(-2j * np.pi * (np.outer(r_r, a_r) / 256.0 + s * a_r[None, :] / 4096.0))
        for rc in range(2):
            for ac in range(2):
                off = ((s * 2 + rc) * 2 + ac) * 128
                Gre[:, off:off + 128] = ph[rc * 128:(rc + 1) * 128, ac * 128:(ac + 1) * 128].real
                Gim[:, off:off + 128] = ph[rc * 128:(rc + 1) * 128, ac * 128:(ac + 1) * 128].imag
    BD2 = np.zeros((128, 72), np.complex128)
    for c in range(8):
        for s in range(16):
            for b in range(9):
                BD2[c * 16 + s, c * 9 + b] = np.exp(-2j * np.pi * s * b / 16.0)
    BDA = np.zeros((72, 128), np.complex128)
    for c in range(8):
        for b in range(9):
            for nu in range(16):
                BDA[c * 9 + b, c * 16 + nu] = np.exp(2j * np.pi * b * nu / 16.0)
    Hre = np.zeros((128, 16 * 2 * 128))
    Him = np.zeros((128, 16 * 2 * 128))
    for nu in range(16):
        for ac in range(2):
            a = (ac * 128 + np.arange(128))[:, None]
            eta = (128 + np.arange(128))[None, :]
            ph = np.exp(2j * np.pi * (a * nu / 4096.0 + a * eta / 256.0))
            off = (nu * 2 + ac) * 128
            Hre[:, off:off + 128] = ph.real
            Him[:, off:off + 128] = ph.imag
    cm = np.zeros((72, 256))
    for c in range(8):
        for b in range(9):
            if b < 8:
                w = np.full(256, 2.0)
                if b == 0:
                    w[0] = 1.0
            else:
                w = np.zeros(256)
                w[0] = 1.0
            cm[c * 9 + b] = w / 4096.0
    return {
        "Gre": Gre.astype(BF), "Gim": Gim.astype(BF),
        "BD2re": np.ascontiguousarray(BD2.real).astype(BF),
        "BD2im": np.ascontiguousarray(BD2.imag).astype(BF),
        "BD2im_neg": np.ascontiguousarray(-BD2.imag).astype(BF),
        "BDAre": np.ascontiguousarray(BDA.real).astype(BF),
        "BDAim": np.ascontiguousarray(BDA.imag).astype(BF),
        "BDAim_neg": np.ascontiguousarray(-BDA.imag).astype(BF),
        "Hre": Hre.astype(BF), "Him": Him.astype(BF),
        "cmask": cm.astype(np.float32),
        "identity": np.eye(128).astype(BF),
    }



class _Plan:
    def __init__(self, nc):
        self.nc = nc
        self.steps = {e: [] for e in ("t", "v", "s", "g", "q")}
        self.sems = {}
        self.counts = {}

    def sem(self, name):
        if name not in self.sems:
            self.sems[name] = self.nc.alloc_semaphore("sem_" + name)
            self.counts[name] = 0
        return self.sems[name]

    def n(self, name):
        self.sem(name)
        return self.counts[name]

    def add(self, eng, kind, *args, waits=(), incs=()):
        w = [(self.sem(nm), v) for nm, v in waits]
        inc = []
        for nm, amt in incs:
            inc.append((self.sem(nm), amt))
            self.counts[nm] += amt
        self.steps[eng].append((kind, args, w, inc))

    def replay(self, key, eng):
        for kind, args, waits, incs in self.steps[key]:
            for sem, val in waits:
                eng.wait_ge(sem, val)
            ins = self._emit(eng, kind, args)
            for sem, amt in incs:
                ins.then_inc(sem, amt)

    @staticmethod
    def _emit(eng, kind, a):
        if kind == "mm":
            out, lhsT, rhs, start, stop = a
            return eng.matmul(out, lhsT, rhs, start=start, stop=stop)
        if kind == "tr":
            out, in_, ident = a
            return eng.transpose(out, in_, ident)
        if kind == "copy":
            out, in_ = a
            if hasattr(eng, "tensor_copy"):
                return eng.tensor_copy(out, in_)
            return eng.copy(out, in_)
        if kind == "tt":
            out, in0, in1, op = a
            return eng.tensor_tensor(out, in0, in1, op=op)
        if kind == "ts":
            out, in0, s1, s2, op0, op1 = a
            if op1 is None:
                return eng.tensor_scalar(out, in0, s1, None, op0=op0)
            return eng.tensor_scalar(out, in0, s1, s2, op0=op0, op1=op1)
        if kind == "stt":
            out, in0, scal, in1, op0, op1 = a
            return eng.scalar_tensor_tensor(out, in0, scal, in1, op0=op0, op1=op1)
        if kind == "act":
            out, in_, func, bias, scale = a
            return eng.activation(out, in_, func, bias=bias, scale=scale)
        if kind == "red":
            out, in_, op, absval = a
            return eng.tensor_reduce(out, in_, axis=AX, op=op, apply_absolute_value=absval)
        if kind == "bnst":
            out, in_ = a
            return eng.bn_stats(out, in_)
        if kind == "bnag":
            out, in_ = a
            return eng.bn_aggr(out, in_)
        if kind == "dma":
            out, in_ = a
            return eng.dma_start(out, in_)
        if kind == "memset":
            ap, val = a
            return eng.memset(ap, val)
        if kind == "recip":
            out, in_ = a
            return eng.reciprocal(out, in_)
        if kind == "coll":
            kind_s, op, groups, ins_, outs_ = a
            return eng.collective_compute(kind_s, op, replica_groups=groups,
                                          ins=ins_, outs=outs_)
        raise KeyError(kind)


def _run_block(nc, plan):
    with nc.Block() as blk:
        if plan.steps["q"]:
            blk.sync(lambda e: plan.replay("q", e))
        if plan.steps["g"]:
            blk.gpsimd(lambda e: plan.replay("g", e))
        if plan.steps["v"]:
            blk.vector(lambda e: plan.replay("v", e))
        if plan.steps["s"]:
            blk.scalar(lambda e: plan.replay("s", e))
        if plan.steps["t"]:
            blk.tensor(lambda e: plan.replay("t", e))



class _ConvBufs:
    _SEQ = [0]

    def __init__(self, nc, stack):
        ent = stack.enter_context
        _ConvBufs._SEQ[0] += 1
        pref = "cv%d_" % _ConvBufs._SEQ[0]
        T = lambda n, sh, dt: ent(nc.sbuf_tensor(pref + n, sh, dt))
        self.S = T("S", [128, 16 * 2 * 256], BF16)
        self.E = T("E", [128, 2 * 2 * 4096], BF16)
        self.UT = T("UT", [128, 2 * 2 * 4096], BF16)
        self.Fbuf = T("F", [128, 2 * 1024], BF16)
        self.Z = T("Z", [128, 2 * 1024], BF16)
        self.Ksb = T("Ksb", [128, 2 * 1024], BF16)
        self.Kbuf = self.Ksb
        self.Usb = T("Usb", [128, 2 * 1024], BF16)
        self.tmp = T("tmp", [128, 2 * 1024], F32)
        self.tmp2 = T("tmp2", [128, 1024], F32)
        self.ysb = T("ysb", [128, 2 * 256], BF16)


class _Conv:

    def __init__(self, P, b, cb, pb):
        self.P, self.b, self.c, self.pb = P, b, cb, pb
        self.ptr_grp = 0
        self.ptr_evac = {}

    def tr_group(self, trs, dest, cons, extra_waits=(), evac_extra=()):
        P, b = self.P, self.b
        g = self.ptr_grp
        bank = self.pb["ptr"][g % 2]
        waits = list(extra_waits)
        if g - 2 in self.ptr_evac:
            waits.append(self.ptr_evac[g - 2])
        for i, in_ap in enumerate(trs):
            P.add("t", "tr", bank[:, i * 128:(i + 1) * 128], in_ap, self.c.ident[:],
                  waits=waits if i == 0 else (),
                  incs=[("ft", 1)] if i == len(trs) - 1 else ())
        cn = "fv" if cons == "v" else "fs"
        P.add(cons, "copy", dest, bank[:, 0:128 * len(trs)],
              waits=[("ft", P.n("ft"))] + list(evac_extra), incs=[(cn, 1)])
        self.ptr_evac[g] = (cn, P.n(cn))
        self.ptr_grp += 1

    def strided_tr(self, tiles, rchunks, extra_first=()):
        b = self.b
        first = True
        for s in range(16):
            for rc in range(rchunks):
                trs = [tiles[cq][:, s + 2048 * rc: s + 2048 * rc + 16 * 127 + 1: 16]
                      for cq in range(2)]
                off = (s * 2 + rc) * 256
                self.tr_group(trs, b.S[:, off:off + 256],
                              "v" if (s + rc) % 2 == 0 else "s",
                              extra_waits=extra_first if first else ())
                first = False

    def stage1(self, rchunks):
        P, b, c = self.P, self.b, self.c
        s1_evac = {}
        k = 0
        for s in range(16):
            for ac in range(2):
                for im in range(2):
                    bank = self.pb["pA"] if k % 2 == 0 else self.pb["pB"]
                    cons = "v" if k % 2 == 0 else "s"
                    waits = []
                    if k - 2 in s1_evac:
                        waits.append(s1_evac[k - 2])
                    if k <= 1:
                        waits += [("fv", P.n("fv")), ("fs", P.n("fs"))]
                    for rc in range(rchunks):
                        off = ((s * 2 + rc) * 2 + ac) * 128
                        G = (c.Gim if im else c.Gre)[:, off:off + 128]
                        soff = (s * 2 + rc) * 256
                        P.add("t", "mm", bank[:, 0:256], G, b.S[:, soff:soff + 256],
                              rc == 0, rc == rchunks - 1,
                              waits=waits if rc == 0 else (),
                              incs=[("ft", 1)] if rc == rchunks - 1 else ())
                    eoff = (ac * 2 + im) * 4096
                    dest = b.E[:, eoff:eoff + 4096].rearrange(
                        "p (g c2 s2) -> p g c2 s2", c2=8, s2=16)[:, :, :, s]
                    cn = "fv" if cons == "v" else "fs"
                    P.add(cons, "copy", dest, bank[:, 0:256],
                          waits=[("ft", P.n("ft"))], incs=[(cn, 1)])
                    s1_evac[k] = (cn, P.n(cn))
                    k += 1
        return P.n("fv"), P.n("fs")

    def pair_turn1(self, pair, extra_waits=()):
        b = self.b
        trs = []
        for im in range(2):
            for lg in range(2):
                g = pair * 2 + lg
                for ac in range(2):
                    eoff = (ac * 2 + im) * 4096
                    trs.append(b.E[:, eoff + g * 128: eoff + (g + 1) * 128])
        par = pair % 2
        dest = b.Fbuf[:, par * 1024: par * 1024 + 1024]
        self.tr_group(trs, dest, "v" if par == 0 else "s", extra_waits=extra_waits)

    def pair_stage2(self, pair, x_prev):
        P, b, c = self.P, self.b, self.c
        par = pair % 2
        fw = [self.ptr_evac[self.ptr_grp - 1]]
        if par in x_prev:
            fw.append(x_prev[par])
        F = lambda im: b.Fbuf[:, par * 1024 + im * 512: par * 1024 + im * 512 + 512]
        pXre, pXim = self.pb["pXre"], self.pb["pXim"]
        P.add("t", "mm", pXre[par][0:72, 0:512], c.BD2re[0:128, 0:72], F(0), True, False, waits=fw)
        P.add("t", "mm", pXre[par][0:72, 0:512], c.BD2im_neg[0:128, 0:72], F(1), False, True)
        P.add("t", "mm", pXim[par][0:72, 0:512], c.BD2im[0:128, 0:72], F(0), True, False)
        P.add("t", "mm", pXim[par][0:72, 0:512], c.BD2re[0:128, 0:72], F(1), False, True,
              incs=[("ft", 1)])
        return P.n("ft")

    def emit_ker(self, ker_tiles, kout, extra_first=()):
        P, b, c = self.P, self.b, self.c
        self.strided_tr(ker_tiles, 1, extra_first=extra_first)
        bar = self.stage1(1)
        x_prev = {}
        kq_guard = {}
        for pair in range(16):
            ew = [("fv", bar[0]), ("fs", bar[1])] if pair < 2 else ()
            self.pair_turn1(pair, extra_waits=ew)
            nft = self.pair_stage2(pair, x_prev)
            par = pair % 2
            cm = c.cmask[0:72, :].unsqueeze(1).broadcast_to([72, 2, 256])
            waits = [("ft", nft)]
            if par in kq_guard:
                waits.append(kq_guard[par])
            pXre, pXim = self.pb["pXre"], self.pb["pXim"]
            P.add("v", "tt",
                  b.Ksb[0:72, par * 1024: par * 1024 + 512].rearrange("p (l f) -> p l f", l=2),
                  pXre[par][0:72, 0:512].rearrange("p (l f) -> p l f", l=2), cm, ALU.mult,
                  waits=waits)
            P.add("v", "tt",
                  b.Ksb[0:72, par * 1024 + 512: par * 1024 + 1024].rearrange("p (l f) -> p l f", l=2),
                  pXim[par][0:72, 0:512].rearrange("p (l f) -> p l f", l=2), cm, ALU.mult,
                  incs=[("fv", 1)])
            x_prev[par] = ("fv", P.n("fv"))
            ks = "kq%d" % par
            P.add("q", "dma", kout[pair], b.Ksb[0:72, par * 1024: par * 1024 + 1024],
                  waits=[("fv", P.n("fv"))], incs=[(ks, 16)])
            kq_guard[par] = (ks, P.n(ks))

    def emit_kv(self, kv_tiles, kin, ych_ap, k_extra_wait=()):
        P, b, c = self.P, self.b, self.c
        pXre, pXim = self.pb["pXre"], self.pb["pXim"]
        pA, pB = self.pb["pA"], self.pb["pB"]
        self.strided_tr(kv_tiles, 2)
        bar = self.stage1(2)
        x_prev = {}
        z_done = {}
        u_prev = None
        usb_turn = {}
        ustage_prev = {}
        for pair in range(16):
            par = pair % 2
            kw = list(k_extra_wait)
            if par in z_done:
                kw.append(z_done[par])
            P.add("q", "dma", b.Kbuf[0:72, par * 1024: par * 1024 + 1024],
                  kin[pair], waits=kw, incs=[("kq", 16)])
            nkq = P.n("kq")
            ew = [("fv", bar[0]), ("fs", bar[1])] if pair < 2 else ()
            self.pair_turn1(pair, extra_waits=ew)
            nft = self.pair_stage2(pair, x_prev)
            sw = [("ft", nft)]
            if par in z_done:
                sw.append(z_done[par])
            P.add("s", "copy", b.tmp[0:72, par * 1024: par * 1024 + 512],
                  pXre[par][0:72, 0:512], waits=sw)
            P.add("s", "copy", b.tmp[0:72, par * 1024 + 512: par * 1024 + 1024],
                  pXim[par][0:72, 0:512], incs=[("fs", 1)])
            nfs = P.n("fs")
            x_prev[par] = ("fs", nfs)
            K = lambda im: b.Kbuf[0:72, par * 1024 + im * 512: par * 1024 + im * 512 + 512]
            Z = lambda im: b.Z[0:72, par * 1024 + im * 512: par * 1024 + im * 512 + 512]
            tre = b.tmp[0:72, par * 1024: par * 1024 + 512]
            tim = b.tmp[0:72, par * 1024 + 512: par * 1024 + 1024]
            zA, zB = b.tmp2[0:72, 0:512], b.tmp2[0:72, 512:1024]
            vw = [("fs", nfs), ("kq", nkq)]
            if par in ustage_prev:
                vw.append(ustage_prev[par])
            P.add("v", "tt", zA, tre, K(0), ALU.mult, waits=vw)
            P.add("v", "tt", zB, tim, K(1), ALU.mult)
            P.add("v", "tt", Z(0), zA, zB, ALU.subtract)
            P.add("v", "tt", zA, tre, K(1), ALU.mult)
            P.add("v", "tt", zB, tim, K(0), ALU.mult)
            P.add("v", "tt", Z(1), zA, zB, ALU.add, incs=[("fv", 1)])
            z_done[par] = ("fv", P.n("fv"))
            uw = [z_done[par]]
            if u_prev is not None:
                uw.append(u_prev)
            P.add("t", "mm", pA[:, 0:512], c.BDAre[0:72, 0:128], Z(0), True, False, waits=uw)
            P.add("t", "mm", pA[:, 0:512], c.BDAim_neg[0:72, 0:128], Z(1), False, True)
            P.add("t", "mm", pB[:, 0:512], c.BDAim[0:72, 0:128], Z(0), True, False)
            P.add("t", "mm", pB[:, 0:512], c.BDAre[0:72, 0:128], Z(1), False, True,
                  incs=[("ft", 1)])
            ustage_prev[par] = ("ft", P.n("ft"))
            sw = [("ft", P.n("ft"))]
            if par in usb_turn:
                sw.append(usb_turn[par])
            P.add("s", "copy", b.Usb[:, par * 1024: par * 1024 + 512], pA[:, 0:512],
                  waits=sw)
            P.add("s", "act", b.Usb[:, par * 1024 + 512: par * 1024 + 1024], pB[:, 0:512],
                  ACTF.Copy, 0.0, -1.0, incs=[("fs", 1)])
            u_prev = ("fs", P.n("fs"))
            trs = []
            for ac in range(2):
                for im in range(2):
                    for lg in range(2):
                        off = par * 1024 + im * 512 + lg * 256 + ac * 128
                        trs.append(b.Usb[:, off: off + 128])
            g0 = pair * 2
            dest = b.UT[:].rearrange("p (aci gg f) -> p aci gg f", aci=4, gg=32)[:, :, g0:g0 + 2, :]
            self.tr_group(trs, dest, "v" if par == 0 else "s", extra_waits=[u_prev])
            usb_turn[par] = ("ft", P.n("ft"))
        bar2 = (("fv", P.n("fv")), ("fs", P.n("fs")))
        py_prev = {}
        tb_done = {}
        for nu in range(16):
            par = nu % 2
            bank = pXre[par]
            w = [bar2[0], bar2[1]] if nu < 2 else []
            if par in py_prev:
                w.append(py_prev[par])
            if nu < 2 and par in x_prev:
                w.append(x_prev[par])
            first = True
            for ac in range(2):
                hoff = (nu * 2 + ac) * 128
                UTr = b.UT[:, 0 * 4096 + 0:].rearrange("p (aci gg f) -> p aci gg f", aci=4, gg=32)
                base_re = (ac * 2 + 0) * 4096
                base_im = (ac * 2 + 1) * 4096
                UTre = b.UT[:, base_re:base_re + 4096].rearrange(
                    "p (g c2 n2) -> p g c2 n2", c2=8, n2=16)[:, :, :, nu]
                UTim = b.UT[:, base_im:base_im + 4096].rearrange(
                    "p (g c2 n2) -> p g c2 n2", c2=8, n2=16)[:, :, :, nu]
                P.add("t", "mm", bank[:, 0:256], self.c.Hre[:, hoff:hoff + 128], UTre,
                      first, False, waits=w if first else ())
                first = False
                P.add("t", "mm", bank[:, 0:256], self.c.Him[:, hoff:hoff + 128], UTim,
                      False, ac == 1, incs=[("ft", 1)] if ac == 1 else ())
            cons = "v" if par == 0 else "s"
            cn = "fv" if par == 0 else "fs"
            yw = [("ft", P.n("ft"))]
            if par in tb_done:
                yw.append(tb_done[par])
            P.add(cons, "copy", self.b.ysb[:, par * 256: par * 256 + 256], bank[:, 0:256],
                  waits=yw, incs=[(cn, 1)])
            py_prev[par] = (cn, P.n(cn))
            trs = [self.b.ysb[:, par * 256 + cq * 128: par * 256 + (cq + 1) * 128]
                   for cq in range(2)]
            dest = ych_ap.rearrange("p (cq t) -> p cq t", cq=2)[:, :, nu:2048:16]
            self.tr_group(trs, dest, "v" if par == 0 else "s",
                          extra_waits=[py_prev[par]])
            tb_done[par] = ("ft", P.n("ft"))



def _build(debug=()):
    nc = bass.Bass("TRN2", target_bir_lowering=False)

    din = {}
    def DI(name, shape, dt=F32):
        din[name] = nc.dram_tensor(name, list(shape), dt, kind="ExternalInput")
        return din[name]

    u_own = DI("u_own", [TOK, D])
    u_ctx = DI("u_ctx", [TOK, D])
    uT_own = DI("uT_own", [D, TOK])
    ctxm = DI("ctxm", [128, 1])
    for w in ("WqT", "WkT", "WvT", "WoT"):
        DI(w, [D, D])
    for bname in ("bq", "bk", "bv", "bo", "cbq", "cbk", "dskip"):
        DI(bname, [128, 16])
    DI("cwq", [128, 48])
    DI("cwk", [128, 48])
    DI("ker_sl", [256, D])
    DI("gw_s", [1024, D])
    DI("uw_s", [1024, D])
    DI("dw_s", [256, I_FF])
    fc = _fft_constants()
    for name, arr in fc.items():
        DI("c_" + name, arr.shape, BF16 if arr.dtype == BF else F32)

    outT = nc.dram_tensor("outT", [D, TOK], F32, kind="ExternalOutput")

    wbf = {w: nc.dram_tensor("bf_" + w, [D, D], BF16, kind="Internal")
           for w in ("WqT", "WkT", "WvT", "WoT")}
    kv_scr = nc.dram_tensor("kv_scr", [D, WIN], BF16, kind="Internal")
    q_scr = nc.dram_tensor("q_scr", [D, TOK], BF16, kind="Internal")
    y_scr = nc.dram_tensor("y_scr", [D, TOK], BF16, kind="Internal")
    attn_scr = nc.dram_tensor("attn_scr", [D, TOK], BF16, kind="Internal")
    xdq_scr = nc.dram_tensor("xdq_scr", [D, TOK], BF16, kind="Internal")
    xn_scr = nc.dram_tensor("xn_scr", [TOK, D], BF16, kind="Internal")
    h_scr = nc.dram_tensor("h_scr", [I_FF, TOK], BF16, kind="Internal")
    ccK_in = nc.dram_tensor("ccK_in", [16, 72, 1024], BF16, kind="Internal")
    ccK_out = nc.dram_tensor("ccK_out", [8, 16, 72, 1024], BF16, kind="Internal")
    ccW_in = nc.dram_tensor("ccW_in", [6144, 1024], BF16, kind="Internal")
    ccW_out = nc.dram_tensor("ccW_out", [8, 6144, 1024], BF16, kind="Internal")
    dbg = {}
    for name in debug:
        scr = dict(kv_scr=kv_scr, q_scr=q_scr, y_scr=y_scr, attn_scr=attn_scr,
                   xdq_scr=xdq_scr, h_scr=h_scr, xn_scr=xn_scr, ccW_in=ccW_in,
                   ccK_in=ccK_in)[name]
        dbg[name] = (scr, nc.dram_tensor("dbg_" + name, list(scr.shape), BF16,
                                         kind="ExternalOutput"))

    with ExitStack() as persist:
        ent = persist.enter_context

        class _Consts:
            pass
        c = _Consts()
        for name in fc:
            tgt = "ident" if name == "identity" else name
            setattr(c, tgt, ent(nc.sbuf_tensor("k_" + tgt, list(fc[name].shape),
                                               BF16 if fc[name].dtype == BF else F32)))
        biases = ent(nc.sbuf_tensor("biases", [128, 16 * 7], F32))
        convw = ent(nc.sbuf_tensor("convw", [128, 16 * 6], F32))
        maskv = ent(nc.sbuf_tensor("maskv", [128, 1], F32))
        epsv = ent(nc.sbuf_tensor("epsv", [128, 1], F32))
        bm = ent(nc.sbuf_tensor("bm", [128, 16 * 3], F32))

        def bias_ap(idx, ecn):
            return biases[:, idx * 16 + ecn: idx * 16 + ecn + 1]

        def convw_ap(which, ecn, j):
            return convw[:, which * 48 + ecn * 3 + j: which * 48 + ecn * 3 + j + 1]

        pA = ent(nc.psum_tensor("pA", [128, 512], F32))
        pB = ent(nc.psum_tensor("pB", [128, 512], F32))
        ptr0 = ent(nc.psum_tensor("ptr0", [128, 1024], BF16))
        ptr1 = ent(nc.psum_tensor("ptr1", [128, 1024], BF16))
        pC = ent(nc.psum_tensor("pC", [128, 512], F32))
        pD = ent(nc.psum_tensor("pD", [128, 512], F32))
        pE = ent(nc.psum_tensor("pE", [128, 512], F32))
        pF = ent(nc.psum_tensor("pF", [128, 512], F32))
        pb = {"pA": pA, "pB": pB, "ptr": [ptr0, ptr1],
              "pXre": [pC, pD], "pXim": [pE, pF]}

        P = _Plan(nc)

        for name in fc:
            tgt = "ident" if name == "identity" else name
            rows = fc[name].shape[0]
            P.add("q", "dma", getattr(c, tgt)[0:rows, :], din["c_" + name][:],
                  incs=[("ld", 16)])
        for i, bname in enumerate(("bq", "bk", "bv", "bo", "cbq", "cbk", "dskip")):
            P.add("q", "dma", biases[:, i * 16:(i + 1) * 16], din[bname][:],
                  incs=[("ld", 16)])
        P.add("q", "dma", convw[:, 0:48], din["cwq"][:], incs=[("ld", 16)])
        P.add("q", "dma", convw[:, 48:96], din["cwk"][:], incs=[("ld", 16)])
        P.add("q", "dma", maskv[:], ctxm[:], incs=[("ld", 16)])
        n_const_ld = P.n("ld")
        P.add("v", "memset", epsv[:], 1e-5)
        cv = _Conv(P, None, c, pb)
        for w in ("WqT", "WkT", "WvT", "WoT"):
            P.add("g", "dma", wbf[w][:], din[w][:], incs=[("gd", 16)])
        n_wcast = P.n("gd")
        P.add("v", "ts", bm[:, 0:48], biases[:, 0:48], maskv[:, 0:1], None, ALU.mult, None,
              waits=[("ld", n_const_ld)], incs=[("fv", 1)])

        with ExitStack() as sc3:
            e3 = sc3.enter_context
            ubuf = e3(nc.sbuf_tensor("ubuf", [128, 2 * 2048], F32))
            xsq = e3(nc.sbuf_tensor("xsq", [128, 2048], F32))
            xnf = e3(nc.sbuf_tensor("xnf", [128, 2048], F32))
            cscr = e3(nc.sbuf_tensor("cscr", [128, 2 * 1024], F32))
            xnb = e3(nc.sbuf_tensor("xnb", [128, 2 * 2048], BF16))
            stats = e3(nc.sbuf_tensor("stats", [128, 32], F32))
            lnsc = e3(nc.sbuf_tensor("lnsc", [128, 8], F32))
            xnT = e3(nc.sbuf_tensor("xnT", [128, 16 * 1024], BF16))
            wslab = e3(nc.sbuf_tensor("wslab", [128, 2 * 2048], BF16))
            kraw = e3(nc.sbuf_tensor("kraw", [128, 2 * 1028], BF16))
            vraw = e3(nc.sbuf_tensor("vraw", [128, 2 * 1024], BF16))
            qraw = e3(nc.sbuf_tensor("qraw", [128, 2 * 1028], BF16))
            ktail = e3(nc.sbuf_tensor("ktail", [128, 16 * 4], BF16))
            qtail = e3(nc.sbuf_tensor("qtail", [128, 16 * 4], BF16))
            kvec = e3(nc.sbuf_tensor("kvec", [128, 2 * 1024], BF16))
            qcec = e3(nc.sbuf_tensor("qcec", [128, 2 * 1024], BF16))
            xq8 = e3(nc.sbuf_tensor("xq8", [128, 2048], FP8))
            xdq = e3(nc.sbuf_tensor("xdq", [128, 2048], BF16))
            xsc = e3(nc.sbuf_tensor("xsc", [128, 4 * 64], F32))

            uload_done = {}
            xstage_guard = {}
            xn_done = {}
            wsl_done = {}
            conv_done_par = {}
            kv_dma_par = {}
            qc_dma_par = {}
            pp_evac = {}
            pp_k = 0

            def ln_tile(gi, own):
                par = gi % 2
                src = u_own if own else u_ctx
                row0 = (gi * 128) % 2048
                lw = []
                if par in uload_done:
                    lw.append(uload_done[par])
                P.add("q", "dma", ubuf[:, par * 2048:(par + 1) * 2048],
                      src[row0:row0 + 128, :], waits=lw, incs=[("lu%d" % par, 16)])
                u_ap = ubuf[:, par * 2048:(par + 1) * 2048]
                nld_u = P.n("lu%d" % par)
                vw = [("lu%d" % par, nld_u)]
                if gi > 0:
                    vw.append(xn_done[gi - 1])
                P.add("s", "act", xsq[:], u_ap, ACTF.Square, 0.0, 1.0,
                      waits=[("lu%d" % par, nld_u)] + ([xn_done[gi - 1]] if gi > 0 else []),
                      incs=[("fs", 1)])
                nfs_sq = P.n("fs")
                P.add("v", "red", stats[:, 0:1], u_ap.unsqueeze(1), ALU.add, False,
                      waits=vw)
                P.add("v", "red", stats[:, 1:2], xsq[:].unsqueeze(1), ALU.add, False,
                      waits=[("fs", nfs_sq)], incs=[("fv", 1)])
                P.add("v", "ts", stats[:, 2:3], stats[:, 0:1], 1.0 / 2048.0, None,
                      ALU.mult, None, waits=[("fv", P.n("fv"))], incs=[("fv", 1)])
                P.add("v", "tt", stats[:, 3:4], stats[:, 2:3], stats[:, 2:3], ALU.mult,
                      waits=[("fv", P.n("fv"))], incs=[("fv", 1)])
                P.add("v", "ts", stats[:, 4:5], stats[:, 1:2], 1.0 / 2048.0,
                      stats[:, 3:4], ALU.mult, ALU.subtract,
                      waits=[("fv", P.n("fv"))], incs=[("fv", 1)])
                P.add("s", "act", lnsc[:, 0:1], stats[:, 4:5], ACTF.Ln, epsv[:, 0:1], 1.0,
                      waits=[("fv", P.n("fv"))])
                P.add("s", "act", lnsc[:, 1:2], lnsc[:, 0:1], ACTF.Exp, 0.0, -0.5,
                      incs=[("fs", 1)])
                P.add("v", "ts", xnf[:], u_ap, stats[:, 2:3], lnsc[:, 1:2],
                      ALU.subtract, ALU.mult, waits=[("fs", P.n("fs"))], incs=[("fv", 1)])
                uload_done[par] = ("fv", P.n("fv"))
                P.add("v", "copy", xnb[:, par * 2048:(par + 1) * 2048], xnf[:],
                      incs=[("fv", 1)])
                xn_done[gi] = ("fv", P.n("fv"))
                if own and _DEBUG:
                    P.add("q", "dma", xn_scr[(gi * 128) % 2048:(gi * 128) % 2048 + 128, :],
                          xnb[:, par * 2048:(par + 1) * 2048], waits=[xn_done[gi]],
                          incs=[("xs", 16)])
                if own:
                    row0o = row0
                    P.add("v", "red", xsc[:, 0:64], xnf[:].rearrange("p (b k) -> p b k", k=32),
                          ALU.max, True, incs=[("fv", 1)])
                    P.add("v", "ts", xsc[:, 64:128], xsc[:, 0:64], 1.0 / 448.0, 1e-30,
                          ALU.mult, ALU.max, waits=[("fv", P.n("fv"))], incs=[("fv", 1)])
                    P.add("v", "recip", xsc[:, 128:192], xsc[:, 64:128],
                          waits=[("fv", P.n("fv"))], incs=[("fv", 1)])
                    P.add("v", "ts", xsc[:, 192:256], xsc[:, 64:128], 2.0, None, ALU.mult, None,
                          waits=[("fv", P.n("fv"))], incs=[("fv", 1)])
                    P.add("v", "stt", xq8[:].rearrange("p (b k) -> p b k", k=32),
                          xnf[:].rearrange("p (b k) -> p b k", k=32), 0.5,
                          xsc[:, 128:192].unsqueeze(-1).broadcast_to([128, 64, 32]),
                          ALU.mult, ALU.mult, waits=[("fv", P.n("fv"))])
                    P.add("v", "tt", xdq[:].rearrange("p (b k) -> p b k", k=32),
                          xq8[:].rearrange("p (b k) -> p b k", k=32),
                          xsc[:, 192:256].unsqueeze(-1).broadcast_to([128, 64, 32]),
                          ALU.mult, incs=[("fv", 1)])
                    nq = ("fv", P.n("fv"))
                    for hc4 in range(4):
                        gslot = cv.ptr_grp % 2
                        stg = wstage_x[:, gslot * 512:(gslot + 1) * 512]
                        trs = [xdq[:, (hc4 * 4 + i2) * 128:(hc4 * 4 + i2 + 1) * 128]
                               for i2 in range(4)]
                        xsem = "xd%d" % gslot
                        ee = [(xsem, xstage_guard[gslot])] if gslot in xstage_guard else []
                        cv.tr_group(trs, stg, "s", extra_waits=[nq], evac_extra=ee)
                        dest = xdq_scr[hc4 * 512:(hc4 + 1) * 512, row0o:row0o + 128].rearrange(
                            "(b p) t -> p b t", p=128)
                        P.add("q", "dma", dest, stg.rearrange("p (b f) -> p b f", b=4),
                              waits=[cv.ptr_evac[cv.ptr_grp - 1]], incs=[(xsem, 16)])
                        xstage_guard[gslot] = P.n(xsem)

            wstage_x = e3(nc.sbuf_tensor("wstage_x", [128, 1024], BF16))

            def xn_tr(gi, gb):
                par = gi % 2
                src = xnb[:, par * 2048:(par + 1) * 2048]
                for hc4 in range(4):
                    trs = [src[:, (hc4 * 4 + i2) * 128:(hc4 * 4 + i2 + 1) * 128]
                           for i2 in range(4)]
                    dest = xnT[:].rearrange("p (hc t) -> p hc t", hc=16)[
                        :, hc4 * 4:(hc4 + 1) * 4, gb * 128:(gb + 1) * 128]
                    ew = [xn_done[gi]]
                    if gi == 0 and hc4 == 0:
                        ew.append(("ld", n_const_ld))
                    cv.tr_group(trs, dest, "v" if hc4 % 2 == 0 else "s",
                                extra_waits=ew)

            qtsave = e3(nc.sbuf_tensor("qtsave", [128, 32], BF16))
            qmini_prev = None
            qts_bar = None
            for grp in range(4):
                own_grp = grp >= 2
                for gb in range(8):
                    gi = grp * 8 + gb
                    ln_tile(gi, own_grp)
                    xn_tr(gi, gb)
                xnt_bar = (("fv", P.n("fv")), ("fs", P.n("fs")))
                if grp == 1:
                    P.add("v", "copy", qtsave[:].rearrange("p (hc t) -> p hc t", t=2),
                          xnT[:].rearrange("p (hc t) -> p hc t", hc=16)[:, :, 1022:1024],
                          waits=[xnt_bar[0], xnt_bar[1]], incs=[("fv", 1)])
                    qts_bar = ("fv", P.n("fv"))
                projs = ([("WqT", 0, qraw, True)] if own_grp else []) + \
                    [("WkT", 1, kraw, True), ("WvT", 2, vraw, False)]
                for ec in range(16):
                    raw_done = {}
                    rpar = ec % 2
                    for wname, bidx, rawbuf, haspad in projs:
                        wpar = pp_k % 2
                        lw = []
                        if wpar in wsl_done:
                            lw.append(wsl_done[wpar])
                        if grp == 0 and ec == 0 and wname == projs[0][0]:
                            lw = lw + [("gd", n_wcast)]
                        P.add("q", "dma", wslab[:, wpar * 2048:(wpar + 1) * 2048].rearrange(
                            "p (hc f) -> p hc f", hc=16),
                            wbf[wname][:, ec * 128:(ec + 1) * 128].rearrange(
                            "(hc p) f -> p hc f", p=128), waits=lw, incs=[("ls%d" % wpar, 16)])
                        nld = P.n("ls%d" % wpar)
                        pad = 4 if haspad else 0
                        rw = rawbuf[:, rpar * (1024 + pad): rpar * (1024 + pad) + 1024 + pad]
                        for blk2 in range(2):
                            bank = [pA, pB, pC, pD][pp_k % 4]
                            cons = "v" if pp_k % 2 == 0 else "s"
                            waits = [("ls%d" % wpar, nld)]
                            if ec == 0 and wname == projs[0][0] and blk2 == 0:
                                waits += [xnt_bar[0], xnt_bar[1]]
                            if pp_k - 4 in pp_evac:
                                waits.append(pp_evac[pp_k - 4])
                            for hc in range(16):
                                P.add("t", "mm", bank[:, 0:512],
                                      wslab[:, wpar * 2048 + hc * 128: wpar * 2048 + (hc + 1) * 128],
                                      xnT[:].rearrange("p (hc t) -> p hc t", hc=16)[
                                          :, hc, blk2 * 512:(blk2 + 1) * 512],
                                      hc == 0, hc == 15,
                                      waits=waits if hc == 0 else (),
                                      incs=[("ft", 1)] if hc == 15 else ())
                            dst = rw[:, pad + blk2 * 512: pad + (blk2 + 1) * 512]
                            cn = "fv" if cons == "v" else "fs"
                            evw = [("ft", P.n("ft"))]
                            if rpar in conv_done_par:
                                evw.append(conv_done_par[rpar])
                            if own_grp:
                                if cons == "v":
                                    P.add(cons, "ts", dst, bank[:, 0:512], 1.0,
                                          bias_ap(bidx, ec), ALU.mult, ALU.add,
                                          waits=evw, incs=[(cn, 1)])
                                else:
                                    P.add(cons, "act", dst, bank[:, 0:512], ACTF.Identity,
                                          bias_ap(bidx, ec), 1.0,
                                          waits=evw, incs=[(cn, 1)])
                            else:
                                if cons == "v":
                                    P.add(cons, "ts", dst, bank[:, 0:512], maskv[:, 0:1],
                                          bm[:, bidx * 16 + ec: bidx * 16 + ec + 1],
                                          ALU.mult, ALU.add,
                                          waits=evw, incs=[(cn, 1)])
                                else:
                                    P.add(cons, "act", dst, bank[:, 0:512], ACTF.Identity,
                                          bm[:, bidx * 16 + ec: bidx * 16 + ec + 1],
                                          maskv[:, 0:1],
                                          waits=evw, incs=[(cn, 1)])
                            pp_evac[pp_k] = (cn, P.n(cn))
                            raw_done[wname] = (cn, P.n(cn))
                            pp_k += 1
                        if wname == "WqT" and grp == 2:
                            mw = [qts_bar]
                            if qmini_prev is not None:
                                mw.append(qmini_prev)
                            for hc in range(16):
                                P.add("t", "mm", pb["pXim"][0][:, 0:2],
                                      wslab[:, wpar * 2048 + hc * 128: wpar * 2048 + (hc + 1) * 128],
                                      qtsave[:, hc * 2:(hc + 1) * 2],
                                      hc == 0, hc == 15,
                                      waits=mw if hc == 0 else (),
                                      incs=[("ft", 1)] if hc == 15 else ())
                            qw0 = qraw[:, rpar * 1028: rpar * 1028 + 1028]
                            P.add("v", "ts", qw0[:, 2:4], pb["pXim"][0][:, 0:2],
                                  maskv[:, 0:1], bm[:, 0 * 16 + ec: 0 * 16 + ec + 1],
                                  ALU.mult, ALU.add,
                                  waits=[("ft", P.n("ft"))], incs=[("fv", 1)])
                            qmini_prev = ("fv", P.n("fv"))
                        wsl_done[wpar] = ("ft", P.n("ft"))
                    kw_ = kraw[:, rpar * 1028: rpar * 1028 + 1028]
                    vw_ = vraw[:, rpar * 1024: rpar * 1024 + 1024]
                    tw = [raw_done["WkT"], raw_done["WvT"]]
                    if own_grp:
                        tw.append(raw_done["WqT"])
                    if grp == 0:
                        P.add("v", "memset", kw_[:, 0:4], 0.0, waits=tw)
                    else:
                        P.add("v", "copy", kw_[:, 0:4], ktail[:, ec * 4:(ec + 1) * 4], waits=tw)
                    P.add("v", "ts", cscr[:, 0:1024], kw_[:, 2:1026], convw_ap(1, ec, 0),
                          bias_ap(5, ec), ALU.mult, ALU.add)
                    P.add("v", "stt", cscr[:, 1024:2048], kw_[:, 3:1027], convw_ap(1, ec, 1),
                          cscr[:, 0:1024], ALU.mult, ALU.add)
                    P.add("v", "stt", cscr[:, 0:1024], kw_[:, 4:1028], convw_ap(1, ec, 2),
                          cscr[:, 1024:2048], ALU.mult, ALU.add)
                    P.add("v", "copy", ktail[:, ec * 4:(ec + 1) * 4], kw_[:, 1024:1028])
                    kvw2 = [("kv%d" % rpar, kv_dma_par[rpar])] if rpar in kv_dma_par else []
                    P.add("v", "tt", kvec[:, rpar * 1024:(rpar + 1) * 1024],
                          cscr[:, 0:1024], vw_, ALU.mult, waits=kvw2, incs=[("fv", 1)])
                    kv_done = ("fv", P.n("fv"))
                    P.add("q", "dma",
                          kv_scr[ec * 128:(ec + 1) * 128, grp * 1024:(grp + 1) * 1024],
                          kvec[:, rpar * 1024:(rpar + 1) * 1024],
                          waits=[kv_done], incs=[("kv%d" % rpar, 16)])
                    kv_dma_par[rpar] = P.n("kv%d" % rpar)
                    if own_grp:
                        qw_ = qraw[:, rpar * 1028: rpar * 1028 + 1028]
                        if grp != 2:
                            P.add("v", "copy", qw_[:, 0:4], qtail[:, ec * 4:(ec + 1) * 4])
                        P.add("v", "ts", cscr[:, 0:1024], qw_[:, 2:1026], convw_ap(0, ec, 0),
                              bias_ap(4, ec), ALU.mult, ALU.add)
                        P.add("v", "stt", cscr[:, 1024:2048], qw_[:, 3:1027], convw_ap(0, ec, 1),
                              cscr[:, 0:1024], ALU.mult, ALU.add)
                        qcw2 = [("qc%d" % rpar, qc_dma_par[rpar])] if rpar in qc_dma_par else []
                        P.add("v", "stt", qcec[:, rpar * 1024:(rpar + 1) * 1024],
                              qw_[:, 4:1028], convw_ap(0, ec, 2),
                              cscr[:, 1024:2048], ALU.mult, ALU.add,
                              waits=qcw2, incs=[("fv", 1)])
                        P.add("v", "copy", qtail[:, ec * 4:(ec + 1) * 4], qw_[:, 1024:1028],
                              incs=[("fv", 1)])
                        P.add("q", "dma",
                              q_scr[ec * 128:(ec + 1) * 128, (grp - 2) * 1024:(grp - 1) * 1024],
                              qcec[:, rpar * 1024:(rpar + 1) * 1024],
                              waits=[("fv", P.n("fv"))], incs=[("qc%d" % rpar, 16)])
                        qc_dma_par[rpar] = P.n("qc%d" % rpar)
                    conv_done_par[rpar] = ("fv", P.n("fv"))
            kv_all = [(n, P.n(n)) for n in ("kv0", "kv1", "qc0", "qc1")]

        with ExitStack() as sc1:
            e1 = sc1.enter_context
            kert = [e1(nc.sbuf_tensor(f"kert{i}", [128, 2048], BF16)) for i in range(2)]
            p3_bar = [("ft", P.n("ft")), ("fv", P.n("fv")), ("fs", P.n("fs"))] + \
                list(kv_all) + [("xd0", P.n("xd0")), ("xd1", P.n("xd1"))]
            for i in range(2):
                P.add("g", "dma", kert[i][:], din["ker_sl"][i * 128:(i + 1) * 128, :],
                      waits=p3_bar if i == 0 else (), incs=[("gk", 16)])
            n_kert = P.n("gk")

            bufsA = _ConvBufs(nc, sc1)
            cv.b = bufsA
            cv.emit_ker(kert, ccK_in[:], extra_first=[("gk", n_kert), ("ld", n_const_ld)])
            nkq_ker = (P.n("kq0"), P.n("kq1"))

            wtile = e1(nc.sbuf_tensor("wtile", [128, 2 * 2048], F32))
            wq8 = e1(nc.sbuf_tensor("wq8", [128, 2048], FP8))
            wdq = e1(nc.sbuf_tensor("wdq", [128, 2048], BF16))
            wsc = e1(nc.sbuf_tensor("wsc", [128, 4 * 64], F32))
            wstage = e1(nc.sbuf_tensor("wstage", [128, 1024], BF16))

            def wtile_src(j):
                if j < 8:
                    return din["gw_s"][j * 128:(j + 1) * 128, :]
                if j < 16:
                    return din["uw_s"][(j - 8) * 128:(j - 8 + 1) * 128, :]
                jj = j - 16
                r, qq = jj // 4, jj % 4
                return din["dw_s"][r * 128:(r + 1) * 128, qq * 2048:(qq + 1) * 2048]

            def wtile_dst(j, hc4, stg):
                if j < 16:
                    base = 0 if j < 8 else 2048
                    col = (j % 8) * 128
                    rows = ccW_in[base + hc4 * 512: base + (hc4 + 1) * 512, col:col + 128]
                    return rows.rearrange("(b p) i -> p b i", p=128)
                jj = j - 16
                r, qq = jj // 4, jj % 4
                rows = ccW_in[4096 + hc4 * 512: 4096 + (hc4 + 1) * 512,
                              qq * 256 + r * 128: qq * 256 + (r + 1) * 128]
                return rows.rearrange("(b p) e -> p b e", p=128)

            wq_done = {}
            wstage_guard = {}
            wt_prev = None
            for j in range(24):
                par = j % 2
                lw = []
                if j == 0:
                    lw += list(kv_all) + [("xd0", P.n("xd0")), ("xd1", P.n("xd1"))]
                if par in wq_done:
                    lw.append(wq_done[par])
                P.add("q", "dma", wtile[:, par * 2048:(par + 1) * 2048], wtile_src(j),
                      waits=lw, incs=[("lwt%d" % par, 16)])
                nld = P.n("lwt%d" % par)
                src = wtile[:, par * 2048:(par + 1) * 2048]
                vw = [("lwt%d" % par, nld)]
                if wt_prev is not None:
                    vw.append(wt_prev)
                P.add("v", "red", wsc[:, 0:64], src.rearrange("p (b k) -> p b k", k=32),
                      ALU.max, True, waits=vw, incs=[("fv", 1)])
                P.add("v", "ts", wsc[:, 64:128], wsc[:, 0:64], 1.0 / 448.0, 1e-30,
                      ALU.mult, ALU.max, waits=[("fv", P.n("fv"))], incs=[("fv", 1)])
                P.add("v", "recip", wsc[:, 128:192], wsc[:, 64:128],
                      waits=[("fv", P.n("fv"))], incs=[("fv", 1)])
                P.add("v", "ts", wsc[:, 192:256], wsc[:, 64:128], 2.0, None, ALU.mult, None,
                      waits=[("fv", P.n("fv"))], incs=[("fv", 1)])
                P.add("v", "stt", wq8[:].rearrange("p (b k) -> p b k", k=32),
                      src.rearrange("p (b k) -> p b k", k=32), 0.5,
                      wsc[:, 128:192].unsqueeze(-1).broadcast_to([128, 64, 32]),
                      ALU.mult, ALU.mult, waits=[("fv", P.n("fv"))])
                P.add("v", "tt", wdq[:].rearrange("p (b k) -> p b k", k=32),
                      wq8[:].rearrange("p (b k) -> p b k", k=32),
                      wsc[:, 192:256].unsqueeze(-1).broadcast_to([128, 64, 32]),
                      ALU.mult, incs=[("fv", 1)])
                wq_done[par] = ("fv", P.n("fv"))
                for hc4 in range(4):
                    gslot = cv.ptr_grp % 2
                    stg = wstage[:, gslot * 512:(gslot + 1) * 512]
                    trs = [wdq[:, (hc4 * 4 + i2) * 128:(hc4 * 4 + i2 + 1) * 128]
                           for i2 in range(4)]
                    wsem = "wd%d" % gslot
                    ee = [(wsem, wstage_guard[gslot])] if gslot in wstage_guard else []
                    cv.tr_group(trs, stg, "s", extra_waits=[wq_done[par]], evac_extra=ee)
                    P.add("q", "dma", wtile_dst(j, hc4, stg),
                          stg.rearrange("p (b f) -> p b f", b=4),
                          waits=[cv.ptr_evac[cv.ptr_grp - 1]], incs=[(wsem, 16)])
                    wstage_guard[gslot] = P.n(wsem)
                wt_prev = ("ft", P.n("ft"))
            n_wd = (P.n("wd0"), P.n("wd1"))

        P.add("g", "coll", "AllGather", ALU.bypass, [[0, 1, 2, 3, 4, 5, 6, 7]],
              [ccK_in[:]], [ccK_out[:]],
              waits=[("kq0", nkq_ker[0]), ("kq1", nkq_ker[1])], incs=[("ck", 1)])
        P.add("g", "coll", "AllGather", ALU.bypass, [[0, 1, 2, 3, 4, 5, 6, 7]],
              [ccW_in[:]], [ccW_out[:]],
              waits=[("wd0", n_wd[0]), ("wd1", n_wd[1])], incs=[("cw", 1)])

        with ExitStack() as sc4:
            bufsB = _ConvBufs(nc, sc4)
            cv.b = bufsB
            kvt = [sc4.enter_context(nc.sbuf_tensor(f"kvt{i}", [128, 4096], BF16))
                   for i in range(2)]
            ych = sc4.enter_context(nc.sbuf_tensor("ych", [128, 2 * 2048], BF16))
            for ch in range(8):
                lw = list(kv_all)
                if ch > 0:
                    lw.append(("ft", P.n("ft")))
                else:
                    lw += [("ft", P.n("ft")), ("wd0", n_wd[0]), ("wd1", n_wd[1]),
                           ("kq0", nkq_ker[0]), ("kq1", nkq_ker[1])]
                for i in range(2):
                    P.add("q", "dma", kvt[i][:],
                          kv_scr[(2 * ch + i) * 128:(2 * ch + i + 1) * 128, :],
                          waits=lw if i == 0 else (), incs=[("kt", 16)])
                nld = P.n("kt")
                ef = [("kt", nld)] + ([("st", 16 * ch)] if ch > 0 else [])
                cv.strided_tr(kvt, 2, extra_first=ef)
                _emit_kv_rest(P, cv, ccK_out[ch], ych[:], k_wait=("ck", 1))
                P.add("q", "dma", y_scr[2 * ch * 128:(2 * ch + 2) * 128, :].rearrange(
                    "(cq p) t -> p cq t", cq=2), ych[:],
                    waits=[("fv", P.n("fv")), ("fs", P.n("fs"))], incs=[("st", 16)])
            n_yst = P.n("st")

        with ExitStack() as sc5:
            e5 = sc5.enter_context
            yblk = e5(nc.sbuf_tensor("yblk", [128, 16 * 512], BF16))
            kvob = e5(nc.sbuf_tensor("kvob", [128, 16 * 512], BF16))
            qcb = e5(nc.sbuf_tensor("qcb", [128, 16 * 512], BF16))
            yTb = e5(nc.sbuf_tensor("yTb", [128, 16 * 512], BF16))
            atb = e5(nc.sbuf_tensor("atb", [128, 16 * 512], BF16))
            wslab5 = e5(nc.sbuf_tensor("wslab5", [128, 2 * 2048], BF16))
            cscr5 = e5(nc.sbuf_tensor("cscr5", [128, 512], F32))
            wsl_done5 = {}
            wo_evac = {}
            p5_prev = None
            for blk in range(4):
                t0 = blk * 512
                lw = [("st", n_yst)]
                if p5_prev is not None:
                    lw.append(p5_prev)
                P.add("q", "dma", yblk[:].rearrange("p (hc t) -> p hc t", hc=16),
                      y_scr[:, t0:t0 + 512].rearrange("(hc p) t -> p hc t", p=128),
                      waits=lw, incs=[("l5", 16)])
                P.add("q", "dma", kvob[:].rearrange("p (hc t) -> p hc t", hc=16),
                      kv_scr[:, 2048 + t0: 2048 + t0 + 512].rearrange("(hc p) t -> p hc t", p=128),
                      incs=[("l5", 16)])
                P.add("q", "dma", qcb[:].rearrange("p (hc t) -> p hc t", hc=16),
                      q_scr[:, t0:t0 + 512].rearrange("(hc p) t -> p hc t", p=128),
                      incs=[("l5", 16)])
                nld = P.n("l5")
                for ec in range(16):
                    sl = slice(ec * 512, (ec + 1) * 512)
                    vw2 = [("l5", nld)] if ec == 0 else []
                    if ec == 0 and p5_prev is not None:
                        vw2.append(p5_prev)
                    P.add("v", "stt", cscr5[:, 0:512], kvob[:, sl],
                          bias_ap(6, ec), yblk[:, sl], ALU.mult, ALU.add, waits=vw2)
                    P.add("v", "tt", yTb[:, sl], cscr5[:, 0:512], qcb[:, sl], ALU.mult,
                          incs=[("fv", 1)] if ec == 15 else ())
                ybar = ("fv", P.n("fv"))
                for ec in range(16):
                    wpar = ec % 2
                    lw2 = []
                    if blk == 0 and ec == 0:
                        lw2.append(("gd", n_wcast))
                    if wpar in wsl_done5:
                        lw2.append(wsl_done5[wpar])
                    P.add("q", "dma", wslab5[:, wpar * 2048:(wpar + 1) * 2048].rearrange(
                        "p (hc f) -> p hc f", hc=16),
                        wbf["WoT"][:, ec * 128:(ec + 1) * 128].rearrange(
                        "(hc p) f -> p hc f", p=128), waits=lw2, incs=[("l5w%d" % wpar, 16)])
                    nld2 = P.n("l5w%d" % wpar)
                    bank = pA if ec % 2 == 0 else pB
                    cons = "v" if ec % 2 == 0 else "s"
                    waits = [("l5w%d" % wpar, nld2), ybar]
                    if ec - 2 in wo_evac:
                        waits.append(wo_evac[ec - 2])
                    for hc in range(16):
                        P.add("t", "mm", bank[:, 0:512],
                              wslab5[:, wpar * 2048 + hc * 128: wpar * 2048 + (hc + 1) * 128],
                              yTb[:, hc * 512:(hc + 1) * 512],
                              hc == 0, hc == 15,
                              waits=waits if hc == 0 else (),
                              incs=[("ft", 1)] if hc == 15 else ())
                    cn = "fv" if cons == "v" else "fs"
                    ew2 = [("ft", P.n("ft"))]
                    if p5_prev is not None and ec < 2:
                        ew2.append(p5_prev)
                    if cons == "v":
                        P.add(cons, "ts", atb[:, ec * 512:(ec + 1) * 512], bank[:, 0:512], 1.0,
                              bias_ap(3, ec), ALU.mult, ALU.add,
                              waits=ew2, incs=[(cn, 1)])
                    else:
                        P.add(cons, "act", atb[:, ec * 512:(ec + 1) * 512], bank[:, 0:512],
                              ACTF.Identity, bias_ap(3, ec), 1.0,
                              waits=ew2, incs=[(cn, 1)])
                    wo_evac[ec] = (cn, P.n(cn))
                    wsl_done5[wpar] = ("ft", P.n("ft"))
                wo_evac = {}
                P.add("q", "dma", attn_scr[:, t0:t0 + 512].rearrange("(hc p) t -> p hc t", p=128),
                      atb[:].rearrange("p (hc t) -> p hc t", hc=16),
                      waits=[("fv", P.n("fv")), ("fs", P.n("fs"))], incs=[("at", 16)])
                p5_prev = ("at", P.n("at"))
            n_at = P.n("at")

        with ExitStack() as sc6:
            e6 = sc6.enter_context
            xdqT = e6(nc.sbuf_tensor("xdqT", [128, 16 * 2048], BF16))
            P.add("q", "dma", xdqT[:].rearrange("p (hc t) -> p hc t", hc=16),
                  xdq_scr[:].rearrange("(hc p) t -> p hc t", p=128),
                  waits=[("xd0", P.n("xd0")), ("xd1", P.n("xd1"))], incs=[("lx", 16)])
            n_xdqt = P.n("lx")
            islab = e6(nc.sbuf_tensor("islab", [128, 4 * 8192], BF16))
            hsil = e6(nc.sbuf_tensor("hsil", [128, 512], BF16))
            hf = e6(nc.sbuf_tensor("hf", [128, 512], F32))
            hq8 = e6(nc.sbuf_tensor("hq8", [128, 512], FP8))
            hdq = e6(nc.sbuf_tensor("hdq", [128, 2 * 512], BF16))
            hsc = e6(nc.sbuf_tensor("hsc", [128, 4 * 16], F32))
            hstage = e6(nc.sbuf_tensor("hstage", [128, 1024], BF16))
            isl_done = {}
            hstage_guard = {}
            h_done = {}
            hf_done = {}
            tr_done = {}
            pend_tr = None

            def h_tr(kk, ic, ts_):
                gslot = cv.ptr_grp % 2
                stg = hstage[:, gslot * 512:(gslot + 1) * 512]
                hp = kk % 2
                trs = [hdq[:, hp * 512 + i2 * 128: hp * 512 + (i2 + 1) * 128]
                       for i2 in range(4)]
                hsem = "hd%d" % gslot
                ee = [(hsem, hstage_guard[gslot])] if gslot in hstage_guard else []
                cv.tr_group(trs, stg, "s", extra_waits=[h_done[kk]], evac_extra=ee)
                tr_done[kk] = ("ft", P.n("ft"))
                dest = h_scr[ic * 512:(ic + 1) * 512, ts_ * 128:(ts_ + 1) * 128].rearrange(
                    "(b p) t -> p b t", p=128)
                P.add("q", "dma", dest, stg.rearrange("p (b f) -> p b f", b=4),
                      waits=[cv.ptr_evac[cv.ptr_grp - 1]], incs=[(hsem, 16)])
                hstage_guard[gslot] = P.n(hsem)

            for ic in range(16):
                ipar = ic % 2
                rank, li = ic // 2, (ic % 2) * 512
                lw = [("cw", 1)]
                if ipar in isl_done:
                    lw.append(isl_done[ipar])
                P.add("q", "dma", islab[:, ipar * 16384: ipar * 16384 + 8192].rearrange(
                    "p (hc f) -> p hc f", hc=16),
                    ccW_out[rank, 0:2048, li:li + 512].rearrange("(hc p) f -> p hc f", p=128),
                    waits=lw, incs=[("li%d" % ipar, 16)])
                P.add("q", "dma", islab[:, ipar * 16384 + 8192: ipar * 16384 + 16384].rearrange(
                    "p (hc f) -> p hc f", hc=16),
                    ccW_out[rank, 2048:4096, li:li + 512].rearrange("(hc p) f -> p hc f", p=128),
                    incs=[("li%d" % ipar, 16)])
                nld = P.n("li%d" % ipar)
                for ts_ in range(16):
                    kk = ic * 16 + ts_
                    bank = [pC, pD, pA][kk % 3]
                    bank2 = [pE, pF, pB][kk % 3]
                    waits = [("li%d" % ipar, nld), ("lx", n_xdqt)]
                    if kk < 3:
                        waits.append(("at", n_at))
                    if kk - 3 in h_done:
                        waits.append(h_done[kk - 3])
                    for hc in range(16):
                        P.add("t", "mm", bank[:, 0:512],
                              xdqT[:].rearrange("p (hc t) -> p hc t", hc=16)[
                                  :, hc, ts_ * 128:(ts_ + 1) * 128],
                              islab[:, ipar * 16384 + hc * 512: ipar * 16384 + (hc + 1) * 512],
                              hc == 0, hc == 15,
                              waits=waits if hc == 0 else ())
                    for hc in range(16):
                        P.add("t", "mm", bank2[:, 0:512],
                              xdqT[:].rearrange("p (hc t) -> p hc t", hc=16)[
                                  :, hc, ts_ * 128:(ts_ + 1) * 128],
                              islab[:, ipar * 16384 + 8192 + hc * 512:
                                    ipar * 16384 + 8192 + (hc + 1) * 512],
                              hc == 0, hc == 15,
                              incs=[("ft", 1)] if hc == 15 else ())
                    nftg = P.n("ft")
                    silw = [("ft", nftg)]
                    if kk - 1 in hf_done:
                        silw.append(hf_done[kk - 1])
                    P.add("s", "act", hsil[:], bank[:, 0:512], ACTF.Silu, 0.0, 1.0,
                          waits=silw, incs=[("fs", 1)])
                    hfw = [("fs", P.n("fs"))]
                    if kk - 2 in tr_done:
                        hfw.append(tr_done[kk - 2])
                    P.add("v", "tt", hf[:], hsil[:], bank2[:, 0:512], ALU.mult,
                          waits=hfw, incs=[("fv", 1)])
                    hf_done[kk] = ("fv", P.n("fv"))
                    P.add("v", "red", hsc[:, 0:16], hf[:].rearrange("p (b k) -> p b k", k=32),
                          ALU.max, True, incs=[("fv", 1)])
                    P.add("v", "ts", hsc[:, 16:32], hsc[:, 0:16], 1.0 / 448.0, 1e-30,
                          ALU.mult, ALU.max, waits=[("fv", P.n("fv"))], incs=[("fv", 1)])
                    P.add("v", "recip", hsc[:, 32:48], hsc[:, 16:32],
                          waits=[("fv", P.n("fv"))], incs=[("fv", 1)])
                    P.add("v", "ts", hsc[:, 48:64], hsc[:, 16:32], 2.0, None, ALU.mult, None,
                          waits=[("fv", P.n("fv"))], incs=[("fv", 1)])
                    hp = kk % 2
                    P.add("v", "stt", hq8[:].rearrange("p (b k) -> p b k", k=32),
                          hf[:].rearrange("p (b k) -> p b k", k=32), 0.5,
                          hsc[:, 32:48].unsqueeze(-1).broadcast_to([128, 16, 32]),
                          ALU.mult, ALU.mult, waits=[("fv", P.n("fv"))])
                    P.add("v", "tt",
                          hdq[:, hp * 512:(hp + 1) * 512].rearrange("p (b k) -> p b k", k=32),
                          hq8[:].rearrange("p (b k) -> p b k", k=32),
                          hsc[:, 48:64].unsqueeze(-1).broadcast_to([128, 16, 32]),
                          ALU.mult, incs=[("fv", 1)])
                    h_done[kk] = ("fv", P.n("fv"))
                    if pend_tr is not None:
                        h_tr(*pend_tr)
                    pend_tr = (kk, ic, ts_)
                isl_done[ipar] = ("ft", P.n("ft"))
            h_tr(*pend_tr)
            n_hd = (P.n("hd0"), P.n("hd1"))

        with ExitStack() as sc7:
            e7 = sc7.enter_context
            hslab = e7(nc.sbuf_tensor("hslab", [128, 64 * 512], BF16))
            dwsl = e7(nc.sbuf_tensor("dwsl", [128, 2 * 16384], BF16))
            uTs = e7(nc.sbuf_tensor("uTs", [128, 2 * 512], F32))
            atl = e7(nc.sbuf_tensor("atl", [128, 2 * 512], BF16))
            ofin = e7(nc.sbuf_tensor("ofin", [128, 2 * 512], F32))
            dw_done = {}
            fin_done = {}
            st_par = {}
            ua_done = {}
            for tc in range(4):
                t0 = tc * 512
                lw = [("hd0", n_hd[0]), ("hd1", n_hd[1]), ("at", n_at)]
                if tc > 0:
                    lw.append(("ft", P.n("ft")))
                    lw.append(("fv", P.n("fv")))
                    lw.append(("fs", P.n("fs")))
                P.add("q", "dma", hslab[:].rearrange("p (ic t) -> p ic t", ic=64),
                      h_scr[:, t0:t0 + 512].rearrange("(ic p) t -> p ic t", p=128),
                      waits=lw, incs=[("l6", 16)])
                for ec in range(16):
                    epar = ec % 2
                    erank = ec // 2
                    wpar = erank % 2
                    if epar == 0:
                        lw2 = []
                        if wpar in dw_done:
                            lw2.append(dw_done[wpar])
                        P.add("q", "dma",
                              dwsl[:, wpar * 16384:(wpar + 1) * 16384].rearrange(
                                  "p (icl c) -> p icl c", icl=16),
                              ccW_out[erank, 4096:6144, :].rearrange(
                                  "(icl p) c -> p icl c", p=128),
                              waits=lw2, incs=[("l6w%d" % wpar, 16)])
                    nld2 = P.n("l6w%d" % wpar)
                    uw2 = [ua_done[epar]] if epar in ua_done else []
                    P.add("q", "dma", uTs[:, epar * 512:(epar + 1) * 512],
                          uT_own[ec * 128:(ec + 1) * 128, t0:t0 + 512],
                          waits=uw2, incs=[("l6", 16)])
                    P.add("q", "dma", atl[:, epar * 512:(epar + 1) * 512],
                          attn_scr[ec * 128:(ec + 1) * 128, t0:t0 + 512],
                          incs=[("l6", 16)])
                    nldu = P.n("l6")
                    bank = pA if epar == 0 else pB
                    waits = [("l6", nldu), ("l6w%d" % wpar, nld2)]
                    if epar in fin_done:
                        waits.append(fin_done[epar])
                    first = True
                    for m in range(4):
                        for icl in range(16):
                            off = wpar * 16384 + icl * 1024 + m * 256 + epar * 128
                            ih = m * 16 + icl
                            P.add("t", "mm", bank[:, 0:512], dwsl[:, off: off + 128],
                                  hslab[:, ih * 512:(ih + 1) * 512],
                                  first, (m == 3 and icl == 15),
                                  waits=waits if first else (),
                                  incs=[("ft", 1)] if (m == 3 and icl == 15) else ())
                            first = False
                    of = ofin[:, epar * 512:(epar + 1) * 512]
                    ow = [("ft", P.n("ft"))]
                    if epar in st_par:
                        ow.append(("so%d" % epar, st_par[epar]))
                    P.add("v", "tt", of, bank[:, 0:512], atl[:, epar * 512:(epar + 1) * 512],
                          ALU.add, waits=ow)
                    P.add("v", "tt", of, of, uTs[:, epar * 512:(epar + 1) * 512], ALU.add,
                          incs=[("fv", 1)])
                    fin_done[epar] = ("fv", P.n("fv"))
                    ua_done[epar] = ("fv", P.n("fv"))
                    P.add("q", "dma", outT[ec * 128:(ec + 1) * 128, t0:t0 + 512], of,
                          waits=[fin_done[epar]], incs=[("so%d" % epar, 16)])
                    st_par[epar] = P.n("so%d" % epar)
                    if epar == 1:
                        dw_done[wpar] = ("ft", P.n("ft"))

        P.add("g", "memset", biases[0:1, 0:8], 0.0,
              waits=[("so0", P.n("so0")), ("so1", P.n("so1")), ("st", P.n("st"))])
        for name, (scr, dout) in dbg.items():
            P.add("g", "dma", dout[:], scr[:], incs=[("db", 16)])
        if dbg:
            P.add("g", "memset", biases[0:1, 8:16], 0.0, waits=[("db", P.n("db"))])

        _run_block(nc, P)
    return nc


def _emit_kv_rest(P, cv, kin, ych_ap, k_wait):
    b, c = cv.b, cv.c
    pXre, pXim = cv.pb["pXre"], cv.pb["pXim"]
    pA, pB = cv.pb["pA"], cv.pb["pB"]
    bar = cv.stage1(2)
    x_prev = {}
    z_done = {}
    st_state = {"u_prev": None}
    usb_turn = {}
    ustage_prev = {}
    prev_pair = None

    def stageA_turn2(pp):
        ppar = pp % 2
        Zp = lambda im: b.Z[0:72, ppar * 1024 + im * 512: ppar * 1024 + im * 512 + 512]
        uw = [z_done[ppar]]
        if st_state["u_prev"] is not None:
            uw.append(st_state["u_prev"])
        P.add("t", "mm", pA[:, 0:512], c.BDAre[0:72, 0:128], Zp(0), True, False, waits=uw)
        P.add("t", "mm", pA[:, 0:512], c.BDAim_neg[0:72, 0:128], Zp(1), False, True)
        P.add("t", "mm", pB[:, 0:512], c.BDAim[0:72, 0:128], Zp(0), True, False)
        P.add("t", "mm", pB[:, 0:512], c.BDAre[0:72, 0:128], Zp(1), False, True,
              incs=[("ft", 1)])
        ustage_prev[ppar] = ("ft", P.n("ft"))
        sw = [("ft", P.n("ft"))]
        if ppar in usb_turn:
            sw.append(usb_turn[ppar])
        P.add("s", "copy", b.Usb[:, ppar * 1024: ppar * 1024 + 512], pA[:, 0:512], waits=sw)
        P.add("s", "act", b.Usb[:, ppar * 1024 + 512: ppar * 1024 + 1024], pB[:, 0:512],
              ACTF.Copy, 0.0, -1.0, incs=[("fs", 1)])
        st_state["u_prev"] = ("fs", P.n("fs"))
        trs = []
        for ac in range(2):
            for im in range(2):
                for lg in range(2):
                    off = ppar * 1024 + im * 512 + lg * 256 + ac * 128
                    trs.append(b.Usb[:, off: off + 128])
        g0 = pp * 2
        dest = b.UT[:].rearrange("p (aci gg f) -> p aci gg f", aci=4, gg=32)[:, :, g0:g0 + 2, :]
        cv.tr_group(trs, dest, "v" if ppar == 0 else "s",
                    extra_waits=[st_state["u_prev"]])
        usb_turn[ppar] = ("ft", P.n("ft"))

    for pair in range(16):
        par = pair % 2
        kw = [k_wait]
        if par in z_done:
            kw.append(z_done[par])
        ks = "kk%d" % par
        P.add("q", "dma", b.Kbuf[0:72, par * 1024: par * 1024 + 1024],
              kin[pair], waits=kw, incs=[(ks, 16)])
        nkq = P.n(ks)
        ew = [("fv", bar[0]), ("fs", bar[1])] if pair < 2 else ()
        cv.pair_turn1(pair, extra_waits=ew)
        nft = cv.pair_stage2(pair, x_prev)
        sw = [("ft", nft)]
        if par in z_done:
            sw.append(z_done[par])
        P.add("s", "copy", b.tmp[0:72, par * 1024: par * 1024 + 512],
              pXre[par][0:72, 0:512], waits=sw)
        P.add("s", "copy", b.tmp[0:72, par * 1024 + 512: par * 1024 + 1024],
              pXim[par][0:72, 0:512], incs=[("fs", 1)])
        nfs = P.n("fs")
        x_prev[par] = ("fs", nfs)
        K = lambda im: b.Kbuf[0:72, par * 1024 + im * 512: par * 1024 + im * 512 + 512]
        Z = lambda im: b.Z[0:72, par * 1024 + im * 512: par * 1024 + im * 512 + 512]
        tre = b.tmp[0:72, par * 1024: par * 1024 + 512]
        tim = b.tmp[0:72, par * 1024 + 512: par * 1024 + 1024]
        zA, zB = b.tmp2[0:72, 0:512], b.tmp2[0:72, 512:1024]
        vw = [("fs", nfs), (ks, nkq)]
        if par in ustage_prev:
            vw.append(ustage_prev[par])
        P.add("v", "tt", zA, tre, K(0), ALU.mult, waits=vw)
        P.add("v", "tt", zB, tim, K(1), ALU.mult)
        P.add("v", "tt", Z(0), zA, zB, ALU.subtract)
        P.add("v", "tt", zA, tre, K(1), ALU.mult)
        P.add("v", "tt", zB, tim, K(0), ALU.mult)
        P.add("v", "tt", Z(1), zA, zB, ALU.add, incs=[("fv", 1)])
        z_done[par] = ("fv", P.n("fv"))
        if prev_pair is not None:
            stageA_turn2(prev_pair)
        prev_pair = pair
    stageA_turn2(prev_pair)
    bar2 = (("fv", P.n("fv")), ("fs", P.n("fs")))
    py_prev = {}
    tb_done = {}
    for nu in range(16):
        par = nu % 2
        bank = pXre[par]
        w = [bar2[0], bar2[1]] if nu < 2 else []
        if par in py_prev:
            w.append(py_prev[par])
        if nu < 2 and par in x_prev:
            w.append(x_prev[par])
        first = True
        for ac in range(2):
            hoff = (nu * 2 + ac) * 128
            base_re = (ac * 2 + 0) * 4096
            base_im = (ac * 2 + 1) * 4096
            UTre = b.UT[:, base_re:base_re + 4096].rearrange(
                "p (g c2 n2) -> p g c2 n2", c2=8, n2=16)[:, :, :, nu]
            UTim = b.UT[:, base_im:base_im + 4096].rearrange(
                "p (g c2 n2) -> p g c2 n2", c2=8, n2=16)[:, :, :, nu]
            P.add("t", "mm", bank[:, 0:256], c.Hre[:, hoff:hoff + 128], UTre,
                  first, False, waits=w if first else ())
            first = False
            P.add("t", "mm", bank[:, 0:256], c.Him[:, hoff:hoff + 128], UTim,
                  False, ac == 1, incs=[("ft", 1)] if ac == 1 else ())
        cons = "v" if par == 0 else "s"
        cn = "fv" if par == 0 else "fs"
        yw = [("ft", P.n("ft"))]
        if par in tb_done:
            yw.append(tb_done[par])
        P.add(cons, "copy", b.ysb[:, par * 256: par * 256 + 256], bank[:, 0:256],
              waits=yw, incs=[(cn, 1)])
        py_prev[par] = (cn, P.n(cn))
        trs = [b.ysb[:, par * 256 + cq * 128: par * 256 + (cq + 1) * 128]
               for cq in range(2)]
        dest = ych_ap.rearrange("p (cq t) -> p cq t", cq=2)[:, :, nu:2048:16]
        cv.tr_group(trs, dest, "v" if par == 0 else "s", extra_waits=[py_prev[par]])
        tb_done[par] = ("ft", P.n("ft"))



_CACHE = {}


def _get_nc():
    if "nc" not in _CACHE:
        _CACHE["nc"] = _build(_DEBUG)
    return _CACHE["nc"]


def _chunkvec(v):
    return np.ascontiguousarray(np.asarray(v, np.float32).reshape(16, 128).T)


def _chunkw3(v):
    return np.ascontiguousarray(
        np.asarray(v, np.float32).reshape(16, 128, 3).transpose(1, 0, 2).reshape(128, 48))


def kernel(**inputs):
    u = np.asarray(inputs["L0_attn_u"], dtype=np.float32)
    fc = _fft_constants()
    shared = {
        "WqT": np.ascontiguousarray(np.asarray(inputs["L0_attn_Wq"], np.float32).T),
        "WkT": np.ascontiguousarray(np.asarray(inputs["L0_attn_Wk"], np.float32).T),
        "WvT": np.ascontiguousarray(np.asarray(inputs["L0_attn_Wv"], np.float32).T),
        "WoT": np.ascontiguousarray(np.asarray(inputs["L0_attn_Wo"], np.float32).T),
        "bq": _chunkvec(inputs["L0_attn_bq"]),
        "bk": _chunkvec(inputs["L0_attn_bk"]),
        "bv": _chunkvec(inputs["L0_attn_bv"]),
        "bo": _chunkvec(inputs["L0_attn_bo"]),
        "cbq": _chunkvec(inputs["L0_attn_convq_b"]),
        "cbk": _chunkvec(inputs["L0_attn_convk_b"]),
        "dskip": _chunkvec(inputs["L0_attn_D_skip"]),
        "cwq": _chunkw3(inputs["L0_attn_convq_w"]),
        "cwk": _chunkw3(inputs["L0_attn_convk_w"]),
    }
    for n, a in fc.items():
        shared["c_" + n] = a
    ker = np.asarray(inputs["L0_attn_ssm_kernel"], np.float32)
    gw = np.asarray(inputs["L0_ffn_gate_w"], np.float32)
    uw = np.asarray(inputs["L0_ffn_up_w"], np.float32)
    dw = np.asarray(inputs["L0_ffn_down_w"], np.float32)
    zeros = np.zeros((TOK, D), np.float32)
    in_maps = []
    for core in range(8):
        s, h = core // 2, core % 2
        m = dict(shared)
        uo = np.ascontiguousarray(u[s, h * TOK:(h + 1) * TOK, :])
        m["u_own"] = uo
        m["u_ctx"] = np.ascontiguousarray(u[s, 0:TOK, :]) if h == 1 else uo
        m["uT_own"] = np.ascontiguousarray(uo.T)
        m["ctxm"] = np.full((128, 1), float(h), np.float32)
        m["ker_sl"] = np.ascontiguousarray(ker[core * 256:(core + 1) * 256, :])
        m["gw_s"] = np.ascontiguousarray(gw[core * 1024:(core + 1) * 1024, :])
        m["uw_s"] = np.ascontiguousarray(uw[core * 1024:(core + 1) * 1024, :])
        m["dw_s"] = np.ascontiguousarray(dw[core * 256:(core + 1) * 256, :])
        in_maps.append(m)
    nc = _get_nc()
    res = run_bass_kernel_spmd(nc, in_maps, core_ids=list(range(8)))
    out = np.empty((4, 4096, D), np.float32)
    for core in range(8):
        s, h = core // 2, core % 2
        out[s, h * TOK:(h + 1) * TOK, :] = res.results[core]["outT"].T
    if _DEBUG:
        kernel._dbg = [{k: v for k, v in r.items() if k.startswith("dbg_")}
                       for r in res.results]
    return out


# version 05
# speedup vs baseline: 1.2620x; 1.0469x over previous
import numpy as np
import ml_dtypes
import concourse.bass as bass
import concourse.mybir as mybir
from concourse.bass_utils import run_bass_kernel_spmd
from contextlib import ExitStack

F32 = mybir.dt.float32
BF16 = mybir.dt.bfloat16
FP8 = mybir.dt.float8e4
ALU = mybir.AluOpType
AX = mybir.AxisListType.X
ACTF = mybir.ActivationFunctionType
BF = ml_dtypes.bfloat16

_DEBUG = ()

D = 2048
TOK = 2048
WIN = 4096
I_FF = 8192



def _fft_constants():
    Gre = np.zeros((128, 16 * 2 * 2 * 128))
    Gim = np.zeros((128, 16 * 2 * 2 * 128))
    r_r = np.arange(256)
    a_r = np.arange(256)
    for s in range(16):
        ph = np.exp(-2j * np.pi * (np.outer(r_r, a_r) / 256.0 + s * a_r[None, :] / 4096.0))
        for rc in range(2):
            for ac in range(2):
                off = ((s * 2 + rc) * 2 + ac) * 128
                Gre[:, off:off + 128] = ph[rc * 128:(rc + 1) * 128, ac * 128:(ac + 1) * 128].real
                Gim[:, off:off + 128] = ph[rc * 128:(rc + 1) * 128, ac * 128:(ac + 1) * 128].imag
    BD2 = np.zeros((128, 72), np.complex128)
    for c in range(8):
        for s in range(16):
            for b in range(9):
                BD2[c * 16 + s, c * 9 + b] = np.exp(-2j * np.pi * s * b / 16.0)
    BDA = np.zeros((72, 128), np.complex128)
    for c in range(8):
        for b in range(9):
            for nu in range(16):
                BDA[c * 9 + b, c * 16 + nu] = np.exp(2j * np.pi * b * nu / 16.0)
    Hre = np.zeros((128, 16 * 2 * 128))
    Him = np.zeros((128, 16 * 2 * 128))
    for nu in range(16):
        for ac in range(2):
            a = (ac * 128 + np.arange(128))[:, None]
            eta = (128 + np.arange(128))[None, :]
            ph = np.exp(2j * np.pi * (a * nu / 4096.0 + a * eta / 256.0))
            off = (nu * 2 + ac) * 128
            Hre[:, off:off + 128] = ph.real
            Him[:, off:off + 128] = ph.imag
    cm = np.zeros((72, 256))
    for c in range(8):
        for b in range(9):
            if b < 8:
                w = np.full(256, 2.0)
                if b == 0:
                    w[0] = 1.0
            else:
                w = np.zeros(256)
                w[0] = 1.0
            cm[c * 9 + b] = w / 4096.0
    return {
        "Gre": Gre.astype(BF), "Gim": Gim.astype(BF),
        "BD2re": np.ascontiguousarray(BD2.real).astype(BF),
        "BD2im": np.ascontiguousarray(BD2.imag).astype(BF),
        "BD2im_neg": np.ascontiguousarray(-BD2.imag).astype(BF),
        "BDAre": np.ascontiguousarray(BDA.real).astype(BF),
        "BDAim": np.ascontiguousarray(BDA.imag).astype(BF),
        "BDAim_neg": np.ascontiguousarray(-BDA.imag).astype(BF),
        "Hre": Hre.astype(BF), "Him": Him.astype(BF),
        "cmask": cm.astype(np.float32),
        "identity": np.eye(128).astype(BF),
    }



class _Plan:
    def __init__(self, nc):
        self.nc = nc
        self.steps = {e: [] for e in ("t", "v", "s", "g", "q")}
        self.sems = {}
        self.counts = {}

    def sem(self, name):
        if name not in self.sems:
            self.sems[name] = self.nc.alloc_semaphore("sem_" + name)
            self.counts[name] = 0
        return self.sems[name]

    def n(self, name):
        self.sem(name)
        return self.counts[name]

    def add(self, eng, kind, *args, waits=(), incs=()):
        w = [(self.sem(nm), v) for nm, v in waits]
        inc = []
        for nm, amt in incs:
            inc.append((self.sem(nm), amt))
            self.counts[nm] += amt
        self.steps[eng].append((kind, args, w, inc))

    def replay(self, key, eng):
        for kind, args, waits, incs in self.steps[key]:
            for sem, val in waits:
                eng.wait_ge(sem, val)
            ins = self._emit(eng, kind, args)
            for sem, amt in incs:
                ins.then_inc(sem, amt)

    @staticmethod
    def _emit(eng, kind, a):
        if kind == "mm":
            out, lhsT, rhs, start, stop = a
            return eng.matmul(out, lhsT, rhs, start=start, stop=stop)
        if kind == "tr":
            out, in_, ident = a
            return eng.transpose(out, in_, ident)
        if kind == "copy":
            out, in_ = a
            if hasattr(eng, "tensor_copy"):
                return eng.tensor_copy(out, in_)
            return eng.copy(out, in_)
        if kind == "tt":
            out, in0, in1, op = a
            return eng.tensor_tensor(out, in0, in1, op=op)
        if kind == "ts":
            out, in0, s1, s2, op0, op1 = a
            if op1 is None:
                return eng.tensor_scalar(out, in0, s1, None, op0=op0)
            return eng.tensor_scalar(out, in0, s1, s2, op0=op0, op1=op1)
        if kind == "stt":
            out, in0, scal, in1, op0, op1 = a
            return eng.scalar_tensor_tensor(out, in0, scal, in1, op0=op0, op1=op1)
        if kind == "act":
            out, in_, func, bias, scale = a
            return eng.activation(out, in_, func, bias=bias, scale=scale)
        if kind == "red":
            out, in_, op, absval = a
            return eng.tensor_reduce(out, in_, axis=AX, op=op, apply_absolute_value=absval)
        if kind == "bnst":
            out, in_ = a
            return eng.bn_stats(out, in_)
        if kind == "bnag":
            out, in_ = a
            return eng.bn_aggr(out, in_)
        if kind == "dma":
            out, in_ = a
            return eng.dma_start(out, in_)
        if kind == "memset":
            ap, val = a
            return eng.memset(ap, val)
        if kind == "recip":
            out, in_ = a
            return eng.reciprocal(out, in_)
        if kind == "coll":
            kind_s, op, groups, ins_, outs_ = a
            return eng.collective_compute(kind_s, op, replica_groups=groups,
                                          ins=ins_, outs=outs_)
        raise KeyError(kind)


def _run_block(nc, plan):
    with nc.Block() as blk:
        if plan.steps["q"]:
            blk.sync(lambda e: plan.replay("q", e))
        if plan.steps["g"]:
            blk.gpsimd(lambda e: plan.replay("g", e))
        if plan.steps["v"]:
            blk.vector(lambda e: plan.replay("v", e))
        if plan.steps["s"]:
            blk.scalar(lambda e: plan.replay("s", e))
        if plan.steps["t"]:
            blk.tensor(lambda e: plan.replay("t", e))



class _ConvBufs:
    _SEQ = [0]

    def __init__(self, nc, stack):
        ent = stack.enter_context
        _ConvBufs._SEQ[0] += 1
        pref = "cv%d_" % _ConvBufs._SEQ[0]
        T = lambda n, sh, dt: ent(nc.sbuf_tensor(pref + n, sh, dt))
        self.S = T("S", [128, 16 * 2 * 256], BF16)
        self.E = T("E", [128, 2 * 2 * 4096], BF16)
        self.UT = T("UT", [128, 2 * 2 * 4096], BF16)
        self.Fbuf = T("F", [128, 2 * 1024], BF16)
        self.Z = T("Z", [128, 2 * 1024], BF16)
        self.Ksb = T("Ksb", [128, 2 * 1024], BF16)
        self.Kbuf = self.Ksb
        self.Usb = T("Usb", [128, 2 * 1024], BF16)
        self.tmp = T("tmp", [128, 2 * 1024], F32)
        self.tmp2 = T("tmp2", [128, 1024], F32)
        self.ysb = T("ysb", [128, 2 * 256], BF16)


class _Conv:

    def __init__(self, P, b, cb, pb):
        self.P, self.b, self.c, self.pb = P, b, cb, pb
        self.ptr_grp = 0
        self.ptr_evac = {}

    def tr_group(self, trs, dest, cons, extra_waits=(), evac_extra=()):
        P, b = self.P, self.b
        g = self.ptr_grp
        bank = self.pb["ptr"][g % 2]
        waits = list(extra_waits)
        if g - 2 in self.ptr_evac:
            waits.append(self.ptr_evac[g - 2])
        for i, in_ap in enumerate(trs):
            P.add("t", "tr", bank[:, i * 128:(i + 1) * 128], in_ap, self.c.ident[:],
                  waits=waits if i == 0 else (),
                  incs=[("ft", 1)] if i == len(trs) - 1 else ())
        cn = "fv" if cons == "v" else "fs"
        P.add(cons, "copy", dest, bank[:, 0:128 * len(trs)],
              waits=[("ft", P.n("ft"))] + list(evac_extra), incs=[(cn, 1)])
        self.ptr_evac[g] = (cn, P.n(cn))
        self.ptr_grp += 1

    def strided_tr(self, tiles, rchunks, extra_first=()):
        b = self.b
        first = True
        for s in range(16):
            for rc in range(rchunks):
                trs = [tiles[cq][:, s + 2048 * rc: s + 2048 * rc + 16 * 127 + 1: 16]
                      for cq in range(2)]
                off = (s * 2 + rc) * 256
                self.tr_group(trs, b.S[:, off:off + 256],
                              "v" if (s + rc) % 2 == 0 else "s",
                              extra_waits=extra_first if first else ())
                first = False

    def stage1(self, rchunks):
        P, b, c = self.P, self.b, self.c
        s1_evac = {}
        k = 0
        for s in range(16):
            for ac in range(2):
                for im in range(2):
                    bank = self.pb["pA"] if k % 2 == 0 else self.pb["pB"]
                    cons = "v" if k % 2 == 0 else "s"
                    waits = []
                    if k - 2 in s1_evac:
                        waits.append(s1_evac[k - 2])
                    if k <= 1:
                        waits += [("fv", P.n("fv")), ("fs", P.n("fs"))]
                    for rc in range(rchunks):
                        off = ((s * 2 + rc) * 2 + ac) * 128
                        G = (c.Gim if im else c.Gre)[:, off:off + 128]
                        soff = (s * 2 + rc) * 256
                        P.add("t", "mm", bank[:, 0:256], G, b.S[:, soff:soff + 256],
                              rc == 0, rc == rchunks - 1,
                              waits=waits if rc == 0 else (),
                              incs=[("ft", 1)] if rc == rchunks - 1 else ())
                    eoff = (ac * 2 + im) * 4096
                    dest = b.E[:, eoff:eoff + 4096].rearrange(
                        "p (g c2 s2) -> p g c2 s2", c2=8, s2=16)[:, :, :, s]
                    cn = "fv" if cons == "v" else "fs"
                    P.add(cons, "copy", dest, bank[:, 0:256],
                          waits=[("ft", P.n("ft"))], incs=[(cn, 1)])
                    s1_evac[k] = (cn, P.n(cn))
                    k += 1
        return P.n("fv"), P.n("fs")

    def pair_turn1(self, pair, extra_waits=()):
        b = self.b
        trs = []
        for im in range(2):
            for lg in range(2):
                g = pair * 2 + lg
                for ac in range(2):
                    eoff = (ac * 2 + im) * 4096
                    trs.append(b.E[:, eoff + g * 128: eoff + (g + 1) * 128])
        par = pair % 2
        dest = b.Fbuf[:, par * 1024: par * 1024 + 1024]
        self.tr_group(trs, dest, "v" if par == 0 else "s", extra_waits=extra_waits)

    def pair_stage2(self, pair, x_prev):
        P, b, c = self.P, self.b, self.c
        par = pair % 2
        fw = [self.ptr_evac[self.ptr_grp - 1]]
        if par in x_prev:
            fw.append(x_prev[par])
        F = lambda im: b.Fbuf[:, par * 1024 + im * 512: par * 1024 + im * 512 + 512]
        pXre, pXim = self.pb["pXre"], self.pb["pXim"]
        P.add("t", "mm", pXre[par][0:72, 0:512], c.BD2re[0:128, 0:72], F(0), True, False, waits=fw)
        P.add("t", "mm", pXre[par][0:72, 0:512], c.BD2im_neg[0:128, 0:72], F(1), False, True)
        P.add("t", "mm", pXim[par][0:72, 0:512], c.BD2im[0:128, 0:72], F(0), True, False)
        P.add("t", "mm", pXim[par][0:72, 0:512], c.BD2re[0:128, 0:72], F(1), False, True,
              incs=[("ft", 1)])
        return P.n("ft")

    def emit_ker(self, ker_tiles, kout, extra_first=()):
        P, b, c = self.P, self.b, self.c
        self.strided_tr(ker_tiles, 1, extra_first=extra_first)
        bar = self.stage1(1)
        x_prev = {}
        kq_guard = {}
        for pair in range(16):
            ew = [("fv", bar[0]), ("fs", bar[1])] if pair < 2 else ()
            self.pair_turn1(pair, extra_waits=ew)
            nft = self.pair_stage2(pair, x_prev)
            par = pair % 2
            cm = c.cmask[0:72, :].unsqueeze(1).broadcast_to([72, 2, 256])
            waits = [("ft", nft)]
            if par in kq_guard:
                waits.append(kq_guard[par])
            pXre, pXim = self.pb["pXre"], self.pb["pXim"]
            P.add("v", "tt",
                  b.Ksb[0:72, par * 1024: par * 1024 + 512].rearrange("p (l f) -> p l f", l=2),
                  pXre[par][0:72, 0:512].rearrange("p (l f) -> p l f", l=2), cm, ALU.mult,
                  waits=waits)
            P.add("v", "tt",
                  b.Ksb[0:72, par * 1024 + 512: par * 1024 + 1024].rearrange("p (l f) -> p l f", l=2),
                  pXim[par][0:72, 0:512].rearrange("p (l f) -> p l f", l=2), cm, ALU.mult,
                  incs=[("fv", 1)])
            x_prev[par] = ("fv", P.n("fv"))
            ks = "kq%d" % par
            P.add("q", "dma", kout[pair], b.Ksb[0:72, par * 1024: par * 1024 + 1024],
                  waits=[("fv", P.n("fv"))], incs=[(ks, 16)])
            kq_guard[par] = (ks, P.n(ks))

    def emit_kv(self, kv_tiles, kin, ych_ap, k_extra_wait=()):
        P, b, c = self.P, self.b, self.c
        pXre, pXim = self.pb["pXre"], self.pb["pXim"]
        pA, pB = self.pb["pA"], self.pb["pB"]
        self.strided_tr(kv_tiles, 2)
        bar = self.stage1(2)
        x_prev = {}
        z_done = {}
        u_prev = None
        usb_turn = {}
        ustage_prev = {}
        for pair in range(16):
            par = pair % 2
            kw = list(k_extra_wait)
            if par in z_done:
                kw.append(z_done[par])
            P.add("q", "dma", b.Kbuf[0:72, par * 1024: par * 1024 + 1024],
                  kin[pair], waits=kw, incs=[("kq", 16)])
            nkq = P.n("kq")
            ew = [("fv", bar[0]), ("fs", bar[1])] if pair < 2 else ()
            self.pair_turn1(pair, extra_waits=ew)
            nft = self.pair_stage2(pair, x_prev)
            sw = [("ft", nft)]
            if par in z_done:
                sw.append(z_done[par])
            P.add("s", "copy", b.tmp[0:72, par * 1024: par * 1024 + 512],
                  pXre[par][0:72, 0:512], waits=sw)
            P.add("s", "copy", b.tmp[0:72, par * 1024 + 512: par * 1024 + 1024],
                  pXim[par][0:72, 0:512], incs=[("fs", 1)])
            nfs = P.n("fs")
            x_prev[par] = ("fs", nfs)
            K = lambda im: b.Kbuf[0:72, par * 1024 + im * 512: par * 1024 + im * 512 + 512]
            Z = lambda im: b.Z[0:72, par * 1024 + im * 512: par * 1024 + im * 512 + 512]
            tre = b.tmp[0:72, par * 1024: par * 1024 + 512]
            tim = b.tmp[0:72, par * 1024 + 512: par * 1024 + 1024]
            zA, zB = b.tmp2[0:72, 0:512], b.tmp2[0:72, 512:1024]
            vw = [("fs", nfs), ("kq", nkq)]
            if par in ustage_prev:
                vw.append(ustage_prev[par])
            P.add("v", "tt", zA, tre, K(0), ALU.mult, waits=vw)
            P.add("v", "tt", zB, tim, K(1), ALU.mult)
            P.add("v", "tt", Z(0), zA, zB, ALU.subtract)
            P.add("v", "tt", zA, tre, K(1), ALU.mult)
            P.add("v", "tt", zB, tim, K(0), ALU.mult)
            P.add("v", "tt", Z(1), zA, zB, ALU.add, incs=[("fv", 1)])
            z_done[par] = ("fv", P.n("fv"))
            uw = [z_done[par]]
            if u_prev is not None:
                uw.append(u_prev)
            P.add("t", "mm", pA[:, 0:512], c.BDAre[0:72, 0:128], Z(0), True, False, waits=uw)
            P.add("t", "mm", pA[:, 0:512], c.BDAim_neg[0:72, 0:128], Z(1), False, True)
            P.add("t", "mm", pB[:, 0:512], c.BDAim[0:72, 0:128], Z(0), True, False)
            P.add("t", "mm", pB[:, 0:512], c.BDAre[0:72, 0:128], Z(1), False, True,
                  incs=[("ft", 1)])
            ustage_prev[par] = ("ft", P.n("ft"))
            sw = [("ft", P.n("ft"))]
            if par in usb_turn:
                sw.append(usb_turn[par])
            P.add("s", "copy", b.Usb[:, par * 1024: par * 1024 + 512], pA[:, 0:512],
                  waits=sw)
            P.add("s", "act", b.Usb[:, par * 1024 + 512: par * 1024 + 1024], pB[:, 0:512],
                  ACTF.Copy, 0.0, -1.0, incs=[("fs", 1)])
            u_prev = ("fs", P.n("fs"))
            trs = []
            for ac in range(2):
                for im in range(2):
                    for lg in range(2):
                        off = par * 1024 + im * 512 + lg * 256 + ac * 128
                        trs.append(b.Usb[:, off: off + 128])
            g0 = pair * 2
            dest = b.UT[:].rearrange("p (aci gg f) -> p aci gg f", aci=4, gg=32)[:, :, g0:g0 + 2, :]
            self.tr_group(trs, dest, "v" if par == 0 else "s", extra_waits=[u_prev])
            usb_turn[par] = ("ft", P.n("ft"))
        bar2 = (("fv", P.n("fv")), ("fs", P.n("fs")))
        py_prev = {}
        tb_done = {}
        for nu in range(16):
            par = nu % 2
            bank = pXre[par]
            w = [bar2[0], bar2[1]] if nu < 2 else []
            if par in py_prev:
                w.append(py_prev[par])
            if nu < 2 and par in x_prev:
                w.append(x_prev[par])
            first = True
            for ac in range(2):
                hoff = (nu * 2 + ac) * 128
                UTr = b.UT[:, 0 * 4096 + 0:].rearrange("p (aci gg f) -> p aci gg f", aci=4, gg=32)
                base_re = (ac * 2 + 0) * 4096
                base_im = (ac * 2 + 1) * 4096
                UTre = b.UT[:, base_re:base_re + 4096].rearrange(
                    "p (g c2 n2) -> p g c2 n2", c2=8, n2=16)[:, :, :, nu]
                UTim = b.UT[:, base_im:base_im + 4096].rearrange(
                    "p (g c2 n2) -> p g c2 n2", c2=8, n2=16)[:, :, :, nu]
                P.add("t", "mm", bank[:, 0:256], self.c.Hre[:, hoff:hoff + 128], UTre,
                      first, False, waits=w if first else ())
                first = False
                P.add("t", "mm", bank[:, 0:256], self.c.Him[:, hoff:hoff + 128], UTim,
                      False, ac == 1, incs=[("ft", 1)] if ac == 1 else ())
            cons = "v" if par == 0 else "s"
            cn = "fv" if par == 0 else "fs"
            yw = [("ft", P.n("ft"))]
            if par in tb_done:
                yw.append(tb_done[par])
            P.add(cons, "copy", self.b.ysb[:, par * 256: par * 256 + 256], bank[:, 0:256],
                  waits=yw, incs=[(cn, 1)])
            py_prev[par] = (cn, P.n(cn))
            trs = [self.b.ysb[:, par * 256 + cq * 128: par * 256 + (cq + 1) * 128]
                   for cq in range(2)]
            dest = ych_ap.rearrange("p (cq t) -> p cq t", cq=2)[:, :, nu:2048:16]
            self.tr_group(trs, dest, "v" if par == 0 else "s",
                          extra_waits=[py_prev[par]])
            tb_done[par] = ("ft", P.n("ft"))



def _build(debug=()):
    nc = bass.Bass("TRN2", target_bir_lowering=False)

    din = {}
    def DI(name, shape, dt=F32):
        din[name] = nc.dram_tensor(name, list(shape), dt, kind="ExternalInput")
        return din[name]

    u_own = DI("u_own", [TOK, D])
    u_ctx = DI("u_ctx", [TOK, D])
    uT_own = DI("uT_own", [D, TOK])
    ctxm = DI("ctxm", [128, 1])
    for w in ("WqT", "WkT", "WvT", "WoT"):
        DI(w, [D, D])
    for bname in ("bq", "bk", "bv", "bo", "cbq", "cbk", "dskip"):
        DI(bname, [128, 16])
    DI("cwq", [128, 48])
    DI("cwk", [128, 48])
    DI("ker_sl", [256, D])
    DI("gw_s", [1024, D])
    DI("uw_s", [1024, D])
    DI("dw_s", [256, I_FF])
    fc = _fft_constants()
    for name, arr in fc.items():
        DI("c_" + name, arr.shape, BF16 if arr.dtype == BF else F32)

    outT = nc.dram_tensor("outT", [D, TOK], F32, kind="ExternalOutput")

    wbf = {w: nc.dram_tensor("bf_" + w, [D, D], BF16, kind="Internal")
           for w in ("WqT", "WkT", "WvT", "WoT")}
    kv_scr = nc.dram_tensor("kv_scr", [D, WIN], BF16, kind="Internal")
    q_scr = nc.dram_tensor("q_scr", [D, TOK], BF16, kind="Internal")
    y_scr = nc.dram_tensor("y_scr", [D, TOK], BF16, kind="Internal")
    attn_scr = nc.dram_tensor("attn_scr", [D, TOK], BF16, kind="Internal")
    xdq_scr = nc.dram_tensor("xdq_scr", [D, TOK], BF16, kind="Internal")
    xn_scr = nc.dram_tensor("xn_scr", [TOK, D], BF16, kind="Internal")
    h_scr = nc.dram_tensor("h_scr", [I_FF, TOK], BF16, kind="Internal")
    ccK_in = nc.dram_tensor("ccK_in", [16, 72, 1024], BF16, kind="Internal")
    ccK_out = nc.dram_tensor("ccK_out", [8, 16, 72, 1024], BF16, kind="Internal")
    ccW_in = nc.dram_tensor("ccW_in", [6144, 1024], BF16, kind="Internal")
    ccW_out = nc.dram_tensor("ccW_out", [8, 6144, 1024], BF16, kind="Internal")
    dbg = {}
    for name in debug:
        scr = dict(kv_scr=kv_scr, q_scr=q_scr, y_scr=y_scr, attn_scr=attn_scr,
                   xdq_scr=xdq_scr, h_scr=h_scr, xn_scr=xn_scr, ccW_in=ccW_in,
                   ccK_in=ccK_in)[name]
        dbg[name] = (scr, nc.dram_tensor("dbg_" + name, list(scr.shape), BF16,
                                         kind="ExternalOutput"))

    with ExitStack() as persist:
        ent = persist.enter_context

        class _Consts:
            pass
        c = _Consts()
        for name in fc:
            tgt = "ident" if name == "identity" else name
            setattr(c, tgt, ent(nc.sbuf_tensor("k_" + tgt, list(fc[name].shape),
                                               BF16 if fc[name].dtype == BF else F32)))
        biases = ent(nc.sbuf_tensor("biases", [128, 16 * 7], F32))
        convw = ent(nc.sbuf_tensor("convw", [128, 16 * 6], F32))
        maskv = ent(nc.sbuf_tensor("maskv", [128, 1], F32))
        epsv = ent(nc.sbuf_tensor("epsv", [128, 1], F32))
        bm = ent(nc.sbuf_tensor("bm", [128, 16 * 3], F32))

        def bias_ap(idx, ecn):
            return biases[:, idx * 16 + ecn: idx * 16 + ecn + 1]

        def convw_ap(which, ecn, j):
            return convw[:, which * 48 + ecn * 3 + j: which * 48 + ecn * 3 + j + 1]

        pA = ent(nc.psum_tensor("pA", [128, 512], F32))
        pB = ent(nc.psum_tensor("pB", [128, 512], F32))
        ptr0 = ent(nc.psum_tensor("ptr0", [128, 1024], BF16))
        ptr1 = ent(nc.psum_tensor("ptr1", [128, 1024], BF16))
        pC = ent(nc.psum_tensor("pC", [128, 512], F32))
        pD = ent(nc.psum_tensor("pD", [128, 512], F32))
        pE = ent(nc.psum_tensor("pE", [128, 512], F32))
        pF = ent(nc.psum_tensor("pF", [128, 512], F32))
        pb = {"pA": pA, "pB": pB, "ptr": [ptr0, ptr1],
              "pXre": [pC, pD], "pXim": [pE, pF]}

        P = _Plan(nc)

        for name in fc:
            tgt = "ident" if name == "identity" else name
            rows = fc[name].shape[0]
            P.add("q", "dma", getattr(c, tgt)[0:rows, :], din["c_" + name][:],
                  incs=[("ld", 16)])
        for i, bname in enumerate(("bq", "bk", "bv", "bo", "cbq", "cbk", "dskip")):
            P.add("q", "dma", biases[:, i * 16:(i + 1) * 16], din[bname][:],
                  incs=[("ld", 16)])
        P.add("q", "dma", convw[:, 0:48], din["cwq"][:], incs=[("ld", 16)])
        P.add("q", "dma", convw[:, 48:96], din["cwk"][:], incs=[("ld", 16)])
        P.add("q", "dma", maskv[:], ctxm[:], incs=[("ld", 16)])
        n_const_ld = P.n("ld")
        P.add("v", "memset", epsv[:], 1e-5)
        cv = _Conv(P, None, c, pb)
        for w in ("WqT", "WkT", "WvT", "WoT"):
            P.add("g", "dma", wbf[w][:], din[w][:], incs=[("gd", 16)])
        n_wcast = P.n("gd")
        P.add("v", "ts", bm[:, 0:48], biases[:, 0:48], maskv[:, 0:1], None, ALU.mult, None,
              waits=[("ld", n_const_ld)], incs=[("fv", 1)])

        with ExitStack() as sc3:
            e3 = sc3.enter_context
            ubuf = e3(nc.sbuf_tensor("ubuf", [128, 2 * 2048], F32))
            xsq = e3(nc.sbuf_tensor("xsq", [128, 2048], F32))
            xnf = e3(nc.sbuf_tensor("xnf", [128, 2048], F32))
            cscr = e3(nc.sbuf_tensor("cscr", [128, 2 * 1024], F32))
            xnb = e3(nc.sbuf_tensor("xnb", [128, 2 * 2048], BF16))
            stats = e3(nc.sbuf_tensor("stats", [128, 32], F32))
            lnsc = e3(nc.sbuf_tensor("lnsc", [128, 8], F32))
            xnT = e3(nc.sbuf_tensor("xnT", [128, 16 * 1024], BF16))
            wslab = e3(nc.sbuf_tensor("wslab", [128, 4 * 2048], BF16))
            kraw = e3(nc.sbuf_tensor("kraw", [128, 2 * 1028], BF16))
            vraw = e3(nc.sbuf_tensor("vraw", [128, 2 * 1024], BF16))
            qraw = e3(nc.sbuf_tensor("qraw", [128, 2 * 1028], BF16))
            ktail = e3(nc.sbuf_tensor("ktail", [128, 16 * 4], BF16))
            qtail = e3(nc.sbuf_tensor("qtail", [128, 16 * 4], BF16))
            kvec = e3(nc.sbuf_tensor("kvec", [128, 2 * 1024], BF16))
            qcec = e3(nc.sbuf_tensor("qcec", [128, 2 * 1024], BF16))
            xq8 = e3(nc.sbuf_tensor("xq8", [128, 2048], FP8))
            xdq = e3(nc.sbuf_tensor("xdq", [128, 2048], BF16))
            xsc = e3(nc.sbuf_tensor("xsc", [128, 4 * 64], F32))

            uload_done = {}
            xstage_guard = {}
            xn_done = {}
            wsl_done = {}
            conv_done_par = {}
            kv_dma_par = {}
            qc_dma_par = {}
            pp_evac = {}
            pp_k = 0

            def ln_tile(gi, own):
                par = gi % 2
                src = u_own if own else u_ctx
                row0 = (gi * 128) % 2048
                lw = []
                if par in uload_done:
                    lw.append(uload_done[par])
                P.add("q", "dma", ubuf[:, par * 2048:(par + 1) * 2048],
                      src[row0:row0 + 128, :], waits=lw, incs=[("lu%d" % par, 16)])
                u_ap = ubuf[:, par * 2048:(par + 1) * 2048]
                nld_u = P.n("lu%d" % par)
                vw = [("lu%d" % par, nld_u)]
                if gi > 0:
                    vw.append(xn_done[gi - 1])
                P.add("s", "act", xsq[:], u_ap, ACTF.Square, 0.0, 1.0,
                      waits=[("lu%d" % par, nld_u)] + ([xn_done[gi - 1]] if gi > 0 else []),
                      incs=[("fs", 1)])
                nfs_sq = P.n("fs")
                P.add("v", "red", stats[:, 0:1], u_ap.unsqueeze(1), ALU.add, False,
                      waits=vw)
                P.add("v", "red", stats[:, 1:2], xsq[:].unsqueeze(1), ALU.add, False,
                      waits=[("fs", nfs_sq)], incs=[("fv", 1)])
                P.add("v", "ts", stats[:, 2:3], stats[:, 0:1], 1.0 / 2048.0, None,
                      ALU.mult, None, waits=[("fv", P.n("fv"))], incs=[("fv", 1)])
                P.add("v", "tt", stats[:, 3:4], stats[:, 2:3], stats[:, 2:3], ALU.mult,
                      waits=[("fv", P.n("fv"))], incs=[("fv", 1)])
                P.add("v", "ts", stats[:, 4:5], stats[:, 1:2], 1.0 / 2048.0,
                      stats[:, 3:4], ALU.mult, ALU.subtract,
                      waits=[("fv", P.n("fv"))], incs=[("fv", 1)])
                P.add("s", "act", lnsc[:, 0:1], stats[:, 4:5], ACTF.Ln, epsv[:, 0:1], 1.0,
                      waits=[("fv", P.n("fv"))])
                P.add("s", "act", lnsc[:, 1:2], lnsc[:, 0:1], ACTF.Exp, 0.0, -0.5,
                      incs=[("fs", 1)])
                P.add("v", "ts", xnf[:], u_ap, stats[:, 2:3], lnsc[:, 1:2],
                      ALU.subtract, ALU.mult, waits=[("fs", P.n("fs"))], incs=[("fv", 1)])
                uload_done[par] = ("fv", P.n("fv"))
                P.add("v", "copy", xnb[:, par * 2048:(par + 1) * 2048], xnf[:],
                      incs=[("fv", 1)])
                xn_done[gi] = ("fv", P.n("fv"))
                if own and _DEBUG:
                    P.add("q", "dma", xn_scr[(gi * 128) % 2048:(gi * 128) % 2048 + 128, :],
                          xnb[:, par * 2048:(par + 1) * 2048], waits=[xn_done[gi]],
                          incs=[("xs", 16)])
                if own:
                    row0o = row0
                    P.add("v", "red", xsc[:, 0:64], xnf[:].rearrange("p (b k) -> p b k", k=32),
                          ALU.max, True, incs=[("fv", 1)])
                    P.add("v", "ts", xsc[:, 64:128], xsc[:, 0:64], 1.0 / 448.0, 1e-30,
                          ALU.mult, ALU.max, waits=[("fv", P.n("fv"))], incs=[("fv", 1)])
                    P.add("v", "recip", xsc[:, 128:192], xsc[:, 64:128],
                          waits=[("fv", P.n("fv"))], incs=[("fv", 1)])
                    P.add("v", "ts", xsc[:, 192:256], xsc[:, 64:128], 2.0, None, ALU.mult, None,
                          waits=[("fv", P.n("fv"))], incs=[("fv", 1)])
                    P.add("v", "stt", xq8[:].rearrange("p (b k) -> p b k", k=32),
                          xnf[:].rearrange("p (b k) -> p b k", k=32), 0.5,
                          xsc[:, 128:192].unsqueeze(-1).broadcast_to([128, 64, 32]),
                          ALU.mult, ALU.mult, waits=[("fv", P.n("fv"))])
                    P.add("v", "tt", xdq[:].rearrange("p (b k) -> p b k", k=32),
                          xq8[:].rearrange("p (b k) -> p b k", k=32),
                          xsc[:, 192:256].unsqueeze(-1).broadcast_to([128, 64, 32]),
                          ALU.mult, incs=[("fv", 1)])
                    nq = ("fv", P.n("fv"))
                    for hc4 in range(4):
                        gslot = cv.ptr_grp % 2
                        stg = wstage_x[:, gslot * 512:(gslot + 1) * 512]
                        trs = [xdq[:, (hc4 * 4 + i2) * 128:(hc4 * 4 + i2 + 1) * 128]
                               for i2 in range(4)]
                        xsem = "xd%d" % gslot
                        ee = [(xsem, xstage_guard[gslot])] if gslot in xstage_guard else []
                        cv.tr_group(trs, stg, "s", extra_waits=[nq], evac_extra=ee)
                        dest = xdq_scr[hc4 * 512:(hc4 + 1) * 512, row0o:row0o + 128].rearrange(
                            "(b p) t -> p b t", p=128)
                        P.add("q", "dma", dest, stg.rearrange("p (b f) -> p b f", b=4),
                              waits=[cv.ptr_evac[cv.ptr_grp - 1]], incs=[(xsem, 16)])
                        xstage_guard[gslot] = P.n(xsem)

            wstage_x = e3(nc.sbuf_tensor("wstage_x", [128, 1024], BF16))

            def xn_tr(gi, gb):
                par = gi % 2
                src = xnb[:, par * 2048:(par + 1) * 2048]
                for hc4 in range(4):
                    trs = [src[:, (hc4 * 4 + i2) * 128:(hc4 * 4 + i2 + 1) * 128]
                           for i2 in range(4)]
                    dest = xnT[:].rearrange("p (hc t) -> p hc t", hc=16)[
                        :, hc4 * 4:(hc4 + 1) * 4, gb * 128:(gb + 1) * 128]
                    ew = [xn_done[gi]]
                    if gi == 0 and hc4 == 0:
                        ew.append(("ld", n_const_ld))
                    cv.tr_group(trs, dest, "v" if hc4 % 2 == 0 else "s",
                                extra_waits=ew)

            qtsave = e3(nc.sbuf_tensor("qtsave", [128, 32], BF16))
            qmini_prev = None
            qts_bar = None
            for grp in range(4):
                own_grp = grp >= 2
                for gb in range(8):
                    gi = grp * 8 + gb
                    ln_tile(gi, own_grp)
                    xn_tr(gi, gb)
                xnt_bar = (("fv", P.n("fv")), ("fs", P.n("fs")))
                if grp == 1:
                    P.add("v", "copy", qtsave[:].rearrange("p (hc t) -> p hc t", t=2),
                          xnT[:].rearrange("p (hc t) -> p hc t", hc=16)[:, :, 1022:1024],
                          waits=[xnt_bar[0], xnt_bar[1]], incs=[("fv", 1)])
                    qts_bar = ("fv", P.n("fv"))
                projs = ([("WqT", 0, qraw, True)] if own_grp else []) + \
                    [("WkT", 1, kraw, True), ("WvT", 2, vraw, False)]
                for ec in range(16):
                    raw_done = {}
                    rpar = ec % 2
                    for wname, bidx, rawbuf, haspad in projs:
                        wpar = (pp_k // 2) % 4
                        lw = []
                        if wpar in wsl_done:
                            lw.append(wsl_done[wpar])
                        if grp == 0 and ec == 0 and wname == projs[0][0]:
                            lw = lw + [("gd", n_wcast)]
                        P.add("q", "dma", wslab[:, wpar * 2048:(wpar + 1) * 2048].rearrange(
                            "p (hc f) -> p hc f", hc=16),
                            wbf[wname][:, ec * 128:(ec + 1) * 128].rearrange(
                            "(hc p) f -> p hc f", p=128), waits=lw, incs=[("ls%d" % wpar, 16)])
                        nld = P.n("ls%d" % wpar)
                        pad = 4 if haspad else 0
                        rw = rawbuf[:, rpar * (1024 + pad): rpar * (1024 + pad) + 1024 + pad]
                        for blk2 in range(2):
                            bank = [pA, pB, pC, pD][pp_k % 4]
                            cons = "v" if pp_k % 2 == 0 else "s"
                            waits = [("ls%d" % wpar, nld)]
                            if ec == 0 and wname == projs[0][0] and blk2 == 0:
                                waits += [xnt_bar[0], xnt_bar[1]]
                            if pp_k - 4 in pp_evac:
                                waits.append(pp_evac[pp_k - 4])
                            for hc in range(16):
                                P.add("t", "mm", bank[:, 0:512],
                                      wslab[:, wpar * 2048 + hc * 128: wpar * 2048 + (hc + 1) * 128],
                                      xnT[:].rearrange("p (hc t) -> p hc t", hc=16)[
                                          :, hc, blk2 * 512:(blk2 + 1) * 512],
                                      hc == 0, hc == 15,
                                      waits=waits if hc == 0 else (),
                                      incs=[("ft", 1)] if hc == 15 else ())
                            dst = rw[:, pad + blk2 * 512: pad + (blk2 + 1) * 512]
                            cn = "fv" if cons == "v" else "fs"
                            evw = [("ft", P.n("ft"))]
                            if rpar in conv_done_par:
                                evw.append(conv_done_par[rpar])
                            if own_grp:
                                if cons == "v":
                                    P.add(cons, "ts", dst, bank[:, 0:512], 1.0,
                                          bias_ap(bidx, ec), ALU.mult, ALU.add,
                                          waits=evw, incs=[(cn, 1)])
                                else:
                                    P.add(cons, "act", dst, bank[:, 0:512], ACTF.Identity,
                                          bias_ap(bidx, ec), 1.0,
                                          waits=evw, incs=[(cn, 1)])
                            else:
                                if cons == "v":
                                    P.add(cons, "ts", dst, bank[:, 0:512], maskv[:, 0:1],
                                          bm[:, bidx * 16 + ec: bidx * 16 + ec + 1],
                                          ALU.mult, ALU.add,
                                          waits=evw, incs=[(cn, 1)])
                                else:
                                    P.add(cons, "act", dst, bank[:, 0:512], ACTF.Identity,
                                          bm[:, bidx * 16 + ec: bidx * 16 + ec + 1],
                                          maskv[:, 0:1],
                                          waits=evw, incs=[(cn, 1)])
                            pp_evac[pp_k] = (cn, P.n(cn))
                            raw_done[wname] = (cn, P.n(cn))
                            pp_k += 1
                        if wname == "WqT" and grp == 2:
                            mw = [qts_bar]
                            if qmini_prev is not None:
                                mw.append(qmini_prev)
                            for hc in range(16):
                                P.add("t", "mm", pb["pXim"][0][:, 0:2],
                                      wslab[:, wpar * 2048 + hc * 128: wpar * 2048 + (hc + 1) * 128],
                                      qtsave[:, hc * 2:(hc + 1) * 2],
                                      hc == 0, hc == 15,
                                      waits=mw if hc == 0 else (),
                                      incs=[("ft", 1)] if hc == 15 else ())
                            qw0 = qraw[:, rpar * 1028: rpar * 1028 + 1028]
                            P.add("v", "ts", qw0[:, 2:4], pb["pXim"][0][:, 0:2],
                                  maskv[:, 0:1], bm[:, 0 * 16 + ec: 0 * 16 + ec + 1],
                                  ALU.mult, ALU.add,
                                  waits=[("ft", P.n("ft"))], incs=[("fv", 1)])
                            qmini_prev = ("fv", P.n("fv"))
                        wsl_done[wpar] = ("ft", P.n("ft"))
                    kw_ = kraw[:, rpar * 1028: rpar * 1028 + 1028]
                    vw_ = vraw[:, rpar * 1024: rpar * 1024 + 1024]
                    tw = [raw_done["WkT"], raw_done["WvT"]]
                    if own_grp:
                        tw.append(raw_done["WqT"])
                    if grp == 0:
                        P.add("v", "memset", kw_[:, 0:4], 0.0, waits=tw)
                    else:
                        P.add("v", "copy", kw_[:, 0:4], ktail[:, ec * 4:(ec + 1) * 4], waits=tw)
                    P.add("v", "ts", cscr[:, 0:1024], kw_[:, 2:1026], convw_ap(1, ec, 0),
                          bias_ap(5, ec), ALU.mult, ALU.add)
                    P.add("v", "stt", cscr[:, 1024:2048], kw_[:, 3:1027], convw_ap(1, ec, 1),
                          cscr[:, 0:1024], ALU.mult, ALU.add)
                    P.add("v", "stt", cscr[:, 0:1024], kw_[:, 4:1028], convw_ap(1, ec, 2),
                          cscr[:, 1024:2048], ALU.mult, ALU.add)
                    P.add("v", "copy", ktail[:, ec * 4:(ec + 1) * 4], kw_[:, 1024:1028])
                    kvw2 = [("kv%d" % rpar, kv_dma_par[rpar])] if rpar in kv_dma_par else []
                    P.add("v", "tt", kvec[:, rpar * 1024:(rpar + 1) * 1024],
                          cscr[:, 0:1024], vw_, ALU.mult, waits=kvw2, incs=[("fv", 1)])
                    kv_done = ("fv", P.n("fv"))
                    P.add("q", "dma",
                          kv_scr[ec * 128:(ec + 1) * 128, grp * 1024:(grp + 1) * 1024],
                          kvec[:, rpar * 1024:(rpar + 1) * 1024],
                          waits=[kv_done], incs=[("kv%d" % rpar, 16)])
                    kv_dma_par[rpar] = P.n("kv%d" % rpar)
                    if own_grp:
                        qw_ = qraw[:, rpar * 1028: rpar * 1028 + 1028]
                        if grp != 2:
                            P.add("v", "copy", qw_[:, 0:4], qtail[:, ec * 4:(ec + 1) * 4])
                        P.add("v", "ts", cscr[:, 0:1024], qw_[:, 2:1026], convw_ap(0, ec, 0),
                              bias_ap(4, ec), ALU.mult, ALU.add)
                        P.add("v", "stt", cscr[:, 1024:2048], qw_[:, 3:1027], convw_ap(0, ec, 1),
                              cscr[:, 0:1024], ALU.mult, ALU.add)
                        qcw2 = [("qc%d" % rpar, qc_dma_par[rpar])] if rpar in qc_dma_par else []
                        P.add("v", "stt", qcec[:, rpar * 1024:(rpar + 1) * 1024],
                              qw_[:, 4:1028], convw_ap(0, ec, 2),
                              cscr[:, 1024:2048], ALU.mult, ALU.add,
                              waits=qcw2, incs=[("fv", 1)])
                        P.add("v", "copy", qtail[:, ec * 4:(ec + 1) * 4], qw_[:, 1024:1028],
                              incs=[("fv", 1)])
                        P.add("q", "dma",
                              q_scr[ec * 128:(ec + 1) * 128, (grp - 2) * 1024:(grp - 1) * 1024],
                              qcec[:, rpar * 1024:(rpar + 1) * 1024],
                              waits=[("fv", P.n("fv"))], incs=[("qc%d" % rpar, 16)])
                        qc_dma_par[rpar] = P.n("qc%d" % rpar)
                    conv_done_par[rpar] = ("fv", P.n("fv"))
            kv_all = [(n, P.n(n)) for n in ("kv0", "kv1", "qc0", "qc1")]

        with ExitStack() as sc1:
            e1 = sc1.enter_context
            kert = [e1(nc.sbuf_tensor(f"kert{i}", [128, 2048], BF16)) for i in range(2)]
            p3_bar = [("ft", P.n("ft")), ("fv", P.n("fv")), ("fs", P.n("fs"))] + \
                list(kv_all) + [("xd0", P.n("xd0")), ("xd1", P.n("xd1"))]
            for i in range(2):
                P.add("g", "dma", kert[i][:], din["ker_sl"][i * 128:(i + 1) * 128, :],
                      waits=p3_bar if i == 0 else (), incs=[("gk", 16)])
            n_kert = P.n("gk")

            bufsA = _ConvBufs(nc, sc1)
            cv.b = bufsA
            cv.emit_ker(kert, ccK_in[:], extra_first=[("gk", n_kert), ("ld", n_const_ld)])
            nkq_ker = (P.n("kq0"), P.n("kq1"))

            wtile = e1(nc.sbuf_tensor("wtile", [128, 2 * 2048], F32))
            wq8 = e1(nc.sbuf_tensor("wq8", [128, 2048], FP8))
            wdq = e1(nc.sbuf_tensor("wdq", [128, 2048], BF16))
            wsc = e1(nc.sbuf_tensor("wsc", [128, 4 * 64], F32))
            wstage = e1(nc.sbuf_tensor("wstage", [128, 1024], BF16))

            def wtile_src(j):
                if j < 8:
                    return din["gw_s"][j * 128:(j + 1) * 128, :]
                if j < 16:
                    return din["uw_s"][(j - 8) * 128:(j - 8 + 1) * 128, :]
                jj = j - 16
                r, qq = jj // 4, jj % 4
                return din["dw_s"][r * 128:(r + 1) * 128, qq * 2048:(qq + 1) * 2048]

            def wtile_dst(j, hc4, stg):
                if j < 16:
                    base = 0 if j < 8 else 2048
                    col = (j % 8) * 128
                    rows = ccW_in[base + hc4 * 512: base + (hc4 + 1) * 512, col:col + 128]
                    return rows.rearrange("(b p) i -> p b i", p=128)
                jj = j - 16
                r, qq = jj // 4, jj % 4
                rows = ccW_in[4096 + hc4 * 512: 4096 + (hc4 + 1) * 512,
                              qq * 256 + r * 128: qq * 256 + (r + 1) * 128]
                return rows.rearrange("(b p) e -> p b e", p=128)

            wq_done = {}
            wstage_guard = {}
            wt_prev = None
            for j in range(24):
                par = j % 2
                lw = []
                if j == 0:
                    lw += list(kv_all) + [("xd0", P.n("xd0")), ("xd1", P.n("xd1"))]
                if par in wq_done:
                    lw.append(wq_done[par])
                P.add("q", "dma", wtile[:, par * 2048:(par + 1) * 2048], wtile_src(j),
                      waits=lw, incs=[("lwt%d" % par, 16)])
                nld = P.n("lwt%d" % par)
                src = wtile[:, par * 2048:(par + 1) * 2048]
                vw = [("lwt%d" % par, nld)]
                if wt_prev is not None:
                    vw.append(wt_prev)
                P.add("v", "red", wsc[:, 0:64], src.rearrange("p (b k) -> p b k", k=32),
                      ALU.max, True, waits=vw, incs=[("fv", 1)])
                P.add("v", "ts", wsc[:, 64:128], wsc[:, 0:64], 1.0 / 448.0, 1e-30,
                      ALU.mult, ALU.max, waits=[("fv", P.n("fv"))], incs=[("fv", 1)])
                P.add("v", "recip", wsc[:, 128:192], wsc[:, 64:128],
                      waits=[("fv", P.n("fv"))], incs=[("fv", 1)])
                P.add("v", "ts", wsc[:, 192:256], wsc[:, 64:128], 2.0, None, ALU.mult, None,
                      waits=[("fv", P.n("fv"))], incs=[("fv", 1)])
                P.add("v", "stt", wq8[:].rearrange("p (b k) -> p b k", k=32),
                      src.rearrange("p (b k) -> p b k", k=32), 0.5,
                      wsc[:, 128:192].unsqueeze(-1).broadcast_to([128, 64, 32]),
                      ALU.mult, ALU.mult, waits=[("fv", P.n("fv"))])
                P.add("v", "tt", wdq[:].rearrange("p (b k) -> p b k", k=32),
                      wq8[:].rearrange("p (b k) -> p b k", k=32),
                      wsc[:, 192:256].unsqueeze(-1).broadcast_to([128, 64, 32]),
                      ALU.mult, incs=[("fv", 1)])
                wq_done[par] = ("fv", P.n("fv"))
                for hc4 in range(4):
                    gslot = cv.ptr_grp % 2
                    stg = wstage[:, gslot * 512:(gslot + 1) * 512]
                    trs = [wdq[:, (hc4 * 4 + i2) * 128:(hc4 * 4 + i2 + 1) * 128]
                           for i2 in range(4)]
                    wsem = "wd%d" % gslot
                    ee = [(wsem, wstage_guard[gslot])] if gslot in wstage_guard else []
                    cv.tr_group(trs, stg, "s", extra_waits=[wq_done[par]], evac_extra=ee)
                    P.add("q", "dma", wtile_dst(j, hc4, stg),
                          stg.rearrange("p (b f) -> p b f", b=4),
                          waits=[cv.ptr_evac[cv.ptr_grp - 1]], incs=[(wsem, 16)])
                    wstage_guard[gslot] = P.n(wsem)
                wt_prev = ("ft", P.n("ft"))
            n_wd = (P.n("wd0"), P.n("wd1"))

        P.add("g", "coll", "AllGather", ALU.bypass, [[0, 1, 2, 3, 4, 5, 6, 7]],
              [ccK_in[:]], [ccK_out[:]],
              waits=[("kq0", nkq_ker[0]), ("kq1", nkq_ker[1])], incs=[("ck", 1)])
        P.add("g", "coll", "AllGather", ALU.bypass, [[0, 1, 2, 3, 4, 5, 6, 7]],
              [ccW_in[:]], [ccW_out[:]],
              waits=[("wd0", n_wd[0]), ("wd1", n_wd[1])], incs=[("cw", 1)])

        with ExitStack() as sc4:
            bufsB = _ConvBufs(nc, sc4)
            cv.b = bufsB
            kvt = [sc4.enter_context(nc.sbuf_tensor(f"kvt{i}", [128, 4096], BF16))
                   for i in range(2)]
            ych = sc4.enter_context(nc.sbuf_tensor("ych", [128, 2 * 2048], BF16))
            for ch in range(8):
                lw = list(kv_all)
                if ch > 0:
                    lw.append(("ft", P.n("ft")))
                else:
                    lw += [("ft", P.n("ft")), ("wd0", n_wd[0]), ("wd1", n_wd[1]),
                           ("kq0", nkq_ker[0]), ("kq1", nkq_ker[1])]
                for i in range(2):
                    P.add("q", "dma", kvt[i][:],
                          kv_scr[(2 * ch + i) * 128:(2 * ch + i + 1) * 128, :],
                          waits=lw if i == 0 else (), incs=[("kt", 16)])
                nld = P.n("kt")
                ef = [("kt", nld)] + ([("st", 16 * ch)] if ch > 0 else [])
                cv.strided_tr(kvt, 2, extra_first=ef)
                _emit_kv_rest(P, cv, ccK_out[ch], ych[:], k_wait=("ck", 1))
                P.add("q", "dma", y_scr[2 * ch * 128:(2 * ch + 2) * 128, :].rearrange(
                    "(cq p) t -> p cq t", cq=2), ych[:],
                    waits=[("fv", P.n("fv")), ("fs", P.n("fs"))], incs=[("st", 16)])
            n_yst = P.n("st")

        with ExitStack() as sc5:
            e5 = sc5.enter_context
            yblk = e5(nc.sbuf_tensor("yblk", [128, 16 * 512], BF16))
            kvob = e5(nc.sbuf_tensor("kvob", [128, 16 * 512], BF16))
            qcb = e5(nc.sbuf_tensor("qcb", [128, 16 * 512], BF16))
            yTb = e5(nc.sbuf_tensor("yTb", [128, 16 * 512], BF16))
            atb = e5(nc.sbuf_tensor("atb", [128, 16 * 512], BF16))
            wslab5 = e5(nc.sbuf_tensor("wslab5", [128, 2 * 2048], BF16))
            cscr5 = e5(nc.sbuf_tensor("cscr5", [128, 512], F32))
            wsl_done5 = {}
            wo_evac = {}
            p5_prev = None
            for blk in range(4):
                t0 = blk * 512
                lw = [("st", n_yst)]
                if p5_prev is not None:
                    lw.append(p5_prev)
                P.add("q", "dma", yblk[:].rearrange("p (hc t) -> p hc t", hc=16),
                      y_scr[:, t0:t0 + 512].rearrange("(hc p) t -> p hc t", p=128),
                      waits=lw, incs=[("l5", 16)])
                P.add("q", "dma", kvob[:].rearrange("p (hc t) -> p hc t", hc=16),
                      kv_scr[:, 2048 + t0: 2048 + t0 + 512].rearrange("(hc p) t -> p hc t", p=128),
                      incs=[("l5", 16)])
                P.add("q", "dma", qcb[:].rearrange("p (hc t) -> p hc t", hc=16),
                      q_scr[:, t0:t0 + 512].rearrange("(hc p) t -> p hc t", p=128),
                      incs=[("l5", 16)])
                nld = P.n("l5")
                for ec in range(16):
                    sl = slice(ec * 512, (ec + 1) * 512)
                    vw2 = [("l5", nld)] if ec == 0 else []
                    if ec == 0 and p5_prev is not None:
                        vw2.append(p5_prev)
                    P.add("v", "stt", cscr5[:, 0:512], kvob[:, sl],
                          bias_ap(6, ec), yblk[:, sl], ALU.mult, ALU.add, waits=vw2)
                    P.add("v", "tt", yTb[:, sl], cscr5[:, 0:512], qcb[:, sl], ALU.mult,
                          incs=[("fv", 1)] if ec == 15 else ())
                ybar = ("fv", P.n("fv"))
                for ec in range(16):
                    wpar = ec % 2
                    lw2 = []
                    if blk == 0 and ec == 0:
                        lw2.append(("gd", n_wcast))
                    if wpar in wsl_done5:
                        lw2.append(wsl_done5[wpar])
                    P.add("q", "dma", wslab5[:, wpar * 2048:(wpar + 1) * 2048].rearrange(
                        "p (hc f) -> p hc f", hc=16),
                        wbf["WoT"][:, ec * 128:(ec + 1) * 128].rearrange(
                        "(hc p) f -> p hc f", p=128), waits=lw2, incs=[("l5w%d" % wpar, 16)])
                    nld2 = P.n("l5w%d" % wpar)
                    bank = pA if ec % 2 == 0 else pB
                    cons = "v" if ec % 2 == 0 else "s"
                    waits = [("l5w%d" % wpar, nld2), ybar]
                    if ec - 2 in wo_evac:
                        waits.append(wo_evac[ec - 2])
                    for hc in range(16):
                        P.add("t", "mm", bank[:, 0:512],
                              wslab5[:, wpar * 2048 + hc * 128: wpar * 2048 + (hc + 1) * 128],
                              yTb[:, hc * 512:(hc + 1) * 512],
                              hc == 0, hc == 15,
                              waits=waits if hc == 0 else (),
                              incs=[("ft", 1)] if hc == 15 else ())
                    cn = "fv" if cons == "v" else "fs"
                    ew2 = [("ft", P.n("ft"))]
                    if p5_prev is not None and ec < 2:
                        ew2.append(p5_prev)
                    if cons == "v":
                        P.add(cons, "ts", atb[:, ec * 512:(ec + 1) * 512], bank[:, 0:512], 1.0,
                              bias_ap(3, ec), ALU.mult, ALU.add,
                              waits=ew2, incs=[(cn, 1)])
                    else:
                        P.add(cons, "act", atb[:, ec * 512:(ec + 1) * 512], bank[:, 0:512],
                              ACTF.Identity, bias_ap(3, ec), 1.0,
                              waits=ew2, incs=[(cn, 1)])
                    wo_evac[ec] = (cn, P.n(cn))
                    wsl_done5[wpar] = ("ft", P.n("ft"))
                wo_evac = {}
                P.add("q", "dma", attn_scr[:, t0:t0 + 512].rearrange("(hc p) t -> p hc t", p=128),
                      atb[:].rearrange("p (hc t) -> p hc t", hc=16),
                      waits=[("fv", P.n("fv")), ("fs", P.n("fs"))], incs=[("at", 16)])
                p5_prev = ("at", P.n("at"))
            n_at = P.n("at")

        with ExitStack() as sc6:
            e6 = sc6.enter_context
            xdqT = e6(nc.sbuf_tensor("xdqT", [128, 16 * 2048], BF16))
            P.add("q", "dma", xdqT[:].rearrange("p (hc t) -> p hc t", hc=16),
                  xdq_scr[:].rearrange("(hc p) t -> p hc t", p=128),
                  waits=[("xd0", P.n("xd0")), ("xd1", P.n("xd1"))], incs=[("lx", 16)])
            n_xdqt = P.n("lx")
            islab = e6(nc.sbuf_tensor("islab", [128, 4 * 8192], BF16))
            hsil = e6(nc.sbuf_tensor("hsil", [128, 512], BF16))
            hf = e6(nc.sbuf_tensor("hf", [128, 512], F32))
            hq8 = e6(nc.sbuf_tensor("hq8", [128, 512], FP8))
            hdq = e6(nc.sbuf_tensor("hdq", [128, 2 * 512], BF16))
            hsc = e6(nc.sbuf_tensor("hsc", [128, 4 * 16], F32))
            hstage = e6(nc.sbuf_tensor("hstage", [128, 1024], BF16))
            isl_done = {}
            hstage_guard = {}
            h_done = {}
            hf_done = {}
            tr_done = {}
            pend_tr = None

            def h_tr(kk, ic, ts_):
                gslot = cv.ptr_grp % 2
                stg = hstage[:, gslot * 512:(gslot + 1) * 512]
                hp = kk % 2
                trs = [hdq[:, hp * 512 + i2 * 128: hp * 512 + (i2 + 1) * 128]
                       for i2 in range(4)]
                hsem = "hd%d" % gslot
                ee = [(hsem, hstage_guard[gslot])] if gslot in hstage_guard else []
                cv.tr_group(trs, stg, "s", extra_waits=[h_done[kk]], evac_extra=ee)
                tr_done[kk] = ("ft", P.n("ft"))
                dest = h_scr[ic * 512:(ic + 1) * 512, ts_ * 128:(ts_ + 1) * 128].rearrange(
                    "(b p) t -> p b t", p=128)
                P.add("q", "dma", dest, stg.rearrange("p (b f) -> p b f", b=4),
                      waits=[cv.ptr_evac[cv.ptr_grp - 1]], incs=[(hsem, 16)])
                hstage_guard[gslot] = P.n(hsem)

            for ic in range(16):
                ipar = ic % 2
                rank, li = ic // 2, (ic % 2) * 512
                lw = [("cw", 1)]
                if ipar in isl_done:
                    lw.append(isl_done[ipar])
                P.add("q", "dma", islab[:, ipar * 16384: ipar * 16384 + 8192].rearrange(
                    "p (hc f) -> p hc f", hc=16),
                    ccW_out[rank, 0:2048, li:li + 512].rearrange("(hc p) f -> p hc f", p=128),
                    waits=lw, incs=[("li%d" % ipar, 16)])
                P.add("q", "dma", islab[:, ipar * 16384 + 8192: ipar * 16384 + 16384].rearrange(
                    "p (hc f) -> p hc f", hc=16),
                    ccW_out[rank, 2048:4096, li:li + 512].rearrange("(hc p) f -> p hc f", p=128),
                    incs=[("li%d" % ipar, 16)])
                nld = P.n("li%d" % ipar)
                for ts_ in range(16):
                    kk = ic * 16 + ts_
                    bank = [pC, pD, pA][kk % 3]
                    bank2 = [pE, pF, pB][kk % 3]
                    waits = [("li%d" % ipar, nld), ("lx", n_xdqt)]
                    if kk < 3:
                        waits.append(("at", n_at))
                    if kk - 3 in h_done:
                        waits.append(h_done[kk - 3])
                    for hc in range(16):
                        P.add("t", "mm", bank[:, 0:512],
                              xdqT[:].rearrange("p (hc t) -> p hc t", hc=16)[
                                  :, hc, ts_ * 128:(ts_ + 1) * 128],
                              islab[:, ipar * 16384 + hc * 512: ipar * 16384 + (hc + 1) * 512],
                              hc == 0, hc == 15,
                              waits=waits if hc == 0 else ())
                    for hc in range(16):
                        P.add("t", "mm", bank2[:, 0:512],
                              xdqT[:].rearrange("p (hc t) -> p hc t", hc=16)[
                                  :, hc, ts_ * 128:(ts_ + 1) * 128],
                              islab[:, ipar * 16384 + 8192 + hc * 512:
                                    ipar * 16384 + 8192 + (hc + 1) * 512],
                              hc == 0, hc == 15,
                              incs=[("ft", 1)] if hc == 15 else ())
                    nftg = P.n("ft")
                    silw = [("ft", nftg)]
                    if kk - 1 in hf_done:
                        silw.append(hf_done[kk - 1])
                    P.add("s", "act", hsil[:], bank[:, 0:512], ACTF.Silu, 0.0, 1.0,
                          waits=silw, incs=[("fs", 1)])
                    hfw = [("fs", P.n("fs"))]
                    if kk - 2 in tr_done:
                        hfw.append(tr_done[kk - 2])
                    P.add("v", "tt", hf[:], hsil[:], bank2[:, 0:512], ALU.mult,
                          waits=hfw, incs=[("fv", 1)])
                    hf_done[kk] = ("fv", P.n("fv"))
                    P.add("v", "red", hsc[:, 0:16], hf[:].rearrange("p (b k) -> p b k", k=32),
                          ALU.max, True, incs=[("fv", 1)])
                    P.add("v", "ts", hsc[:, 16:32], hsc[:, 0:16], 1.0 / 448.0, 1e-30,
                          ALU.mult, ALU.max, waits=[("fv", P.n("fv"))], incs=[("fv", 1)])
                    P.add("v", "recip", hsc[:, 32:48], hsc[:, 16:32],
                          waits=[("fv", P.n("fv"))], incs=[("fv", 1)])
                    P.add("v", "ts", hsc[:, 48:64], hsc[:, 16:32], 2.0, None, ALU.mult, None,
                          waits=[("fv", P.n("fv"))], incs=[("fv", 1)])
                    hp = kk % 2
                    P.add("v", "stt", hq8[:].rearrange("p (b k) -> p b k", k=32),
                          hf[:].rearrange("p (b k) -> p b k", k=32), 0.5,
                          hsc[:, 32:48].unsqueeze(-1).broadcast_to([128, 16, 32]),
                          ALU.mult, ALU.mult, waits=[("fv", P.n("fv"))])
                    P.add("v", "tt",
                          hdq[:, hp * 512:(hp + 1) * 512].rearrange("p (b k) -> p b k", k=32),
                          hq8[:].rearrange("p (b k) -> p b k", k=32),
                          hsc[:, 48:64].unsqueeze(-1).broadcast_to([128, 16, 32]),
                          ALU.mult, incs=[("fv", 1)])
                    h_done[kk] = ("fv", P.n("fv"))
                    if pend_tr is not None:
                        h_tr(*pend_tr)
                    pend_tr = (kk, ic, ts_)
                isl_done[ipar] = ("ft", P.n("ft"))
            h_tr(*pend_tr)
            n_hd = (P.n("hd0"), P.n("hd1"))

        with ExitStack() as sc7:
            e7 = sc7.enter_context
            hslab = e7(nc.sbuf_tensor("hslab", [128, 64 * 512], BF16))
            dwsl = e7(nc.sbuf_tensor("dwsl", [128, 2 * 16384], BF16))
            uTs = e7(nc.sbuf_tensor("uTs", [128, 4 * 512], F32))
            atl = e7(nc.sbuf_tensor("atl", [128, 4 * 512], BF16))
            ofin = e7(nc.sbuf_tensor("ofin", [128, 2 * 512], F32))
            dw_done = {}
            fin_done = {}
            st_par = {}
            ua_done = {}
            for tc in range(4):
                t0 = tc * 512
                lw = [("hd0", n_hd[0]), ("hd1", n_hd[1]), ("at", n_at)]
                if tc > 0:
                    lw.append(("ft", P.n("ft")))
                    lw.append(("fv", P.n("fv")))
                    lw.append(("fs", P.n("fs")))
                P.add("q", "dma", hslab[:].rearrange("p (ic t) -> p ic t", ic=64),
                      h_scr[:, t0:t0 + 512].rearrange("(ic p) t -> p ic t", p=128),
                      waits=lw, incs=[("l6", 16)])
                for ec in range(16):
                    epar = ec % 2
                    erank = ec // 2
                    wpar = erank % 2
                    if epar == 0:
                        lw2 = []
                        if wpar in dw_done:
                            lw2.append(dw_done[wpar])
                        P.add("q", "dma",
                              dwsl[:, wpar * 16384:(wpar + 1) * 16384].rearrange(
                                  "p (icl c) -> p icl c", icl=16),
                              ccW_out[erank, 4096:6144, :].rearrange(
                                  "(icl p) c -> p icl c", p=128),
                              waits=lw2, incs=[("l6w%d" % wpar, 16)])
                    nld2 = P.n("l6w%d" % wpar)
                    uslot = ec % 4
                    uw2 = [ua_done[uslot]] if uslot in ua_done else []
                    P.add("q", "dma", uTs[:, uslot * 512:(uslot + 1) * 512],
                          uT_own[ec * 128:(ec + 1) * 128, t0:t0 + 512],
                          waits=uw2, incs=[("l6", 16)])
                    P.add("q", "dma", atl[:, uslot * 512:(uslot + 1) * 512],
                          attn_scr[ec * 128:(ec + 1) * 128, t0:t0 + 512],
                          incs=[("l6", 16)])
                    nldu = P.n("l6")
                    bank = pA if epar == 0 else pB
                    waits = [("l6", nldu), ("l6w%d" % wpar, nld2)]
                    if epar in fin_done:
                        waits.append(fin_done[epar])
                    first = True
                    for m in range(4):
                        for icl in range(16):
                            off = wpar * 16384 + icl * 1024 + m * 256 + epar * 128
                            ih = m * 16 + icl
                            P.add("t", "mm", bank[:, 0:512], dwsl[:, off: off + 128],
                                  hslab[:, ih * 512:(ih + 1) * 512],
                                  first, (m == 3 and icl == 15),
                                  waits=waits if first else (),
                                  incs=[("ft", 1)] if (m == 3 and icl == 15) else ())
                            first = False
                    of = ofin[:, epar * 512:(epar + 1) * 512]
                    ow = [("ft", P.n("ft"))]
                    if epar in st_par:
                        ow.append(("so%d" % epar, st_par[epar]))
                    P.add("v", "tt", of, bank[:, 0:512], atl[:, uslot * 512:(uslot + 1) * 512],
                          ALU.add, waits=ow)
                    P.add("v", "tt", of, of, uTs[:, uslot * 512:(uslot + 1) * 512], ALU.add,
                          incs=[("fv", 1)])
                    fin_done[epar] = ("fv", P.n("fv"))
                    ua_done[uslot] = ("fv", P.n("fv"))
                    P.add("q", "dma", outT[ec * 128:(ec + 1) * 128, t0:t0 + 512], of,
                          waits=[fin_done[epar]], incs=[("so%d" % epar, 16)])
                    st_par[epar] = P.n("so%d" % epar)
                    if epar == 1:
                        dw_done[wpar] = ("ft", P.n("ft"))

        P.add("g", "memset", biases[0:1, 0:8], 0.0,
              waits=[("so0", P.n("so0")), ("so1", P.n("so1")), ("st", P.n("st"))])
        for name, (scr, dout) in dbg.items():
            P.add("g", "dma", dout[:], scr[:], incs=[("db", 16)])
        if dbg:
            P.add("g", "memset", biases[0:1, 8:16], 0.0, waits=[("db", P.n("db"))])

        _run_block(nc, P)
    return nc


def _emit_kv_rest(P, cv, kin, ych_ap, k_wait):
    b, c = cv.b, cv.c
    pXre, pXim = cv.pb["pXre"], cv.pb["pXim"]
    pA, pB = cv.pb["pA"], cv.pb["pB"]
    bar = cv.stage1(2)
    x_prev = {}
    z_done = {}
    st_state = {"u_prev": None}
    usb_turn = {}
    ustage_prev = {}
    prev_pair = None

    def stageA_turn2(pp):
        ppar = pp % 2
        Zp = lambda im: b.Z[0:72, ppar * 1024 + im * 512: ppar * 1024 + im * 512 + 512]
        uw = [z_done[ppar]]
        if st_state["u_prev"] is not None:
            uw.append(st_state["u_prev"])
        P.add("t", "mm", pA[:, 0:512], c.BDAre[0:72, 0:128], Zp(0), True, False, waits=uw)
        P.add("t", "mm", pA[:, 0:512], c.BDAim_neg[0:72, 0:128], Zp(1), False, True)
        P.add("t", "mm", pB[:, 0:512], c.BDAim[0:72, 0:128], Zp(0), True, False)
        P.add("t", "mm", pB[:, 0:512], c.BDAre[0:72, 0:128], Zp(1), False, True,
              incs=[("ft", 1)])
        ustage_prev[ppar] = ("ft", P.n("ft"))
        sw = [("ft", P.n("ft"))]
        if ppar in usb_turn:
            sw.append(usb_turn[ppar])
        P.add("s", "copy", b.Usb[:, ppar * 1024: ppar * 1024 + 512], pA[:, 0:512], waits=sw)
        P.add("s", "act", b.Usb[:, ppar * 1024 + 512: ppar * 1024 + 1024], pB[:, 0:512],
              ACTF.Copy, 0.0, -1.0, incs=[("fs", 1)])
        st_state["u_prev"] = ("fs", P.n("fs"))
        trs = []
        for ac in range(2):
            for im in range(2):
                for lg in range(2):
                    off = ppar * 1024 + im * 512 + lg * 256 + ac * 128
                    trs.append(b.Usb[:, off: off + 128])
        g0 = pp * 2
        dest = b.UT[:].rearrange("p (aci gg f) -> p aci gg f", aci=4, gg=32)[:, :, g0:g0 + 2, :]
        cv.tr_group(trs, dest, "v" if ppar == 0 else "s",
                    extra_waits=[st_state["u_prev"]])
        usb_turn[ppar] = ("ft", P.n("ft"))

    for pair in range(16):
        par = pair % 2
        kw = [k_wait]
        if par in z_done:
            kw.append(z_done[par])
        ks = "kk%d" % par
        P.add("q", "dma", b.Kbuf[0:72, par * 1024: par * 1024 + 1024],
              kin[pair], waits=kw, incs=[(ks, 16)])
        nkq = P.n(ks)
        ew = [("fv", bar[0]), ("fs", bar[1])] if pair < 2 else ()
        cv.pair_turn1(pair, extra_waits=ew)
        nft = cv.pair_stage2(pair, x_prev)
        sw = [("ft", nft)]
        if par in z_done:
            sw.append(z_done[par])
        P.add("s", "copy", b.tmp[0:72, par * 1024: par * 1024 + 512],
              pXre[par][0:72, 0:512], waits=sw)
        P.add("s", "copy", b.tmp[0:72, par * 1024 + 512: par * 1024 + 1024],
              pXim[par][0:72, 0:512], incs=[("fs", 1)])
        nfs = P.n("fs")
        x_prev[par] = ("fs", nfs)
        K = lambda im: b.Kbuf[0:72, par * 1024 + im * 512: par * 1024 + im * 512 + 512]
        Z = lambda im: b.Z[0:72, par * 1024 + im * 512: par * 1024 + im * 512 + 512]
        tre = b.tmp[0:72, par * 1024: par * 1024 + 512]
        tim = b.tmp[0:72, par * 1024 + 512: par * 1024 + 1024]
        zA, zB = b.tmp2[0:72, 0:512], b.tmp2[0:72, 512:1024]
        vw = [("fs", nfs), (ks, nkq)]
        if par in ustage_prev:
            vw.append(ustage_prev[par])
        P.add("v", "tt", zA, tre, K(0), ALU.mult, waits=vw)
        P.add("v", "tt", zB, tim, K(1), ALU.mult)
        P.add("v", "tt", Z(0), zA, zB, ALU.subtract)
        P.add("v", "tt", zA, tre, K(1), ALU.mult)
        P.add("v", "tt", zB, tim, K(0), ALU.mult)
        P.add("v", "tt", Z(1), zA, zB, ALU.add, incs=[("fv", 1)])
        z_done[par] = ("fv", P.n("fv"))
        if prev_pair is not None:
            stageA_turn2(prev_pair)
        prev_pair = pair
    stageA_turn2(prev_pair)
    bar2 = (("fv", P.n("fv")), ("fs", P.n("fs")))
    py_prev = {}
    tb_done = {}
    for nu in range(16):
        par = nu % 2
        bank = pXre[par]
        w = [bar2[0], bar2[1]] if nu < 2 else []
        if par in py_prev:
            w.append(py_prev[par])
        if nu < 2 and par in x_prev:
            w.append(x_prev[par])
        first = True
        for ac in range(2):
            hoff = (nu * 2 + ac) * 128
            base_re = (ac * 2 + 0) * 4096
            base_im = (ac * 2 + 1) * 4096
            UTre = b.UT[:, base_re:base_re + 4096].rearrange(
                "p (g c2 n2) -> p g c2 n2", c2=8, n2=16)[:, :, :, nu]
            UTim = b.UT[:, base_im:base_im + 4096].rearrange(
                "p (g c2 n2) -> p g c2 n2", c2=8, n2=16)[:, :, :, nu]
            P.add("t", "mm", bank[:, 0:256], c.Hre[:, hoff:hoff + 128], UTre,
                  first, False, waits=w if first else ())
            first = False
            P.add("t", "mm", bank[:, 0:256], c.Him[:, hoff:hoff + 128], UTim,
                  False, ac == 1, incs=[("ft", 1)] if ac == 1 else ())
        cons = "v" if par == 0 else "s"
        cn = "fv" if par == 0 else "fs"
        yw = [("ft", P.n("ft"))]
        if par in tb_done:
            yw.append(tb_done[par])
        P.add(cons, "copy", b.ysb[:, par * 256: par * 256 + 256], bank[:, 0:256],
              waits=yw, incs=[(cn, 1)])
        py_prev[par] = (cn, P.n(cn))
        trs = [b.ysb[:, par * 256 + cq * 128: par * 256 + (cq + 1) * 128]
               for cq in range(2)]
        dest = ych_ap.rearrange("p (cq t) -> p cq t", cq=2)[:, :, nu:2048:16]
        cv.tr_group(trs, dest, "v" if par == 0 else "s", extra_waits=[py_prev[par]])
        tb_done[par] = ("ft", P.n("ft"))



_CACHE = {}


def _get_nc():
    if "nc" not in _CACHE:
        _CACHE["nc"] = _build(_DEBUG)
    return _CACHE["nc"]


def _chunkvec(v):
    return np.ascontiguousarray(np.asarray(v, np.float32).reshape(16, 128).T)


def _chunkw3(v):
    return np.ascontiguousarray(
        np.asarray(v, np.float32).reshape(16, 128, 3).transpose(1, 0, 2).reshape(128, 48))


def kernel(**inputs):
    u = np.asarray(inputs["L0_attn_u"], dtype=np.float32)
    fc = _fft_constants()
    shared = {
        "WqT": np.ascontiguousarray(np.asarray(inputs["L0_attn_Wq"], np.float32).T),
        "WkT": np.ascontiguousarray(np.asarray(inputs["L0_attn_Wk"], np.float32).T),
        "WvT": np.ascontiguousarray(np.asarray(inputs["L0_attn_Wv"], np.float32).T),
        "WoT": np.ascontiguousarray(np.asarray(inputs["L0_attn_Wo"], np.float32).T),
        "bq": _chunkvec(inputs["L0_attn_bq"]),
        "bk": _chunkvec(inputs["L0_attn_bk"]),
        "bv": _chunkvec(inputs["L0_attn_bv"]),
        "bo": _chunkvec(inputs["L0_attn_bo"]),
        "cbq": _chunkvec(inputs["L0_attn_convq_b"]),
        "cbk": _chunkvec(inputs["L0_attn_convk_b"]),
        "dskip": _chunkvec(inputs["L0_attn_D_skip"]),
        "cwq": _chunkw3(inputs["L0_attn_convq_w"]),
        "cwk": _chunkw3(inputs["L0_attn_convk_w"]),
    }
    for n, a in fc.items():
        shared["c_" + n] = a
    ker = np.asarray(inputs["L0_attn_ssm_kernel"], np.float32)
    gw = np.asarray(inputs["L0_ffn_gate_w"], np.float32)
    uw = np.asarray(inputs["L0_ffn_up_w"], np.float32)
    dw = np.asarray(inputs["L0_ffn_down_w"], np.float32)
    zeros = np.zeros((TOK, D), np.float32)
    in_maps = []
    for core in range(8):
        s, h = core // 2, core % 2
        m = dict(shared)
        uo = np.ascontiguousarray(u[s, h * TOK:(h + 1) * TOK, :])
        m["u_own"] = uo
        m["u_ctx"] = np.ascontiguousarray(u[s, 0:TOK, :]) if h == 1 else uo
        m["uT_own"] = np.ascontiguousarray(uo.T)
        m["ctxm"] = np.full((128, 1), float(h), np.float32)
        m["ker_sl"] = np.ascontiguousarray(ker[core * 256:(core + 1) * 256, :])
        m["gw_s"] = np.ascontiguousarray(gw[core * 1024:(core + 1) * 1024, :])
        m["uw_s"] = np.ascontiguousarray(uw[core * 1024:(core + 1) * 1024, :])
        m["dw_s"] = np.ascontiguousarray(dw[core * 256:(core + 1) * 256, :])
        in_maps.append(m)
    nc = _get_nc()
    res = run_bass_kernel_spmd(nc, in_maps, core_ids=list(range(8)))
    out = np.empty((4, 4096, D), np.float32)
    for core in range(8):
        s, h = core // 2, core % 2
        out[s, h * TOK:(h + 1) * TOK, :] = res.results[core]["outT"].T
    if _DEBUG:
        kernel._dbg = [{k: v for k, v in r.items() if k.startswith("dbg_")}
                       for r in res.results]
    return out


# version 06
# speedup vs baseline: 1.2927x; 1.0243x over previous
import numpy as np
import ml_dtypes
import concourse.bass as bass
import concourse.mybir as mybir
from concourse.bass_utils import run_bass_kernel_spmd
from contextlib import ExitStack

F32 = mybir.dt.float32
BF16 = mybir.dt.bfloat16
FP8 = mybir.dt.float8e4
ALU = mybir.AluOpType
AX = mybir.AxisListType.X
ACTF = mybir.ActivationFunctionType
BF = ml_dtypes.bfloat16

_DEBUG = ()

D = 2048
TOK = 2048
WIN = 4096
I_FF = 8192



def _fft_constants():
    Gre = np.zeros((128, 16 * 2 * 2 * 128))
    Gim = np.zeros((128, 16 * 2 * 2 * 128))
    r_r = np.arange(256)
    a_r = np.arange(256)
    for s in range(16):
        ph = np.exp(-2j * np.pi * (np.outer(r_r, a_r) / 256.0 + s * a_r[None, :] / 4096.0))
        for rc in range(2):
            for ac in range(2):
                off = ((s * 2 + rc) * 2 + ac) * 128
                Gre[:, off:off + 128] = ph[rc * 128:(rc + 1) * 128, ac * 128:(ac + 1) * 128].real
                Gim[:, off:off + 128] = ph[rc * 128:(rc + 1) * 128, ac * 128:(ac + 1) * 128].imag
    BD2 = np.zeros((128, 72), np.complex128)
    for c in range(8):
        for s in range(16):
            for b in range(9):
                BD2[c * 16 + s, c * 9 + b] = np.exp(-2j * np.pi * s * b / 16.0)
    BDA = np.zeros((72, 128), np.complex128)
    for c in range(8):
        for b in range(9):
            for nu in range(16):
                BDA[c * 9 + b, c * 16 + nu] = np.exp(2j * np.pi * b * nu / 16.0)
    Hre = np.zeros((128, 16 * 2 * 128))
    Him = np.zeros((128, 16 * 2 * 128))
    for nu in range(16):
        for ac in range(2):
            a = (ac * 128 + np.arange(128))[:, None]
            eta = (128 + np.arange(128))[None, :]
            ph = np.exp(2j * np.pi * (a * nu / 4096.0 + a * eta / 256.0))
            off = (nu * 2 + ac) * 128
            Hre[:, off:off + 128] = ph.real
            Him[:, off:off + 128] = ph.imag
    cm = np.zeros((72, 256))
    for c in range(8):
        for b in range(9):
            if b < 8:
                w = np.full(256, 2.0)
                if b == 0:
                    w[0] = 1.0
            else:
                w = np.zeros(256)
                w[0] = 1.0
            cm[c * 9 + b] = w / 4096.0
    return {
        "Gre": Gre.astype(BF), "Gim": Gim.astype(BF),
        "BD2re": np.ascontiguousarray(BD2.real).astype(BF),
        "BD2im": np.ascontiguousarray(BD2.imag).astype(BF),
        "BD2im_neg": np.ascontiguousarray(-BD2.imag).astype(BF),
        "BDAre": np.ascontiguousarray(BDA.real).astype(BF),
        "BDAim": np.ascontiguousarray(BDA.imag).astype(BF),
        "BDAim_neg": np.ascontiguousarray(-BDA.imag).astype(BF),
        "Hre": Hre.astype(BF), "Him": Him.astype(BF),
        "cmask": cm.astype(np.float32),
        "identity": np.eye(128).astype(BF),
    }



class _Plan:
    def __init__(self, nc):
        self.nc = nc
        self.steps = {e: [] for e in ("t", "v", "s", "g", "q")}
        self.sems = {}
        self.counts = {}

    def sem(self, name):
        if name not in self.sems:
            self.sems[name] = self.nc.alloc_semaphore("sem_" + name)
            self.counts[name] = 0
        return self.sems[name]

    def n(self, name):
        self.sem(name)
        return self.counts[name]

    def add(self, eng, kind, *args, waits=(), incs=()):
        w = [(self.sem(nm), v) for nm, v in waits]
        inc = []
        for nm, amt in incs:
            inc.append((self.sem(nm), amt))
            self.counts[nm] += amt
        self.steps[eng].append((kind, args, w, inc))

    def replay(self, key, eng):
        for kind, args, waits, incs in self.steps[key]:
            for sem, val in waits:
                eng.wait_ge(sem, val)
            ins = self._emit(eng, kind, args)
            for sem, amt in incs:
                ins.then_inc(sem, amt)

    @staticmethod
    def _emit(eng, kind, a):
        if kind == "mm":
            out, lhsT, rhs, start, stop = a
            return eng.matmul(out, lhsT, rhs, start=start, stop=stop)
        if kind == "tr":
            out, in_, ident = a
            return eng.transpose(out, in_, ident)
        if kind == "copy":
            out, in_ = a
            if hasattr(eng, "tensor_copy"):
                return eng.tensor_copy(out, in_)
            return eng.copy(out, in_)
        if kind == "tt":
            out, in0, in1, op = a
            return eng.tensor_tensor(out, in0, in1, op=op)
        if kind == "ts":
            out, in0, s1, s2, op0, op1 = a
            if op1 is None:
                return eng.tensor_scalar(out, in0, s1, None, op0=op0)
            return eng.tensor_scalar(out, in0, s1, s2, op0=op0, op1=op1)
        if kind == "stt":
            out, in0, scal, in1, op0, op1 = a
            return eng.scalar_tensor_tensor(out, in0, scal, in1, op0=op0, op1=op1)
        if kind == "act":
            out, in_, func, bias, scale = a
            return eng.activation(out, in_, func, bias=bias, scale=scale)
        if kind == "red":
            out, in_, op, absval = a
            return eng.tensor_reduce(out, in_, axis=AX, op=op, apply_absolute_value=absval)
        if kind == "bnst":
            out, in_ = a
            return eng.bn_stats(out, in_)
        if kind == "bnag":
            out, in_ = a
            return eng.bn_aggr(out, in_)
        if kind == "dma":
            out, in_ = a
            return eng.dma_start(out, in_)
        if kind == "memset":
            ap, val = a
            return eng.memset(ap, val)
        if kind == "recip":
            out, in_ = a
            return eng.reciprocal(out, in_)
        if kind == "coll":
            kind_s, op, groups, ins_, outs_ = a
            return eng.collective_compute(kind_s, op, replica_groups=groups,
                                          ins=ins_, outs=outs_)
        raise KeyError(kind)


def _run_block(nc, plan):
    with nc.Block() as blk:
        if plan.steps["q"]:
            blk.sync(lambda e: plan.replay("q", e))
        if plan.steps["g"]:
            blk.gpsimd(lambda e: plan.replay("g", e))
        if plan.steps["v"]:
            blk.vector(lambda e: plan.replay("v", e))
        if plan.steps["s"]:
            blk.scalar(lambda e: plan.replay("s", e))
        if plan.steps["t"]:
            blk.tensor(lambda e: plan.replay("t", e))



class _ConvBufs:
    _SEQ = [0]

    def __init__(self, nc, stack):
        ent = stack.enter_context
        _ConvBufs._SEQ[0] += 1
        pref = "cv%d_" % _ConvBufs._SEQ[0]
        T = lambda n, sh, dt: ent(nc.sbuf_tensor(pref + n, sh, dt))
        self.S = T("S", [128, 16 * 2 * 256], BF16)
        self.E = T("E", [128, 2 * 2 * 4096], BF16)
        self.UT = T("UT", [128, 2 * 2 * 4096], BF16)
        self.Fbuf = T("F", [128, 2 * 1024], BF16)
        self.Z = T("Z", [128, 2 * 1024], BF16)
        self.Ksb = T("Ksb", [128, 2 * 1024], BF16)
        self.Kbuf = self.Ksb
        self.Usb = T("Usb", [128, 2 * 1024], BF16)
        self.tmp = T("tmp", [128, 2 * 1024], F32)
        self.tmp2 = T("tmp2", [128, 1024], F32)
        self.ysb = T("ysb", [128, 2 * 256], BF16)


class _Conv:

    def __init__(self, P, b, cb, pb):
        self.P, self.b, self.c, self.pb = P, b, cb, pb
        self.ptr_grp = 0
        self.ptr_evac = {}

    def tr_group(self, trs, dest, cons, extra_waits=(), evac_extra=()):
        P, b = self.P, self.b
        g = self.ptr_grp
        bank = self.pb["ptr"][g % 2]
        waits = list(extra_waits)
        if g - 2 in self.ptr_evac:
            waits.append(self.ptr_evac[g - 2])
        for i, in_ap in enumerate(trs):
            P.add("t", "tr", bank[:, i * 128:(i + 1) * 128], in_ap, self.c.ident[:],
                  waits=waits if i == 0 else (),
                  incs=[("ft", 1)] if i == len(trs) - 1 else ())
        cn = "fv" if cons == "v" else "fs"
        P.add(cons, "copy", dest, bank[:, 0:128 * len(trs)],
              waits=[("ft", P.n("ft"))] + list(evac_extra), incs=[(cn, 1)])
        self.ptr_evac[g] = (cn, P.n(cn))
        self.ptr_grp += 1

    def strided_tr(self, tiles, rchunks, extra_first=()):
        b = self.b
        first = True
        for s in range(16):
            if rchunks == 2:
                trs = [tiles[cq][:, s + 2048 * rc: s + 2048 * rc + 16 * 127 + 1: 16]
                       for rc in range(2) for cq in range(2)]
                self.tr_group(trs, b.S[:, s * 512:(s + 1) * 512],
                              "v" if s % 2 == 0 else "s",
                              extra_waits=extra_first if first else ())
                first = False
                continue
            for rc in range(rchunks):
                trs = [tiles[cq][:, s + 2048 * rc: s + 2048 * rc + 16 * 127 + 1: 16]
                      for cq in range(2)]
                off = (s * 2 + rc) * 256
                self.tr_group(trs, b.S[:, off:off + 256],
                              "v" if (s + rc) % 2 == 0 else "s",
                              extra_waits=extra_first if first else ())
                first = False

    def stage1(self, rchunks):
        P, b, c = self.P, self.b, self.c
        banks = [self.pb["pA"], self.pb["pB"], self.pb["pXre"][0], self.pb["pXre"][1]]
        s1_evac = {}
        k = 0
        for s in range(16):
            for ac in range(2):
                for im in range(2):
                    bank = banks[k % 4]
                    cons = "v" if k % 2 == 0 else "s"
                    waits = []
                    if k - 4 in s1_evac:
                        waits.append(s1_evac[k - 4])
                    if k <= 3:
                        waits += [("fv", P.n("fv")), ("fs", P.n("fs"))]
                    for rc in range(rchunks):
                        off = ((s * 2 + rc) * 2 + ac) * 128
                        G = (c.Gim if im else c.Gre)[:, off:off + 128]
                        soff = (s * 2 + rc) * 256
                        P.add("t", "mm", bank[:, 0:256], G, b.S[:, soff:soff + 256],
                              rc == 0, rc == rchunks - 1,
                              waits=waits if rc == 0 else (),
                              incs=[("ft", 1)] if rc == rchunks - 1 else ())
                    eoff = (ac * 2 + im) * 4096
                    dest = b.E[:, eoff:eoff + 4096].rearrange(
                        "p (g c2 s2) -> p g c2 s2", c2=8, s2=16)[:, :, :, s]
                    cn = "fv" if cons == "v" else "fs"
                    P.add(cons, "copy", dest, bank[:, 0:256],
                          waits=[("ft", P.n("ft"))], incs=[(cn, 1)])
                    s1_evac[k] = (cn, P.n(cn))
                    k += 1
        return P.n("fv"), P.n("fs")

    def pair_turn1(self, pair, extra_waits=()):
        b = self.b
        trs = []
        for im in range(2):
            for lg in range(2):
                g = pair * 2 + lg
                for ac in range(2):
                    eoff = (ac * 2 + im) * 4096
                    trs.append(b.E[:, eoff + g * 128: eoff + (g + 1) * 128])
        par = pair % 2
        dest = b.Fbuf[:, par * 1024: par * 1024 + 1024]
        self.tr_group(trs, dest, "v" if par == 0 else "s", extra_waits=extra_waits)

    def pair_stage2(self, pair, x_prev):
        P, b, c = self.P, self.b, self.c
        par = pair % 2
        fw = [self.ptr_evac[self.ptr_grp - 1]]
        if par in x_prev:
            fw.append(x_prev[par])
        F = lambda im: b.Fbuf[:, par * 1024 + im * 512: par * 1024 + im * 512 + 512]
        pXre, pXim = self.pb["pXre"], self.pb["pXim"]
        P.add("t", "mm", pXre[par][0:72, 0:512], c.BD2re[0:128, 0:72], F(0), True, False, waits=fw)
        P.add("t", "mm", pXre[par][0:72, 0:512], c.BD2im_neg[0:128, 0:72], F(1), False, True)
        P.add("t", "mm", pXim[par][0:72, 0:512], c.BD2im[0:128, 0:72], F(0), True, False)
        P.add("t", "mm", pXim[par][0:72, 0:512], c.BD2re[0:128, 0:72], F(1), False, True,
              incs=[("ft", 1)])
        return P.n("ft")

    def emit_ker_head(self, ker_tiles, extra_first=()):
        self.strided_tr(ker_tiles, 1, extra_first=extra_first)
        bar = self.stage1(1)
        self._ker = {"bar": bar, "x_prev": {}, "kq_guard": {}}

    def emit_ker_pair(self, pair, kout):
        P, b, c = self.P, self.b, self.c
        st = self._ker
        bar, x_prev, kq_guard = st["bar"], st["x_prev"], st["kq_guard"]
        ew = [("fv", bar[0]), ("fs", bar[1])] if pair < 2 else ()
        self.pair_turn1(pair, extra_waits=ew)
        nft = self.pair_stage2(pair, x_prev)
        par = pair % 2
        cm = c.cmask[0:72, :].unsqueeze(1).broadcast_to([72, 2, 256])
        waits = [("ft", nft)]
        if par in kq_guard:
            waits.append(kq_guard[par])
        pXre, pXim = self.pb["pXre"], self.pb["pXim"]
        P.add("v", "tt",
              b.Ksb[0:72, par * 1024: par * 1024 + 512].rearrange("p (l f) -> p l f", l=2),
              pXre[par][0:72, 0:512].rearrange("p (l f) -> p l f", l=2), cm, ALU.mult,
              waits=waits)
        P.add("v", "tt",
              b.Ksb[0:72, par * 1024 + 512: par * 1024 + 1024].rearrange("p (l f) -> p l f", l=2),
              pXim[par][0:72, 0:512].rearrange("p (l f) -> p l f", l=2), cm, ALU.mult,
              incs=[("fv", 1)])
        x_prev[par] = ("fv", P.n("fv"))
        ks = "kq%d" % par
        P.add("q", "dma", kout[pair], b.Ksb[0:72, par * 1024: par * 1024 + 1024],
              waits=[("fv", P.n("fv"))], incs=[(ks, 16)])
        kq_guard[par] = (ks, P.n(ks))

    def emit_kv(self, kv_tiles, kin, ych_ap, k_extra_wait=()):
        P, b, c = self.P, self.b, self.c
        pXre, pXim = self.pb["pXre"], self.pb["pXim"]
        pA, pB = self.pb["pA"], self.pb["pB"]
        self.strided_tr(kv_tiles, 2)
        bar = self.stage1(2)
        x_prev = {}
        z_done = {}
        u_prev = None
        usb_turn = {}
        ustage_prev = {}
        for pair in range(16):
            par = pair % 2
            kw = list(k_extra_wait)
            if par in z_done:
                kw.append(z_done[par])
            P.add("q", "dma", b.Kbuf[0:72, par * 1024: par * 1024 + 1024],
                  kin[pair], waits=kw, incs=[("kq", 16)])
            nkq = P.n("kq")
            ew = [("fv", bar[0]), ("fs", bar[1])] if pair < 2 else ()
            self.pair_turn1(pair, extra_waits=ew)
            nft = self.pair_stage2(pair, x_prev)
            sw = [("ft", nft)]
            if par in z_done:
                sw.append(z_done[par])
            P.add("s", "copy", b.tmp[0:72, par * 1024: par * 1024 + 512],
                  pXre[par][0:72, 0:512], waits=sw)
            P.add("s", "copy", b.tmp[0:72, par * 1024 + 512: par * 1024 + 1024],
                  pXim[par][0:72, 0:512], incs=[("fs", 1)])
            nfs = P.n("fs")
            x_prev[par] = ("fs", nfs)
            K = lambda im: b.Kbuf[0:72, par * 1024 + im * 512: par * 1024 + im * 512 + 512]
            Z = lambda im: b.Z[0:72, par * 1024 + im * 512: par * 1024 + im * 512 + 512]
            tre = b.tmp[0:72, par * 1024: par * 1024 + 512]
            tim = b.tmp[0:72, par * 1024 + 512: par * 1024 + 1024]
            zA, zB = b.tmp2[0:72, 0:512], b.tmp2[0:72, 512:1024]
            vw = [("fs", nfs), ("kq", nkq)]
            if par in ustage_prev:
                vw.append(ustage_prev[par])
            P.add("v", "tt", zA, tre, K(0), ALU.mult, waits=vw)
            P.add("v", "tt", zB, tim, K(1), ALU.mult)
            P.add("v", "tt", Z(0), zA, zB, ALU.subtract)
            P.add("v", "tt", zA, tre, K(1), ALU.mult)
            P.add("v", "tt", zB, tim, K(0), ALU.mult)
            P.add("v", "tt", Z(1), zA, zB, ALU.add, incs=[("fv", 1)])
            z_done[par] = ("fv", P.n("fv"))
            uw = [z_done[par]]
            if u_prev is not None:
                uw.append(u_prev)
            P.add("t", "mm", pA[:, 0:512], c.BDAre[0:72, 0:128], Z(0), True, False, waits=uw)
            P.add("t", "mm", pA[:, 0:512], c.BDAim_neg[0:72, 0:128], Z(1), False, True)
            P.add("t", "mm", pB[:, 0:512], c.BDAim[0:72, 0:128], Z(0), True, False)
            P.add("t", "mm", pB[:, 0:512], c.BDAre[0:72, 0:128], Z(1), False, True,
                  incs=[("ft", 1)])
            ustage_prev[par] = ("ft", P.n("ft"))
            sw = [("ft", P.n("ft"))]
            if par in usb_turn:
                sw.append(usb_turn[par])
            P.add("s", "copy", b.Usb[:, par * 1024: par * 1024 + 512], pA[:, 0:512],
                  waits=sw)
            P.add("s", "act", b.Usb[:, par * 1024 + 512: par * 1024 + 1024], pB[:, 0:512],
                  ACTF.Copy, 0.0, -1.0, incs=[("fs", 1)])
            u_prev = ("fs", P.n("fs"))
            trs = []
            for ac in range(2):
                for im in range(2):
                    for lg in range(2):
                        off = par * 1024 + im * 512 + lg * 256 + ac * 128
                        trs.append(b.Usb[:, off: off + 128])
            g0 = pair * 2
            dest = b.UT[:].rearrange("p (aci gg f) -> p aci gg f", aci=4, gg=32)[:, :, g0:g0 + 2, :]
            self.tr_group(trs, dest, "v" if par == 0 else "s", extra_waits=[u_prev])
            usb_turn[par] = ("ft", P.n("ft"))
        bar2 = (("fv", P.n("fv")), ("fs", P.n("fs")))
        py_prev = {}
        tb_done = {}
        for nu in range(16):
            par = nu % 2
            bank = pXre[par]
            w = [bar2[0], bar2[1]] if nu < 2 else []
            if par in py_prev:
                w.append(py_prev[par])
            if nu < 2 and par in x_prev:
                w.append(x_prev[par])
            first = True
            for ac in range(2):
                hoff = (nu * 2 + ac) * 128
                UTr = b.UT[:, 0 * 4096 + 0:].rearrange("p (aci gg f) -> p aci gg f", aci=4, gg=32)
                base_re = (ac * 2 + 0) * 4096
                base_im = (ac * 2 + 1) * 4096
                UTre = b.UT[:, base_re:base_re + 4096].rearrange(
                    "p (g c2 n2) -> p g c2 n2", c2=8, n2=16)[:, :, :, nu]
                UTim = b.UT[:, base_im:base_im + 4096].rearrange(
                    "p (g c2 n2) -> p g c2 n2", c2=8, n2=16)[:, :, :, nu]
                P.add("t", "mm", bank[:, 0:256], self.c.Hre[:, hoff:hoff + 128], UTre,
                      first, False, waits=w if first else ())
                first = False
                P.add("t", "mm", bank[:, 0:256], self.c.Him[:, hoff:hoff + 128], UTim,
                      False, ac == 1, incs=[("ft", 1)] if ac == 1 else ())
            cons = "v" if par == 0 else "s"
            cn = "fv" if par == 0 else "fs"
            yw = [("ft", P.n("ft"))]
            if par in tb_done:
                yw.append(tb_done[par])
            P.add(cons, "copy", self.b.ysb[:, par * 256: par * 256 + 256], bank[:, 0:256],
                  waits=yw, incs=[(cn, 1)])
            py_prev[par] = (cn, P.n(cn))
            trs = [self.b.ysb[:, par * 256 + cq * 128: par * 256 + (cq + 1) * 128]
                   for cq in range(2)]
            dest = ych_ap.rearrange("p (cq t) -> p cq t", cq=2)[:, :, nu:2048:16]
            self.tr_group(trs, dest, "v" if par == 0 else "s",
                          extra_waits=[py_prev[par]])
            tb_done[par] = ("ft", P.n("ft"))



def _build(debug=()):
    nc = bass.Bass("TRN2", target_bir_lowering=False)

    din = {}
    def DI(name, shape, dt=F32):
        din[name] = nc.dram_tensor(name, list(shape), dt, kind="ExternalInput")
        return din[name]

    u_own = DI("u_own", [TOK, D])
    u_ctx = DI("u_ctx", [TOK, D])
    uT_own = DI("uT_own", [D, TOK])
    ctxm = DI("ctxm", [128, 1])
    for w in ("WqT", "WkT", "WvT", "WoT"):
        DI(w, [D, D])
    for bname in ("bq", "bk", "bv", "bo", "cbq", "cbk", "dskip"):
        DI(bname, [128, 16])
    DI("cwq", [128, 48])
    DI("cwk", [128, 48])
    DI("ker_sl", [256, D])
    DI("gw_s", [1024, D])
    DI("uw_s", [1024, D])
    DI("dw_s", [256, I_FF])
    fc = _fft_constants()
    for name, arr in fc.items():
        DI("c_" + name, arr.shape, BF16 if arr.dtype == BF else F32)

    outT = nc.dram_tensor("outT", [D, TOK], F32, kind="ExternalOutput")

    wbf = {w: nc.dram_tensor("bf_" + w, [D, D], BF16, kind="Internal")
           for w in ("WqT", "WkT", "WvT", "WoT")}
    kv_scr = nc.dram_tensor("kv_scr", [D, WIN], BF16, kind="Internal")
    q_scr = nc.dram_tensor("q_scr", [D, TOK], BF16, kind="Internal")
    y_scr = nc.dram_tensor("y_scr", [D, TOK], BF16, kind="Internal")
    attn_scr = nc.dram_tensor("attn_scr", [D, TOK], BF16, kind="Internal")
    xdq_scr = nc.dram_tensor("xdq_scr", [D, TOK], BF16, kind="Internal")
    xn_scr = nc.dram_tensor("xn_scr", [TOK, D], BF16, kind="Internal")
    h_scr = nc.dram_tensor("h_scr", [I_FF, TOK], BF16, kind="Internal")
    ccK_in = nc.dram_tensor("ccK_in", [16, 72, 1024], BF16, kind="Internal")
    ccK_out = nc.dram_tensor("ccK_out", [8, 16, 72, 1024], BF16, kind="Internal")
    ccW_in = nc.dram_tensor("ccW_in", [6144, 1024], BF16, kind="Internal")
    ccW_out = nc.dram_tensor("ccW_out", [8, 6144, 1024], BF16, kind="Internal")
    dbg = {}
    for name in debug:
        scr = dict(kv_scr=kv_scr, q_scr=q_scr, y_scr=y_scr, attn_scr=attn_scr,
                   xdq_scr=xdq_scr, h_scr=h_scr, xn_scr=xn_scr, ccW_in=ccW_in,
                   ccK_in=ccK_in)[name]
        dbg[name] = (scr, nc.dram_tensor("dbg_" + name, list(scr.shape), BF16,
                                         kind="ExternalOutput"))

    with ExitStack() as persist:
        ent = persist.enter_context

        class _Consts:
            pass
        c = _Consts()
        for name in fc:
            tgt = "ident" if name == "identity" else name
            setattr(c, tgt, ent(nc.sbuf_tensor("k_" + tgt, list(fc[name].shape),
                                               BF16 if fc[name].dtype == BF else F32)))
        biases = ent(nc.sbuf_tensor("biases", [128, 16 * 7], F32))
        convw = ent(nc.sbuf_tensor("convw", [128, 16 * 6], F32))
        maskv = ent(nc.sbuf_tensor("maskv", [128, 1], F32))
        epsv = ent(nc.sbuf_tensor("epsv", [128, 1], F32))
        bm = ent(nc.sbuf_tensor("bm", [128, 16 * 3], F32))

        def bias_ap(idx, ecn):
            return biases[:, idx * 16 + ecn: idx * 16 + ecn + 1]

        def convw_ap(which, ecn, j):
            return convw[:, which * 48 + ecn * 3 + j: which * 48 + ecn * 3 + j + 1]

        pA = ent(nc.psum_tensor("pA", [128, 512], F32))
        pB = ent(nc.psum_tensor("pB", [128, 512], F32))
        ptr0 = ent(nc.psum_tensor("ptr0", [128, 1024], BF16))
        ptr1 = ent(nc.psum_tensor("ptr1", [128, 1024], BF16))
        pC = ent(nc.psum_tensor("pC", [128, 512], F32))
        pD = ent(nc.psum_tensor("pD", [128, 512], F32))
        pE = ent(nc.psum_tensor("pE", [128, 512], F32))
        pF = ent(nc.psum_tensor("pF", [128, 512], F32))
        pb = {"pA": pA, "pB": pB, "ptr": [ptr0, ptr1],
              "pXre": [pC, pD], "pXim": [pE, pF]}

        P = _Plan(nc)

        for name in fc:
            tgt = "ident" if name == "identity" else name
            rows = fc[name].shape[0]
            P.add("q", "dma", getattr(c, tgt)[0:rows, :], din["c_" + name][:],
                  incs=[("ld", 16)])
        for i, bname in enumerate(("bq", "bk", "bv", "bo", "cbq", "cbk", "dskip")):
            P.add("q", "dma", biases[:, i * 16:(i + 1) * 16], din[bname][:],
                  incs=[("ld", 16)])
        P.add("q", "dma", convw[:, 0:48], din["cwq"][:], incs=[("ld", 16)])
        P.add("q", "dma", convw[:, 48:96], din["cwk"][:], incs=[("ld", 16)])
        P.add("q", "dma", maskv[:], ctxm[:], incs=[("ld", 16)])
        n_const_ld = P.n("ld")
        P.add("v", "memset", epsv[:], 1e-5)
        cv = _Conv(P, None, c, pb)
        for w in ("WqT", "WkT", "WvT", "WoT"):
            P.add("g", "dma", wbf[w][:], din[w][:], incs=[("gd", 16)])
        n_wcast = P.n("gd")
        P.add("v", "ts", bm[:, 0:48], biases[:, 0:48], maskv[:, 0:1], None, ALU.mult, None,
              waits=[("ld", n_const_ld)], incs=[("fv", 1)])

        with ExitStack() as sc3:
            e3 = sc3.enter_context
            ubuf = e3(nc.sbuf_tensor("ubuf", [128, 2 * 2048], F32))
            xsq = e3(nc.sbuf_tensor("xsq", [128, 2048], F32))
            xnf = e3(nc.sbuf_tensor("xnf", [128, 2048], F32))
            cscr = e3(nc.sbuf_tensor("cscr", [128, 2 * 1024], F32))
            xnb = e3(nc.sbuf_tensor("xnb", [128, 2 * 2048], BF16))
            stats = e3(nc.sbuf_tensor("stats", [128, 32], F32))
            lnsc = e3(nc.sbuf_tensor("lnsc", [128, 8], F32))
            xnT = e3(nc.sbuf_tensor("xnT", [128, 16 * 1024], BF16))
            wslab = e3(nc.sbuf_tensor("wslab", [128, 4 * 2048], BF16))
            kraw = e3(nc.sbuf_tensor("kraw", [128, 2 * 1028], BF16))
            vraw = e3(nc.sbuf_tensor("vraw", [128, 2 * 1024], BF16))
            qraw = e3(nc.sbuf_tensor("qraw", [128, 2 * 1028], BF16))
            ktail = e3(nc.sbuf_tensor("ktail", [128, 16 * 4], BF16))
            qtail = e3(nc.sbuf_tensor("qtail", [128, 16 * 4], BF16))
            kvec = e3(nc.sbuf_tensor("kvec", [128, 2 * 1024], BF16))
            qcec = e3(nc.sbuf_tensor("qcec", [128, 2 * 1024], BF16))
            xq8 = e3(nc.sbuf_tensor("xq8", [128, 2048], FP8))
            xdq = e3(nc.sbuf_tensor("xdq", [128, 2048], BF16))
            xsc = e3(nc.sbuf_tensor("xsc", [128, 4 * 64], F32))

            uload_done = {}
            xstage_guard = {}
            xn_done = {}
            wsl_done = {}
            conv_done_par = {}
            kv_dma_par = {}
            qc_dma_par = {}
            pp_evac = {}
            pp_k = 0

            def ln_tile(gi, own):
                par = gi % 2
                src = u_own if own else u_ctx
                row0 = (gi * 128) % 2048
                lw = []
                if par in uload_done:
                    lw.append(uload_done[par])
                P.add("q", "dma", ubuf[:, par * 2048:(par + 1) * 2048],
                      src[row0:row0 + 128, :], waits=lw, incs=[("lu%d" % par, 16)])
                u_ap = ubuf[:, par * 2048:(par + 1) * 2048]
                nld_u = P.n("lu%d" % par)
                vw = [("lu%d" % par, nld_u)]
                if gi > 0:
                    vw.append(xn_done[gi - 1])
                P.add("s", "act", xsq[:], u_ap, ACTF.Square, 0.0, 1.0,
                      waits=[("lu%d" % par, nld_u)] + ([xn_done[gi - 1]] if gi > 0 else []),
                      incs=[("fs", 1)])
                nfs_sq = P.n("fs")
                P.add("v", "red", stats[:, 0:1], u_ap.unsqueeze(1), ALU.add, False,
                      waits=vw)
                P.add("v", "red", stats[:, 1:2], xsq[:].unsqueeze(1), ALU.add, False,
                      waits=[("fs", nfs_sq)], incs=[("fv", 1)])
                P.add("v", "ts", stats[:, 2:3], stats[:, 0:1], 1.0 / 2048.0, None,
                      ALU.mult, None, waits=[("fv", P.n("fv"))], incs=[("fv", 1)])
                P.add("v", "tt", stats[:, 3:4], stats[:, 2:3], stats[:, 2:3], ALU.mult,
                      waits=[("fv", P.n("fv"))], incs=[("fv", 1)])
                P.add("v", "ts", stats[:, 4:5], stats[:, 1:2], 1.0 / 2048.0,
                      stats[:, 3:4], ALU.mult, ALU.subtract,
                      waits=[("fv", P.n("fv"))], incs=[("fv", 1)])
                P.add("s", "act", lnsc[:, 0:1], stats[:, 4:5], ACTF.Ln, epsv[:, 0:1], 1.0,
                      waits=[("fv", P.n("fv"))])
                P.add("s", "act", lnsc[:, 1:2], lnsc[:, 0:1], ACTF.Exp, 0.0, -0.5,
                      incs=[("fs", 1)])
                P.add("v", "ts", xnf[:], u_ap, stats[:, 2:3], lnsc[:, 1:2],
                      ALU.subtract, ALU.mult, waits=[("fs", P.n("fs"))], incs=[("fv", 1)])
                uload_done[par] = ("fv", P.n("fv"))
                P.add("v", "copy", xnb[:, par * 2048:(par + 1) * 2048], xnf[:],
                      incs=[("fv", 1)])
                xn_done[gi] = ("fv", P.n("fv"))
                if own and _DEBUG:
                    P.add("q", "dma", xn_scr[(gi * 128) % 2048:(gi * 128) % 2048 + 128, :],
                          xnb[:, par * 2048:(par + 1) * 2048], waits=[xn_done[gi]],
                          incs=[("xs", 16)])
                if own:
                    row0o = row0
                    P.add("v", "red", xsc[:, 0:64], xnf[:].rearrange("p (b k) -> p b k", k=32),
                          ALU.max, True, incs=[("fv", 1)])
                    P.add("v", "ts", xsc[:, 64:128], xsc[:, 0:64], 1.0 / 448.0, 1e-30,
                          ALU.mult, ALU.max, waits=[("fv", P.n("fv"))], incs=[("fv", 1)])
                    P.add("v", "recip", xsc[:, 128:192], xsc[:, 64:128],
                          waits=[("fv", P.n("fv"))], incs=[("fv", 1)])
                    P.add("v", "ts", xsc[:, 192:256], xsc[:, 64:128], 2.0, None, ALU.mult, None,
                          waits=[("fv", P.n("fv"))], incs=[("fv", 1)])
                    P.add("v", "stt", xq8[:].rearrange("p (b k) -> p b k", k=32),
                          xnf[:].rearrange("p (b k) -> p b k", k=32), 0.5,
                          xsc[:, 128:192].unsqueeze(-1).broadcast_to([128, 64, 32]),
                          ALU.mult, ALU.mult, waits=[("fv", P.n("fv"))])
                    P.add("v", "tt", xdq[:].rearrange("p (b k) -> p b k", k=32),
                          xq8[:].rearrange("p (b k) -> p b k", k=32),
                          xsc[:, 192:256].unsqueeze(-1).broadcast_to([128, 64, 32]),
                          ALU.mult, incs=[("fv", 1)])
                    nq = ("fv", P.n("fv"))
                    for hc4 in range(4):
                        gslot = cv.ptr_grp % 2
                        stg = wstage_x[:, gslot * 512:(gslot + 1) * 512]
                        trs = [xdq[:, (hc4 * 4 + i2) * 128:(hc4 * 4 + i2 + 1) * 128]
                               for i2 in range(4)]
                        xsem = "xd%d" % gslot
                        ee = [(xsem, xstage_guard[gslot])] if gslot in xstage_guard else []
                        cv.tr_group(trs, stg, "s", extra_waits=[nq], evac_extra=ee)
                        dest = xdq_scr[hc4 * 512:(hc4 + 1) * 512, row0o:row0o + 128].rearrange(
                            "(b p) t -> p b t", p=128)
                        P.add("q", "dma", dest, stg.rearrange("p (b f) -> p b f", b=4),
                              waits=[cv.ptr_evac[cv.ptr_grp - 1]], incs=[(xsem, 16)])
                        xstage_guard[gslot] = P.n(xsem)

            wstage_x = e3(nc.sbuf_tensor("wstage_x", [128, 1024], BF16))

            def xn_tr(gi, gb):
                par = gi % 2
                src = xnb[:, par * 2048:(par + 1) * 2048]
                for hc4 in range(4):
                    trs = [src[:, (hc4 * 4 + i2) * 128:(hc4 * 4 + i2 + 1) * 128]
                           for i2 in range(4)]
                    dest = xnT[:].rearrange("p (hc t) -> p hc t", hc=16)[
                        :, hc4 * 4:(hc4 + 1) * 4, gb * 128:(gb + 1) * 128]
                    ew = [xn_done[gi]]
                    if gi == 0 and hc4 == 0:
                        ew.append(("ld", n_const_ld))
                    cv.tr_group(trs, dest, "v" if hc4 % 2 == 0 else "s",
                                extra_waits=ew)

            qtsave = e3(nc.sbuf_tensor("qtsave", [128, 32], BF16))
            qmini_prev = None
            qts_bar = None
            for grp in range(4):
                own_grp = grp >= 2
                for gb in range(8):
                    gi = grp * 8 + gb
                    ln_tile(gi, own_grp)
                    xn_tr(gi, gb)
                xnt_bar = (("fv", P.n("fv")), ("fs", P.n("fs")))
                if grp == 1:
                    P.add("v", "copy", qtsave[:].rearrange("p (hc t) -> p hc t", t=2),
                          xnT[:].rearrange("p (hc t) -> p hc t", hc=16)[:, :, 1022:1024],
                          waits=[xnt_bar[0], xnt_bar[1]], incs=[("fv", 1)])
                    qts_bar = ("fv", P.n("fv"))
                projs = ([("WqT", 0, qraw, True)] if own_grp else []) + \
                    [("WkT", 1, kraw, True), ("WvT", 2, vraw, False)]
                for ec in range(16):
                    raw_done = {}
                    rpar = ec % 2
                    for wname, bidx, rawbuf, haspad in projs:
                        wpar = (pp_k // 2) % 4
                        lw = []
                        if wpar in wsl_done:
                            lw.append(wsl_done[wpar])
                        if grp == 0 and ec == 0 and wname == projs[0][0]:
                            lw = lw + [("gd", n_wcast)]
                        P.add("q", "dma", wslab[:, wpar * 2048:(wpar + 1) * 2048].rearrange(
                            "p (hc f) -> p hc f", hc=16),
                            wbf[wname][:, ec * 128:(ec + 1) * 128].rearrange(
                            "(hc p) f -> p hc f", p=128), waits=lw, incs=[("ls%d" % wpar, 16)])
                        nld = P.n("ls%d" % wpar)
                        pad = 4 if haspad else 0
                        rw = rawbuf[:, rpar * (1024 + pad): rpar * (1024 + pad) + 1024 + pad]
                        for blk2 in range(2):
                            bank = [pA, pB, pC, pD][pp_k % 4]
                            cons = "v" if pp_k % 2 == 0 else "s"
                            waits = [("ls%d" % wpar, nld)]
                            if ec == 0 and wname == projs[0][0] and blk2 == 0:
                                waits += [xnt_bar[0], xnt_bar[1]]
                            if pp_k - 4 in pp_evac:
                                waits.append(pp_evac[pp_k - 4])
                            for hc in range(16):
                                P.add("t", "mm", bank[:, 0:512],
                                      wslab[:, wpar * 2048 + hc * 128: wpar * 2048 + (hc + 1) * 128],
                                      xnT[:].rearrange("p (hc t) -> p hc t", hc=16)[
                                          :, hc, blk2 * 512:(blk2 + 1) * 512],
                                      hc == 0, hc == 15,
                                      waits=waits if hc == 0 else (),
                                      incs=[("ft", 1)] if hc == 15 else ())
                            dst = rw[:, pad + blk2 * 512: pad + (blk2 + 1) * 512]
                            cn = "fv" if cons == "v" else "fs"
                            evw = [("ft", P.n("ft"))]
                            if rpar in conv_done_par:
                                evw.append(conv_done_par[rpar])
                            if own_grp:
                                if cons == "v":
                                    P.add(cons, "ts", dst, bank[:, 0:512], 1.0,
                                          bias_ap(bidx, ec), ALU.mult, ALU.add,
                                          waits=evw, incs=[(cn, 1)])
                                else:
                                    P.add(cons, "act", dst, bank[:, 0:512], ACTF.Identity,
                                          bias_ap(bidx, ec), 1.0,
                                          waits=evw, incs=[(cn, 1)])
                            else:
                                if cons == "v":
                                    P.add(cons, "ts", dst, bank[:, 0:512], maskv[:, 0:1],
                                          bm[:, bidx * 16 + ec: bidx * 16 + ec + 1],
                                          ALU.mult, ALU.add,
                                          waits=evw, incs=[(cn, 1)])
                                else:
                                    P.add(cons, "act", dst, bank[:, 0:512], ACTF.Identity,
                                          bm[:, bidx * 16 + ec: bidx * 16 + ec + 1],
                                          maskv[:, 0:1],
                                          waits=evw, incs=[(cn, 1)])
                            pp_evac[pp_k] = (cn, P.n(cn))
                            raw_done[wname] = (cn, P.n(cn))
                            pp_k += 1
                        if wname == "WqT" and grp == 2:
                            mw = [qts_bar]
                            if qmini_prev is not None:
                                mw.append(qmini_prev)
                            for hc in range(16):
                                P.add("t", "mm", pb["pXim"][0][:, 0:2],
                                      wslab[:, wpar * 2048 + hc * 128: wpar * 2048 + (hc + 1) * 128],
                                      qtsave[:, hc * 2:(hc + 1) * 2],
                                      hc == 0, hc == 15,
                                      waits=mw if hc == 0 else (),
                                      incs=[("ft", 1)] if hc == 15 else ())
                            qw0 = qraw[:, rpar * 1028: rpar * 1028 + 1028]
                            P.add("v", "ts", qw0[:, 2:4], pb["pXim"][0][:, 0:2],
                                  maskv[:, 0:1], bm[:, 0 * 16 + ec: 0 * 16 + ec + 1],
                                  ALU.mult, ALU.add,
                                  waits=[("ft", P.n("ft"))], incs=[("fv", 1)])
                            qmini_prev = ("fv", P.n("fv"))
                        wsl_done[wpar] = ("ft", P.n("ft"))
                    kw_ = kraw[:, rpar * 1028: rpar * 1028 + 1028]
                    vw_ = vraw[:, rpar * 1024: rpar * 1024 + 1024]
                    tw = [raw_done["WkT"], raw_done["WvT"]]
                    if own_grp:
                        tw.append(raw_done["WqT"])
                    if grp == 0:
                        P.add("v", "memset", kw_[:, 0:4], 0.0, waits=tw)
                    else:
                        P.add("v", "copy", kw_[:, 0:4], ktail[:, ec * 4:(ec + 1) * 4], waits=tw)
                    P.add("v", "ts", cscr[:, 0:1024], kw_[:, 2:1026], convw_ap(1, ec, 0),
                          bias_ap(5, ec), ALU.mult, ALU.add)
                    P.add("v", "stt", cscr[:, 1024:2048], kw_[:, 3:1027], convw_ap(1, ec, 1),
                          cscr[:, 0:1024], ALU.mult, ALU.add)
                    P.add("v", "stt", cscr[:, 0:1024], kw_[:, 4:1028], convw_ap(1, ec, 2),
                          cscr[:, 1024:2048], ALU.mult, ALU.add)
                    P.add("v", "copy", ktail[:, ec * 4:(ec + 1) * 4], kw_[:, 1024:1028])
                    kvw2 = [("kv%d" % rpar, kv_dma_par[rpar])] if rpar in kv_dma_par else []
                    P.add("v", "tt", kvec[:, rpar * 1024:(rpar + 1) * 1024],
                          cscr[:, 0:1024], vw_, ALU.mult, waits=kvw2, incs=[("fv", 1)])
                    kv_done = ("fv", P.n("fv"))
                    P.add("q", "dma",
                          kv_scr[ec * 128:(ec + 1) * 128, grp * 1024:(grp + 1) * 1024],
                          kvec[:, rpar * 1024:(rpar + 1) * 1024],
                          waits=[kv_done], incs=[("kv%d" % rpar, 16)])
                    kv_dma_par[rpar] = P.n("kv%d" % rpar)
                    if own_grp:
                        qw_ = qraw[:, rpar * 1028: rpar * 1028 + 1028]
                        if grp != 2:
                            P.add("v", "copy", qw_[:, 0:4], qtail[:, ec * 4:(ec + 1) * 4])
                        P.add("v", "ts", cscr[:, 0:1024], qw_[:, 2:1026], convw_ap(0, ec, 0),
                              bias_ap(4, ec), ALU.mult, ALU.add)
                        P.add("v", "stt", cscr[:, 1024:2048], qw_[:, 3:1027], convw_ap(0, ec, 1),
                              cscr[:, 0:1024], ALU.mult, ALU.add)
                        qcw2 = [("qc%d" % rpar, qc_dma_par[rpar])] if rpar in qc_dma_par else []
                        P.add("v", "stt", qcec[:, rpar * 1024:(rpar + 1) * 1024],
                              qw_[:, 4:1028], convw_ap(0, ec, 2),
                              cscr[:, 1024:2048], ALU.mult, ALU.add,
                              waits=qcw2, incs=[("fv", 1)])
                        P.add("v", "copy", qtail[:, ec * 4:(ec + 1) * 4], qw_[:, 1024:1028],
                              incs=[("fv", 1)])
                        P.add("q", "dma",
                              q_scr[ec * 128:(ec + 1) * 128, (grp - 2) * 1024:(grp - 1) * 1024],
                              qcec[:, rpar * 1024:(rpar + 1) * 1024],
                              waits=[("fv", P.n("fv"))], incs=[("qc%d" % rpar, 16)])
                        qc_dma_par[rpar] = P.n("qc%d" % rpar)
                    conv_done_par[rpar] = ("fv", P.n("fv"))
            kv_all = [(n, P.n(n)) for n in ("kv0", "kv1", "qc0", "qc1")]

        with ExitStack() as sc1:
            e1 = sc1.enter_context
            kert = [e1(nc.sbuf_tensor(f"kert{i}", [128, 2048], BF16)) for i in range(2)]
            p3_bar = [("ft", P.n("ft")), ("fv", P.n("fv")), ("fs", P.n("fs"))] + \
                list(kv_all) + [("xd0", P.n("xd0")), ("xd1", P.n("xd1"))]
            for i in range(2):
                P.add("g", "dma", kert[i][:], din["ker_sl"][i * 128:(i + 1) * 128, :],
                      waits=p3_bar if i == 0 else (), incs=[("gk", 16)])
            n_kert = P.n("gk")

            bufsA = _ConvBufs(nc, sc1)
            cv.b = bufsA
            cv.emit_ker_head(kert, extra_first=[("gk", n_kert), ("ld", n_const_ld)])

            wtile = e1(nc.sbuf_tensor("wtile", [128, 2 * 2048], F32))
            wq8 = e1(nc.sbuf_tensor("wq8", [128, 2048], FP8))
            wdq = e1(nc.sbuf_tensor("wdq", [128, 2048], BF16))
            wsc = e1(nc.sbuf_tensor("wsc", [128, 4 * 64], F32))
            wstage = e1(nc.sbuf_tensor("wstage", [128, 1024], BF16))

            def wtile_src(j):
                if j < 8:
                    return din["gw_s"][j * 128:(j + 1) * 128, :]
                if j < 16:
                    return din["uw_s"][(j - 8) * 128:(j - 8 + 1) * 128, :]
                jj = j - 16
                r, qq = jj // 4, jj % 4
                return din["dw_s"][r * 128:(r + 1) * 128, qq * 2048:(qq + 1) * 2048]

            def wtile_dst(j, hc4, stg):
                if j < 16:
                    base = 0 if j < 8 else 2048
                    col = (j % 8) * 128
                    rows = ccW_in[base + hc4 * 512: base + (hc4 + 1) * 512, col:col + 128]
                    return rows.rearrange("(b p) i -> p b i", p=128)
                jj = j - 16
                r, qq = jj // 4, jj % 4
                rows = ccW_in[4096 + hc4 * 512: 4096 + (hc4 + 1) * 512,
                              qq * 256 + r * 128: qq * 256 + (r + 1) * 128]
                return rows.rearrange("(b p) e -> p b e", p=128)

            wq_done = {}
            wstage_guard = {}
            wt_prev = None
            for j in range(24):
                par = j % 2
                lw = []
                if j == 0:
                    lw += list(kv_all) + [("xd0", P.n("xd0")), ("xd1", P.n("xd1"))]
                if par in wq_done:
                    lw.append(wq_done[par])
                P.add("q", "dma", wtile[:, par * 2048:(par + 1) * 2048], wtile_src(j),
                      waits=lw, incs=[("lwt%d" % par, 16)])
                nld = P.n("lwt%d" % par)
                src = wtile[:, par * 2048:(par + 1) * 2048]
                vw = [("lwt%d" % par, nld)]
                if wt_prev is not None:
                    vw.append(wt_prev)
                P.add("v", "red", wsc[:, 0:64], src.rearrange("p (b k) -> p b k", k=32),
                      ALU.max, True, waits=vw, incs=[("fv", 1)])
                P.add("v", "ts", wsc[:, 64:128], wsc[:, 0:64], 1.0 / 448.0, 1e-30,
                      ALU.mult, ALU.max, waits=[("fv", P.n("fv"))], incs=[("fv", 1)])
                P.add("v", "recip", wsc[:, 128:192], wsc[:, 64:128],
                      waits=[("fv", P.n("fv"))], incs=[("fv", 1)])
                P.add("v", "ts", wsc[:, 192:256], wsc[:, 64:128], 2.0, None, ALU.mult, None,
                      waits=[("fv", P.n("fv"))], incs=[("fv", 1)])
                P.add("v", "stt", wq8[:].rearrange("p (b k) -> p b k", k=32),
                      src.rearrange("p (b k) -> p b k", k=32), 0.5,
                      wsc[:, 128:192].unsqueeze(-1).broadcast_to([128, 64, 32]),
                      ALU.mult, ALU.mult, waits=[("fv", P.n("fv"))])
                P.add("v", "tt", wdq[:].rearrange("p (b k) -> p b k", k=32),
                      wq8[:].rearrange("p (b k) -> p b k", k=32),
                      wsc[:, 192:256].unsqueeze(-1).broadcast_to([128, 64, 32]),
                      ALU.mult, incs=[("fv", 1)])
                wq_done[par] = ("fv", P.n("fv"))
                for hc4 in range(4):
                    gslot = cv.ptr_grp % 2
                    stg = wstage[:, gslot * 512:(gslot + 1) * 512]
                    trs = [wdq[:, (hc4 * 4 + i2) * 128:(hc4 * 4 + i2 + 1) * 128]
                           for i2 in range(4)]
                    wsem = "wd%d" % gslot
                    ee = [(wsem, wstage_guard[gslot])] if gslot in wstage_guard else []
                    cv.tr_group(trs, stg, "s", extra_waits=[wq_done[par]], evac_extra=ee)
                    P.add("q", "dma", wtile_dst(j, hc4, stg),
                          stg.rearrange("p (b f) -> p b f", b=4),
                          waits=[cv.ptr_evac[cv.ptr_grp - 1]], incs=[(wsem, 16)])
                    wstage_guard[gslot] = P.n(wsem)
                wt_prev = ("ft", P.n("ft"))
                if j < 16:
                    cv.emit_ker_pair(j, ccK_in[:])
            nkq_ker = (P.n("kq0"), P.n("kq1"))
            n_wd = (P.n("wd0"), P.n("wd1"))

        P.add("g", "coll", "AllGather", ALU.bypass, [[0, 1, 2, 3, 4, 5, 6, 7]],
              [ccK_in[:]], [ccK_out[:]],
              waits=[("kq0", nkq_ker[0]), ("kq1", nkq_ker[1])], incs=[("ck", 1)])
        P.add("g", "coll", "AllGather", ALU.bypass, [[0, 1, 2, 3, 4, 5, 6, 7]],
              [ccW_in[:]], [ccW_out[:]],
              waits=[("wd0", n_wd[0]), ("wd1", n_wd[1])], incs=[("cw", 1)])

        with ExitStack() as sc4:
            bufsB = _ConvBufs(nc, sc4)
            cv.b = bufsB
            kvt = [sc4.enter_context(nc.sbuf_tensor(f"kvt{i}", [128, 4096], BF16))
                   for i in range(2)]
            ych = sc4.enter_context(nc.sbuf_tensor("ych", [128, 2 * 2048], BF16))
            for ch in range(8):
                lw = list(kv_all)
                if ch > 0:
                    lw.append(("ft", P.n("ft")))
                else:
                    lw += [("ft", P.n("ft")), ("wd0", n_wd[0]), ("wd1", n_wd[1]),
                           ("kq0", nkq_ker[0]), ("kq1", nkq_ker[1])]
                for i in range(2):
                    P.add("q", "dma", kvt[i][:],
                          kv_scr[(2 * ch + i) * 128:(2 * ch + i + 1) * 128, :],
                          waits=lw if i == 0 else (), incs=[("kt", 16)])
                nld = P.n("kt")
                ef = [("kt", nld)] + ([("st", 16 * ch)] if ch > 0 else [])
                cv.strided_tr(kvt, 2, extra_first=ef)
                _emit_kv_rest(P, cv, ccK_out[ch], ych[:], k_wait=("ck", 1))
                P.add("q", "dma", y_scr[2 * ch * 128:(2 * ch + 2) * 128, :].rearrange(
                    "(cq p) t -> p cq t", cq=2), ych[:],
                    waits=[("fv", P.n("fv")), ("fs", P.n("fs"))], incs=[("st", 16)])
            n_yst = P.n("st")

        with ExitStack() as sc5:
            e5 = sc5.enter_context
            yblk = e5(nc.sbuf_tensor("yblk", [128, 16 * 512], BF16))
            kvob = e5(nc.sbuf_tensor("kvob", [128, 16 * 512], BF16))
            qcb = e5(nc.sbuf_tensor("qcb", [128, 16 * 512], BF16))
            yTb = e5(nc.sbuf_tensor("yTb", [128, 16 * 512], BF16))
            atb = e5(nc.sbuf_tensor("atb", [128, 16 * 512], BF16))
            wslab5 = e5(nc.sbuf_tensor("wslab5", [128, 2 * 2048], BF16))
            cscr5 = e5(nc.sbuf_tensor("cscr5", [128, 512], F32))
            wsl_done5 = {}
            wo_evac = {}
            p5_prev = None
            for blk in range(4):
                t0 = blk * 512
                lw = [("st", n_yst)]
                if p5_prev is not None:
                    lw.append(p5_prev)
                P.add("q", "dma", yblk[:].rearrange("p (hc t) -> p hc t", hc=16),
                      y_scr[:, t0:t0 + 512].rearrange("(hc p) t -> p hc t", p=128),
                      waits=lw, incs=[("l5", 16)])
                P.add("q", "dma", kvob[:].rearrange("p (hc t) -> p hc t", hc=16),
                      kv_scr[:, 2048 + t0: 2048 + t0 + 512].rearrange("(hc p) t -> p hc t", p=128),
                      incs=[("l5", 16)])
                P.add("q", "dma", qcb[:].rearrange("p (hc t) -> p hc t", hc=16),
                      q_scr[:, t0:t0 + 512].rearrange("(hc p) t -> p hc t", p=128),
                      incs=[("l5", 16)])
                nld = P.n("l5")
                for ec in range(16):
                    sl = slice(ec * 512, (ec + 1) * 512)
                    vw2 = [("l5", nld)] if ec == 0 else []
                    if ec == 0 and p5_prev is not None:
                        vw2.append(p5_prev)
                    P.add("v", "stt", cscr5[:, 0:512], kvob[:, sl],
                          bias_ap(6, ec), yblk[:, sl], ALU.mult, ALU.add, waits=vw2)
                    P.add("v", "tt", yTb[:, sl], cscr5[:, 0:512], qcb[:, sl], ALU.mult,
                          incs=[("fv", 1)] if ec == 15 else ())
                ybar = ("fv", P.n("fv"))
                for ec in range(16):
                    wpar = ec % 2
                    lw2 = []
                    if blk == 0 and ec == 0:
                        lw2.append(("gd", n_wcast))
                    if wpar in wsl_done5:
                        lw2.append(wsl_done5[wpar])
                    P.add("q", "dma", wslab5[:, wpar * 2048:(wpar + 1) * 2048].rearrange(
                        "p (hc f) -> p hc f", hc=16),
                        wbf["WoT"][:, ec * 128:(ec + 1) * 128].rearrange(
                        "(hc p) f -> p hc f", p=128), waits=lw2, incs=[("l5w%d" % wpar, 16)])
                    nld2 = P.n("l5w%d" % wpar)
                    bank = pA if ec % 2 == 0 else pB
                    cons = "v" if ec % 2 == 0 else "s"
                    waits = [("l5w%d" % wpar, nld2), ybar]
                    if ec - 2 in wo_evac:
                        waits.append(wo_evac[ec - 2])
                    for hc in range(16):
                        P.add("t", "mm", bank[:, 0:512],
                              wslab5[:, wpar * 2048 + hc * 128: wpar * 2048 + (hc + 1) * 128],
                              yTb[:, hc * 512:(hc + 1) * 512],
                              hc == 0, hc == 15,
                              waits=waits if hc == 0 else (),
                              incs=[("ft", 1)] if hc == 15 else ())
                    cn = "fv" if cons == "v" else "fs"
                    ew2 = [("ft", P.n("ft"))]
                    if p5_prev is not None and ec < 2:
                        ew2.append(p5_prev)
                    if cons == "v":
                        P.add(cons, "ts", atb[:, ec * 512:(ec + 1) * 512], bank[:, 0:512], 1.0,
                              bias_ap(3, ec), ALU.mult, ALU.add,
                              waits=ew2, incs=[(cn, 1)])
                    else:
                        P.add(cons, "act", atb[:, ec * 512:(ec + 1) * 512], bank[:, 0:512],
                              ACTF.Identity, bias_ap(3, ec), 1.0,
                              waits=ew2, incs=[(cn, 1)])
                    wo_evac[ec] = (cn, P.n(cn))
                    wsl_done5[wpar] = ("ft", P.n("ft"))
                wo_evac = {}
                P.add("q", "dma", attn_scr[:, t0:t0 + 512].rearrange("(hc p) t -> p hc t", p=128),
                      atb[:].rearrange("p (hc t) -> p hc t", hc=16),
                      waits=[("fv", P.n("fv")), ("fs", P.n("fs"))], incs=[("at", 16)])
                p5_prev = ("at", P.n("at"))
            n_at = P.n("at")

        with ExitStack() as sc6:
            e6 = sc6.enter_context
            xdqT = e6(nc.sbuf_tensor("xdqT", [128, 16 * 2048], BF16))
            P.add("q", "dma", xdqT[:].rearrange("p (hc t) -> p hc t", hc=16),
                  xdq_scr[:].rearrange("(hc p) t -> p hc t", p=128),
                  waits=[("xd0", P.n("xd0")), ("xd1", P.n("xd1"))], incs=[("lx", 16)])
            n_xdqt = P.n("lx")
            islab = e6(nc.sbuf_tensor("islab", [128, 4 * 8192], BF16))
            hsil = e6(nc.sbuf_tensor("hsil", [128, 512], BF16))
            hf = e6(nc.sbuf_tensor("hf", [128, 512], F32))
            hq8 = e6(nc.sbuf_tensor("hq8", [128, 512], FP8))
            hdq = e6(nc.sbuf_tensor("hdq", [128, 2 * 512], BF16))
            hsc = e6(nc.sbuf_tensor("hsc", [128, 4 * 16], F32))
            hstage = e6(nc.sbuf_tensor("hstage", [128, 1024], BF16))
            isl_done = {}
            hstage_guard = {}
            h_done = {}
            hf_done = {}
            tr_done = {}
            pend_tr = None

            def h_tr(kk, ic, ts_):
                gslot = cv.ptr_grp % 2
                stg = hstage[:, gslot * 512:(gslot + 1) * 512]
                hp = kk % 2
                trs = [hdq[:, hp * 512 + i2 * 128: hp * 512 + (i2 + 1) * 128]
                       for i2 in range(4)]
                hsem = "hd%d" % gslot
                ee = [(hsem, hstage_guard[gslot])] if gslot in hstage_guard else []
                cv.tr_group(trs, stg, "s", extra_waits=[h_done[kk]], evac_extra=ee)
                tr_done[kk] = ("ft", P.n("ft"))
                dest = h_scr[ic * 512:(ic + 1) * 512, ts_ * 128:(ts_ + 1) * 128].rearrange(
                    "(b p) t -> p b t", p=128)
                P.add("q", "dma", dest, stg.rearrange("p (b f) -> p b f", b=4),
                      waits=[cv.ptr_evac[cv.ptr_grp - 1]], incs=[(hsem, 16)])
                hstage_guard[gslot] = P.n(hsem)

            for ic in range(16):
                ipar = ic % 2
                rank, li = ic // 2, (ic % 2) * 512
                lw = [("cw", 1)]
                if ipar in isl_done:
                    lw.append(isl_done[ipar])
                P.add("q", "dma", islab[:, ipar * 16384: ipar * 16384 + 8192].rearrange(
                    "p (hc f) -> p hc f", hc=16),
                    ccW_out[rank, 0:2048, li:li + 512].rearrange("(hc p) f -> p hc f", p=128),
                    waits=lw, incs=[("li%d" % ipar, 16)])
                P.add("q", "dma", islab[:, ipar * 16384 + 8192: ipar * 16384 + 16384].rearrange(
                    "p (hc f) -> p hc f", hc=16),
                    ccW_out[rank, 2048:4096, li:li + 512].rearrange("(hc p) f -> p hc f", p=128),
                    incs=[("li%d" % ipar, 16)])
                nld = P.n("li%d" % ipar)
                for ts_ in range(16):
                    kk = ic * 16 + ts_
                    bank = [pC, pD, pA][kk % 3]
                    bank2 = [pE, pF, pB][kk % 3]
                    waits = [("li%d" % ipar, nld), ("lx", n_xdqt)]
                    if kk < 3:
                        waits.append(("at", n_at))
                    if kk - 3 in h_done:
                        waits.append(h_done[kk - 3])
                    for hc in range(16):
                        P.add("t", "mm", bank[:, 0:512],
                              xdqT[:].rearrange("p (hc t) -> p hc t", hc=16)[
                                  :, hc, ts_ * 128:(ts_ + 1) * 128],
                              islab[:, ipar * 16384 + hc * 512: ipar * 16384 + (hc + 1) * 512],
                              hc == 0, hc == 15,
                              waits=waits if hc == 0 else ())
                    for hc in range(16):
                        P.add("t", "mm", bank2[:, 0:512],
                              xdqT[:].rearrange("p (hc t) -> p hc t", hc=16)[
                                  :, hc, ts_ * 128:(ts_ + 1) * 128],
                              islab[:, ipar * 16384 + 8192 + hc * 512:
                                    ipar * 16384 + 8192 + (hc + 1) * 512],
                              hc == 0, hc == 15,
                              incs=[("ft", 1)] if hc == 15 else ())
                    nftg = P.n("ft")
                    silw = [("ft", nftg)]
                    if kk - 1 in hf_done:
                        silw.append(hf_done[kk - 1])
                    P.add("s", "act", hsil[:], bank[:, 0:512], ACTF.Silu, 0.0, 1.0,
                          waits=silw, incs=[("fs", 1)])
                    hfw = [("fs", P.n("fs"))]
                    if kk - 2 in tr_done:
                        hfw.append(tr_done[kk - 2])
                    P.add("v", "tt", hf[:], hsil[:], bank2[:, 0:512], ALU.mult,
                          waits=hfw, incs=[("fv", 1)])
                    hf_done[kk] = ("fv", P.n("fv"))
                    P.add("v", "red", hsc[:, 0:16], hf[:].rearrange("p (b k) -> p b k", k=32),
                          ALU.max, True, incs=[("fv", 1)])
                    P.add("v", "ts", hsc[:, 16:32], hsc[:, 0:16], 1.0 / 448.0, 1e-30,
                          ALU.mult, ALU.max, waits=[("fv", P.n("fv"))], incs=[("fv", 1)])
                    P.add("v", "recip", hsc[:, 32:48], hsc[:, 16:32],
                          waits=[("fv", P.n("fv"))], incs=[("fv", 1)])
                    P.add("v", "ts", hsc[:, 48:64], hsc[:, 16:32], 2.0, None, ALU.mult, None,
                          waits=[("fv", P.n("fv"))], incs=[("fv", 1)])
                    hp = kk % 2
                    P.add("v", "stt", hq8[:].rearrange("p (b k) -> p b k", k=32),
                          hf[:].rearrange("p (b k) -> p b k", k=32), 0.5,
                          hsc[:, 32:48].unsqueeze(-1).broadcast_to([128, 16, 32]),
                          ALU.mult, ALU.mult, waits=[("fv", P.n("fv"))])
                    P.add("v", "tt",
                          hdq[:, hp * 512:(hp + 1) * 512].rearrange("p (b k) -> p b k", k=32),
                          hq8[:].rearrange("p (b k) -> p b k", k=32),
                          hsc[:, 48:64].unsqueeze(-1).broadcast_to([128, 16, 32]),
                          ALU.mult, incs=[("fv", 1)])
                    h_done[kk] = ("fv", P.n("fv"))
                    if pend_tr is not None:
                        h_tr(*pend_tr)
                    pend_tr = (kk, ic, ts_)
                isl_done[ipar] = ("ft", P.n("ft"))
            h_tr(*pend_tr)
            n_hd = (P.n("hd0"), P.n("hd1"))

        with ExitStack() as sc7:
            e7 = sc7.enter_context
            hslab = e7(nc.sbuf_tensor("hslab", [128, 64 * 512], BF16))
            dwsl = e7(nc.sbuf_tensor("dwsl", [128, 2 * 16384], BF16))
            uTs = e7(nc.sbuf_tensor("uTs", [128, 4 * 512], F32))
            atl = e7(nc.sbuf_tensor("atl", [128, 4 * 512], BF16))
            ofin = e7(nc.sbuf_tensor("ofin", [128, 2 * 512], F32))
            dw_done = {}
            fin_done = {}
            st_par = {}
            ua_done = {}
            for tc in range(4):
                t0 = tc * 512
                lw = [("hd0", n_hd[0]), ("hd1", n_hd[1]), ("at", n_at)]
                if tc > 0:
                    lw.append(("ft", P.n("ft")))
                    lw.append(("fv", P.n("fv")))
                    lw.append(("fs", P.n("fs")))
                P.add("q", "dma", hslab[:].rearrange("p (ic t) -> p ic t", ic=64),
                      h_scr[:, t0:t0 + 512].rearrange("(ic p) t -> p ic t", p=128),
                      waits=lw, incs=[("l6", 16)])
                for ec in range(16):
                    epar = ec % 2
                    erank = ec // 2
                    wpar = erank % 2
                    if epar == 0:
                        lw2 = []
                        if wpar in dw_done:
                            lw2.append(dw_done[wpar])
                        P.add("q", "dma",
                              dwsl[:, wpar * 16384:(wpar + 1) * 16384].rearrange(
                                  "p (icl c) -> p icl c", icl=16),
                              ccW_out[erank, 4096:6144, :].rearrange(
                                  "(icl p) c -> p icl c", p=128),
                              waits=lw2, incs=[("l6w%d" % wpar, 16)])
                    nld2 = P.n("l6w%d" % wpar)
                    uslot = ec % 4
                    uw2 = [ua_done[uslot]] if uslot in ua_done else []
                    P.add("q", "dma", uTs[:, uslot * 512:(uslot + 1) * 512],
                          uT_own[ec * 128:(ec + 1) * 128, t0:t0 + 512],
                          waits=uw2, incs=[("l6", 16)])
                    P.add("q", "dma", atl[:, uslot * 512:(uslot + 1) * 512],
                          attn_scr[ec * 128:(ec + 1) * 128, t0:t0 + 512],
                          incs=[("l6", 16)])
                    nldu = P.n("l6")
                    bank = pA if epar == 0 else pB
                    waits = [("l6", nldu), ("l6w%d" % wpar, nld2)]
                    if epar in fin_done:
                        waits.append(fin_done[epar])
                    first = True
                    for m in range(4):
                        for icl in range(16):
                            off = wpar * 16384 + icl * 1024 + m * 256 + epar * 128
                            ih = m * 16 + icl
                            P.add("t", "mm", bank[:, 0:512], dwsl[:, off: off + 128],
                                  hslab[:, ih * 512:(ih + 1) * 512],
                                  first, (m == 3 and icl == 15),
                                  waits=waits if first else (),
                                  incs=[("ft", 1)] if (m == 3 and icl == 15) else ())
                            first = False
                    of = ofin[:, epar * 512:(epar + 1) * 512]
                    ow = [("ft", P.n("ft"))]
                    if epar in st_par:
                        ow.append(("so%d" % epar, st_par[epar]))
                    P.add("v", "tt", of, bank[:, 0:512], atl[:, uslot * 512:(uslot + 1) * 512],
                          ALU.add, waits=ow)
                    P.add("v", "tt", of, of, uTs[:, uslot * 512:(uslot + 1) * 512], ALU.add,
                          incs=[("fv", 1)])
                    fin_done[epar] = ("fv", P.n("fv"))
                    ua_done[uslot] = ("fv", P.n("fv"))
                    P.add("q", "dma", outT[ec * 128:(ec + 1) * 128, t0:t0 + 512], of,
                          waits=[fin_done[epar]], incs=[("so%d" % epar, 16)])
                    st_par[epar] = P.n("so%d" % epar)
                    if epar == 1:
                        dw_done[wpar] = ("ft", P.n("ft"))

        P.add("g", "memset", biases[0:1, 0:8], 0.0,
              waits=[("so0", P.n("so0")), ("so1", P.n("so1")), ("st", P.n("st"))])
        for name, (scr, dout) in dbg.items():
            P.add("g", "dma", dout[:], scr[:], incs=[("db", 16)])
        if dbg:
            P.add("g", "memset", biases[0:1, 8:16], 0.0, waits=[("db", P.n("db"))])

        _run_block(nc, P)
    return nc


def _emit_kv_rest(P, cv, kin, ych_ap, k_wait):
    b, c = cv.b, cv.c
    pXre, pXim = cv.pb["pXre"], cv.pb["pXim"]
    pA, pB = cv.pb["pA"], cv.pb["pB"]
    bar = cv.stage1(2)
    x_prev = {}
    z_done = {}
    st_state = {"u_prev": None}
    usb_turn = {}
    ustage_prev = {}
    prev_pair = None

    def stageA_turn2(pp):
        ppar = pp % 2
        Zp = lambda im: b.Z[0:72, ppar * 1024 + im * 512: ppar * 1024 + im * 512 + 512]
        uw = [z_done[ppar]]
        if st_state["u_prev"] is not None:
            uw.append(st_state["u_prev"])
        P.add("t", "mm", pA[:, 0:512], c.BDAre[0:72, 0:128], Zp(0), True, False, waits=uw)
        P.add("t", "mm", pA[:, 0:512], c.BDAim_neg[0:72, 0:128], Zp(1), False, True)
        P.add("t", "mm", pB[:, 0:512], c.BDAim[0:72, 0:128], Zp(0), True, False)
        P.add("t", "mm", pB[:, 0:512], c.BDAre[0:72, 0:128], Zp(1), False, True,
              incs=[("ft", 1)])
        ustage_prev[ppar] = ("ft", P.n("ft"))
        sw = [("ft", P.n("ft"))]
        if ppar in usb_turn:
            sw.append(usb_turn[ppar])
        P.add("s", "copy", b.Usb[:, ppar * 1024: ppar * 1024 + 512], pA[:, 0:512], waits=sw)
        P.add("s", "act", b.Usb[:, ppar * 1024 + 512: ppar * 1024 + 1024], pB[:, 0:512],
              ACTF.Copy, 0.0, -1.0, incs=[("fs", 1)])
        st_state["u_prev"] = ("fs", P.n("fs"))
        trs = []
        for ac in range(2):
            for im in range(2):
                for lg in range(2):
                    off = ppar * 1024 + im * 512 + lg * 256 + ac * 128
                    trs.append(b.Usb[:, off: off + 128])
        g0 = pp * 2
        dest = b.UT[:].rearrange("p (aci gg f) -> p aci gg f", aci=4, gg=32)[:, :, g0:g0 + 2, :]
        cv.tr_group(trs, dest, "v" if ppar == 0 else "s",
                    extra_waits=[st_state["u_prev"]])
        usb_turn[ppar] = ("ft", P.n("ft"))

    for pair in range(16):
        par = pair % 2
        kw = [k_wait]
        if par in z_done:
            kw.append(z_done[par])
        ks = "kk%d" % par
        P.add("q", "dma", b.Kbuf[0:72, par * 1024: par * 1024 + 1024],
              kin[pair], waits=kw, incs=[(ks, 16)])
        nkq = P.n(ks)
        ew = [("fv", bar[0]), ("fs", bar[1])] if pair < 2 else ()
        cv.pair_turn1(pair, extra_waits=ew)
        nft = cv.pair_stage2(pair, x_prev)
        sw = [("ft", nft)]
        if par in z_done:
            sw.append(z_done[par])
        P.add("s", "copy", b.tmp[0:72, par * 1024: par * 1024 + 512],
              pXre[par][0:72, 0:512], waits=sw)
        P.add("s", "copy", b.tmp[0:72, par * 1024 + 512: par * 1024 + 1024],
              pXim[par][0:72, 0:512], incs=[("fs", 1)])
        nfs = P.n("fs")
        x_prev[par] = ("fs", nfs)
        K = lambda im: b.Kbuf[0:72, par * 1024 + im * 512: par * 1024 + im * 512 + 512]
        Z = lambda im: b.Z[0:72, par * 1024 + im * 512: par * 1024 + im * 512 + 512]
        tre = b.tmp[0:72, par * 1024: par * 1024 + 512]
        tim = b.tmp[0:72, par * 1024 + 512: par * 1024 + 1024]
        zA, zB = b.tmp2[0:72, 0:512], b.tmp2[0:72, 512:1024]
        vw = [("fs", nfs), (ks, nkq)]
        if par in ustage_prev:
            vw.append(ustage_prev[par])
        P.add("v", "tt", zA, tre, K(0), ALU.mult, waits=vw)
        P.add("v", "tt", zB, tim, K(1), ALU.mult)
        P.add("v", "tt", Z(0), zA, zB, ALU.subtract)
        P.add("v", "tt", zA, tre, K(1), ALU.mult)
        P.add("v", "tt", zB, tim, K(0), ALU.mult)
        P.add("v", "tt", Z(1), zA, zB, ALU.add, incs=[("fv", 1)])
        z_done[par] = ("fv", P.n("fv"))
        if prev_pair is not None:
            stageA_turn2(prev_pair)
        prev_pair = pair
    stageA_turn2(prev_pair)
    bar2 = (("fv", P.n("fv")), ("fs", P.n("fs")))
    py_prev = {}
    tb_done = {}
    for nu in range(16):
        par = nu % 2
        bank = pXre[par]
        w = [bar2[0], bar2[1]] if nu < 2 else []
        if par in py_prev:
            w.append(py_prev[par])
        if nu < 2 and par in x_prev:
            w.append(x_prev[par])
        first = True
        for ac in range(2):
            hoff = (nu * 2 + ac) * 128
            base_re = (ac * 2 + 0) * 4096
            base_im = (ac * 2 + 1) * 4096
            UTre = b.UT[:, base_re:base_re + 4096].rearrange(
                "p (g c2 n2) -> p g c2 n2", c2=8, n2=16)[:, :, :, nu]
            UTim = b.UT[:, base_im:base_im + 4096].rearrange(
                "p (g c2 n2) -> p g c2 n2", c2=8, n2=16)[:, :, :, nu]
            P.add("t", "mm", bank[:, 0:256], c.Hre[:, hoff:hoff + 128], UTre,
                  first, False, waits=w if first else ())
            first = False
            P.add("t", "mm", bank[:, 0:256], c.Him[:, hoff:hoff + 128], UTim,
                  False, ac == 1, incs=[("ft", 1)] if ac == 1 else ())
        cons = "v" if par == 0 else "s"
        cn = "fv" if par == 0 else "fs"
        yw = [("ft", P.n("ft"))]
        if par in tb_done:
            yw.append(tb_done[par])
        P.add(cons, "copy", b.ysb[:, par * 256: par * 256 + 256], bank[:, 0:256],
              waits=yw, incs=[(cn, 1)])
        py_prev[par] = (cn, P.n(cn))
        trs = [b.ysb[:, par * 256 + cq * 128: par * 256 + (cq + 1) * 128]
               for cq in range(2)]
        dest = ych_ap.rearrange("p (cq t) -> p cq t", cq=2)[:, :, nu:2048:16]
        cv.tr_group(trs, dest, "v" if par == 0 else "s", extra_waits=[py_prev[par]])
        tb_done[par] = ("ft", P.n("ft"))



_CACHE = {}


def _get_nc():
    if "nc" not in _CACHE:
        _CACHE["nc"] = _build(_DEBUG)
    return _CACHE["nc"]


def _chunkvec(v):
    return np.ascontiguousarray(np.asarray(v, np.float32).reshape(16, 128).T)


def _chunkw3(v):
    return np.ascontiguousarray(
        np.asarray(v, np.float32).reshape(16, 128, 3).transpose(1, 0, 2).reshape(128, 48))


def kernel(**inputs):
    u = np.asarray(inputs["L0_attn_u"], dtype=np.float32)
    fc = _fft_constants()
    shared = {
        "WqT": np.ascontiguousarray(np.asarray(inputs["L0_attn_Wq"], np.float32).T),
        "WkT": np.ascontiguousarray(np.asarray(inputs["L0_attn_Wk"], np.float32).T),
        "WvT": np.ascontiguousarray(np.asarray(inputs["L0_attn_Wv"], np.float32).T),
        "WoT": np.ascontiguousarray(np.asarray(inputs["L0_attn_Wo"], np.float32).T),
        "bq": _chunkvec(inputs["L0_attn_bq"]),
        "bk": _chunkvec(inputs["L0_attn_bk"]),
        "bv": _chunkvec(inputs["L0_attn_bv"]),
        "bo": _chunkvec(inputs["L0_attn_bo"]),
        "cbq": _chunkvec(inputs["L0_attn_convq_b"]),
        "cbk": _chunkvec(inputs["L0_attn_convk_b"]),
        "dskip": _chunkvec(inputs["L0_attn_D_skip"]),
        "cwq": _chunkw3(inputs["L0_attn_convq_w"]),
        "cwk": _chunkw3(inputs["L0_attn_convk_w"]),
    }
    for n, a in fc.items():
        shared["c_" + n] = a
    ker = np.asarray(inputs["L0_attn_ssm_kernel"], np.float32)
    gw = np.asarray(inputs["L0_ffn_gate_w"], np.float32)
    uw = np.asarray(inputs["L0_ffn_up_w"], np.float32)
    dw = np.asarray(inputs["L0_ffn_down_w"], np.float32)
    zeros = np.zeros((TOK, D), np.float32)
    in_maps = []
    for core in range(8):
        s, h = core // 2, core % 2
        m = dict(shared)
        uo = np.ascontiguousarray(u[s, h * TOK:(h + 1) * TOK, :])
        m["u_own"] = uo
        m["u_ctx"] = np.ascontiguousarray(u[s, 0:TOK, :]) if h == 1 else uo
        m["uT_own"] = np.ascontiguousarray(uo.T)
        m["ctxm"] = np.full((128, 1), float(h), np.float32)
        m["ker_sl"] = np.ascontiguousarray(ker[core * 256:(core + 1) * 256, :])
        m["gw_s"] = np.ascontiguousarray(gw[core * 1024:(core + 1) * 1024, :])
        m["uw_s"] = np.ascontiguousarray(uw[core * 1024:(core + 1) * 1024, :])
        m["dw_s"] = np.ascontiguousarray(dw[core * 256:(core + 1) * 256, :])
        in_maps.append(m)
    nc = _get_nc()
    res = run_bass_kernel_spmd(nc, in_maps, core_ids=list(range(8)))
    out = np.empty((4, 4096, D), np.float32)
    for core in range(8):
        s, h = core // 2, core % 2
        out[s, h * TOK:(h + 1) * TOK, :] = res.results[core]["outT"].T
    if _DEBUG:
        kernel._dbg = [{k: v for k, v in r.items() if k.startswith("dbg_")}
                       for r in res.results]
    return out
